# version 04
# baseline (speedup 1.0000x reference)
import numpy as np
import concourse.bass as bass
import concourse.mybir as mybir
from concourse.bass_utils import run_bass_kernel_spmd
from contextlib import ExitStack

F32 = mybir.dt.float32
BF16 = mybir.dt.bfloat16
I32 = mybir.dt.int32
I8 = mybir.dt.int8
AF = mybir.ActivationFunctionType
ALU = mybir.AluOpType

D = 4096
HKV = 8
HD = 128
I_FFN = 11008
NCORE = 8

CHUNKS = [11, 11, 11, 11, 11, 11, 10, 10]
CHOFF = [0, 11, 22, 33, 44, 55, 66, 76]
MAXCH = 11

COMPUTE_FFN = True
COMPUTE_ATTN = True
TT = 512
NT = 4


class Ctl:
    def __init__(self, nc, stack):
        self.nc = nc
        self.stack = stack
        self.sems = {}
        self.cnt = {}

    def sem(self, name):
        self.sems[name] = self.stack.enter_context(self.nc.semaphore(name))
        self.cnt[name] = 0

    def inc(self, instr, name, k=1):
        instr.then_inc(self.sems[name], k)
        self.cnt[name] += k
        return self.cnt[name]

    def wait(self, eng, name, val=None):
        v = self.cnt[name] if val is None else val
        if v > 0:
            eng.wait_ge(self.sems[name], v)

    def val(self, name):
        return self.cnt[name]


def _dequant_matrix(nc, c, qw_d, sc_d, qz_d, sc_v, G, W, N, NSLOT, ph):
    MASK = 0x0F0F0F0F
    for eng in [nc.vector, nc.gpsimd]:
        c.wait(eng, "d_scr")
    c.wait(nc.gpsimd, "s_pe")
    c.wait(nc.gpsimd, "s_dve")
    c.wait(nc.sync, "s_dve")
    c.wait(nc.scalar, "s_pool")
    vp = 0
    for st in range(0, N, 1024):
        en = min(N, st + 1024)
        v = c.inc(nc.sync.dma_start(out=ph["sc_st"][0:G, 0:en - st],
                                    in_=sc_d[0:G, st:en]), "d_sc", 16)
        c.wait(nc.vector, "d_sc", v)
        nc.vector.wait_ge(c.sems["s_dve"], c.val("s_dve"))
        vp = c.inc(nc.vector.tensor_copy(out=ph["sc_bf"][0:G, st:en],
                                         in_=ph["sc_st"][0:G, 0:en - st]),
                   "s_dve")
        c.wait(nc.sync, "s_dve", vp)
    vsc = vp
    v = c.inc(nc.sync.dma_start(out=ph["qz"][0:G, 0:W], in_=qz_d), "d_qz", 16)
    vms = c.inc(nc.gpsimd.memset(ph["negzs"][:, 0:N], 0.0), "s_pool")
    c.wait(nc.vector, "d_qz", v)
    nc.vector.wait_ge(c.sems["s_dve"], c.val("s_dve"))
    nc.vector.tensor_scalar(
        out=ph["qe"][0:G, 0:W], in0=ph["qz"][0:G, 0:W], scalar1=MASK,
        scalar2=None, op0=ALU.bitwise_and)
    vz = c.inc(nc.vector.tensor_scalar(
        out=ph["qo"][0:G, 0:W], in0=ph["qz"][0:G, 0:W], scalar1=4,
        scalar2=MASK, op0=ALU.logical_shift_right, op1=ALU.bitwise_and),
        "s_dve")
    nc.vector.wait_ge(c.sems["s_dve"], vz)
    c.wait(nc.vector, "s_pool", vms)
    qe8 = ph["qe"][0:G, 0:W].bitcast(I8)
    qo8 = ph["qo"][0:G, 0:W].bitcast(I8)
    nc.vector.scalar_tensor_tensor(
        out=ph["negzs"][0:G, 0:N:2], in0=qe8, scalar=-8.0,
        in1=ph["sc_bf"][0:G, 0:N:2], op0=ALU.add, op1=ALU.mult)
    vz2 = c.inc(nc.vector.scalar_tensor_tensor(
        out=ph["negzs"][0:G, 1:N:2], in0=qo8, scalar=-8.0,
        in1=ph["sc_bf"][0:G, 1:N:2], op0=ALU.add, op1=ALU.mult), "s_dve")
    c.wait(nc.scalar, "s_dve", vz2)
    c.inc(nc.scalar.dma_start(
        out=sc_v[:, :, NSLOT - 1, :].rearrange("t p i -> p t i"),
        in_=ph["negzs"][:, 0:N].rearrange("p (t i) -> p t i", i=128)),
        "d_scr", 16)

    dq = ph["deq"][:, 0:2 * N]
    dqv = dq.rearrange("p (t two i) -> p t two i", two=2, i=128)
    qw_r = qw_d.rearrange("(g p) w -> g p w", p=128)
    qw_rd = [0, 0]
    ps_rd = [0, 0]
    deq_wr = 0
    rs_built = 0
    for g in range(G):
        parity = g % 2
        qbuf = ph["qw"][parity]
        if qw_rd[parity]:
            c.wait(nc.sync, "s_dve", qw_rd[parity])
        v = c.inc(nc.sync.dma_start(out=qbuf[:, 0:W], in_=qw_r[g]),
                  "d_qw%d" % parity, 16)
        if rs_built:
            c.wait(nc.vector, "s_pe", rs_built)
        vrs = c.inc(nc.vector.tensor_scalar(
            out=ph["rs_t"][0:G, 0:128], in0=ph["iota"][0:G, :],
            scalar1=g, scalar2=None, op0=ALU.is_equal), "s_dve")
        c.wait(nc.vector, "d_qw%d" % parity, v)
        nc.vector.wait_ge(c.sems["s_dve"], c.val("s_dve"))
        nc.vector.tensor_scalar(
            out=ph["qe"][:, 0:W], in0=qbuf[:, 0:W], scalar1=MASK,
            scalar2=None, op0=ALU.bitwise_and)
        qw_rd[parity] = c.inc(nc.vector.tensor_scalar(
            out=ph["qo"][:, 0:W], in0=qbuf[:, 0:W], scalar1=4,
            scalar2=MASK, op0=ALU.logical_shift_right, op1=ALU.bitwise_and),
            "s_dve")
        vunp = qw_rd[parity]
        qe8 = ph["qe"][:, 0:W].bitcast(I8)
        qo8 = ph["qo"][:, 0:W].bitcast(I8)
        for ci, ch0 in enumerate(range(0, N, 512)):
            cw = min(N, ch0 + 512) - ch0
            bi = ci % 2
            pb = ph["p_s"][bi]
            if ps_rd[bi]:
                c.wait(nc.tensor, "s_dve", ps_rd[bi])
            c.wait(nc.tensor, "s_dve", max(vsc, vrs))
            vmm = c.inc(nc.tensor.matmul(
                out=pb[:, 0:cw], lhsT=ph["rs_t"][0:G, 0:128],
                rhs=ph["sc_bf"][0:G, ch0:ch0 + cw],
                start=True, stop=True), "s_pe")
            rs_built = vmm
            nc.vector.wait_ge(c.sems["s_pe"], vmm)
            nc.vector.wait_ge(c.sems["s_dve"], vunp)
            if deq_wr and ch0 == 0:
                c.wait(nc.vector, "d_scr", deq_wr)
            a0, a1 = ch0 // 128, (ch0 + cw) // 128
            ins = nc.vector.scalar_tensor_tensor(
                out=dqv[:, a0:a1, parity, 0:128:2],
                in0=qe8[:, ch0 // 2:(ch0 + cw) // 2].rearrange(
                    "p (a i) -> p a i", i=64),
                scalar=-8.0,
                in1=pb[:, 0:cw:2].rearrange("p (a i) -> p a i", i=64),
                op0=ALU.add, op1=ALU.mult)
            ins = nc.vector.scalar_tensor_tensor(
                out=dqv[:, a0:a1, parity, 1:128:2],
                in0=qo8[:, ch0 // 2:(ch0 + cw) // 2].rearrange(
                    "p (a i) -> p a i", i=64),
                scalar=-8.0,
                in1=pb[:, 1:cw:2].rearrange("p (a i) -> p a i", i=64),
                op0=ALU.add, op1=ALU.mult)
            ps_rd[bi] = c.inc(ins, "s_dve")
        if parity == 1:
            c.wait(nc.scalar, "s_dve")
            deq_wr = c.inc(nc.scalar.dma_start(
                out=sc_v[:, :, g - 1:g + 1, :].rearrange(
                    "t p s i -> p t (s i)"),
                in_=dqv.rearrange("p t two i -> p t (two i)")),
                "d_scr", 16)


def build_program():
    T = TT * NT
    nc = bass.Bass("TRN2", target_bir_lowering=False)

    xT = nc.dram_tensor("xT", [D, T], F32, kind="ExternalInput").ap()
    wqT = nc.dram_tensor("wqT", [D, HKV * HD], F32, kind="ExternalInput").ap()
    bqT = nc.dram_tensor("bqT", [HD, HKV], F32, kind="ExternalInput").ap()
    woT = nc.dram_tensor("woT", [D, D], F32, kind="ExternalInput").ap()
    m_in = nc.dram_tensor("m", [HKV, HD, HD], F32, kind="ExternalInput").ap()
    zT = nc.dram_tensor("zT", [HD, HKV], F32, kind="ExternalInput").ap()
    g_qw = nc.dram_tensor("g_qw", [D, I_FFN // 8], I32, kind="ExternalInput").ap()
    g_sc = nc.dram_tensor("g_sc", [32, I_FFN], F32, kind="ExternalInput").ap()
    g_qz = nc.dram_tensor("g_qz", [32, I_FFN // 8], I32, kind="ExternalInput").ap()
    u_qw = nc.dram_tensor("u_qw", [D, I_FFN // 8], I32, kind="ExternalInput").ap()
    u_sc = nc.dram_tensor("u_sc", [32, I_FFN], F32, kind="ExternalInput").ap()
    u_qz = nc.dram_tensor("u_qz", [32, I_FFN // 8], I32, kind="ExternalInput").ap()
    d_qw = nc.dram_tensor("d_qw", [I_FFN, D // 8], I32, kind="ExternalInput").ap()
    d_sc = nc.dram_tensor("d_sc", [86, D], F32, kind="ExternalInput").ap()
    d_qz = nc.dram_tensor("d_qz", [86, D // 8], I32, kind="ExternalInput").ap()
    yT = nc.dram_tensor("yT", [D, T], F32, kind="ExternalOutput").ap()

    sc_g = nc.dram_tensor("sc_g", [86 * 128 * 33 * 128], BF16)
    sc_u = nc.dram_tensor("sc_u", [86 * 128 * 33 * 128], BF16)
    sc_d = nc.dram_tensor("sc_d", [32 * 128 * 87 * 128], BF16)
    sc_wq = nc.dram_tensor("sc_wq", [8 * 128 * 32 * 128], BF16)
    sc_wo = nc.dram_tensor("sc_wo", [32 * 128 * 8 * 128], BF16)
    g_v = sc_g.ap().rearrange("(t p s i) -> t p s i", p=128, s=33, i=128)
    u_v = sc_u.ap().rearrange("(t p s i) -> t p s i", p=128, s=33, i=128)
    d_v = sc_d.ap().rearrange("(t p s i) -> t p s i", p=128, s=87, i=128)
    wq_v = sc_wq.ap().rearrange("(h p k i) -> h p k i", p=128, k=32, i=128)
    wo_v = sc_wo.ap().rearrange("(o p k i) -> o p k i", p=128, k=8, i=128)

    with ExitStack() as stack:
        e = stack.enter_context
        c = Ctl(nc, stack)
        DMA_SEMS = ["d_in0", "d_in1", "d_sc", "d_qz", "d_qw0", "d_qw1",
                    "d_scr", "d_wg0", "d_wg1", "d_wu0", "d_wu1",
                    "d_wd0", "d_wd1", "d_wd2", "d_wd3", "d_x", "d_y"]
        for s in DMA_SEMS + ["s_pe", "s_dve", "s_act", "s_pool"]:
            c.sem(s)
        c.dma_sems = DMA_SEMS

        M_sb = e(nc.sbuf_tensor("M_sb", [128, HKV * HD], BF16))
        z_sb = e(nc.sbuf_tensor("z_sb", [128, HKV], BF16))
        bq_sb = e(nc.sbuf_tensor("bq_sb", [128, HKV], F32))
        onesK = e(nc.sbuf_tensor("onesK", [128, 1], BF16))
        ones1 = e(nc.sbuf_tensor("ones1", [1, 128], F32))
        D32 = e(nc.sbuf_tensor("D32", [128, 64], BF16))
        D86 = e(nc.sbuf_tensor("D86", [128, 172], BF16))
        sgp = e(nc.sbuf_tensor("sgp", [128, TT], BF16))
        sdp = e(nc.sbuf_tensor("sdp", [128, TT], BF16))
        cst = e(nc.sbuf_tensor("cst", [128, 2], F32))

        nc.vector.memset(onesK[:], 1.0 / 4096.0)
        nc.vector.memset(ones1[:], 1.0)
        nc.vector.memset(D32[:], 0.0)
        nc.vector.memset(D86[:], 0.0)
        nc.vector.memset(sgp[:], 0.0)
        nc.vector.memset(sdp[:], 0.0)
        vm0 = c.inc(nc.vector.memset(cst[:, 0:1], 1e-5), "s_dve")
        nc.vector.wait_ge(c.sems["s_dve"], vm0)
        nc.vector.memset(D32[:, 32:33], 1.0)
        c.inc(nc.vector.memset(D86[:, 86:87], 1.0), "s_dve")

        with ExitStack() as p0a:
            e0 = p0a.enter_context
            ph = {}
            ph["sc_st"] = e0(nc.sbuf_tensor("p0_scst", [86, 1024], F32))
            ph["sc_bf"] = e0(nc.sbuf_tensor("p0_scb", [86, I_FFN], BF16))
            ph["qz"] = e0(nc.sbuf_tensor("p0_qz", [86, I_FFN // 8], I32))
            ph["negzs"] = e0(nc.sbuf_tensor("p0_negzs", [128, I_FFN], BF16))
            ph["qw"] = [e0(nc.sbuf_tensor(f"p0_qw{i}", [128, I_FFN // 8], I32))
                        for i in range(2)]
            ph["qe"] = e0(nc.sbuf_tensor("p0_qe", [128, I_FFN // 8], I32))
            ph["qo"] = e0(nc.sbuf_tensor("p0_qo", [128, I_FFN // 8], I32))
            ph["deq"] = e0(nc.sbuf_tensor("p0_deq", [128, 2 * I_FFN], BF16))
            ph["p_s"] = [e0(nc.psum_tensor(f"p0_ps{i}", [128, 512], F32))
                         for i in range(2)]
            ph["rs_t"] = e0(nc.sbuf_tensor("p0_rst", [96, 128], BF16))
            ph["iota"] = e0(nc.sbuf_tensor("p0_io96", [96, 128], I32))

            vio = c.inc(nc.gpsimd.iota(ph["iota"][:], pattern=[[0, 128]],
                                       base=0, channel_multiplier=1),
                        "s_pool")
            c.wait(nc.vector, "s_pool", vio)

            _dequant_matrix(nc, c, g_qw, g_sc, g_qz, g_v, 32, I_FFN // 8,
                            I_FFN, 33, ph)
            _dequant_matrix(nc, c, u_qw, u_sc, u_qz, u_v, 32, I_FFN // 8,
                            I_FFN, 33, ph)
            _dequant_matrix(nc, c, d_qw, d_sc, d_qz, d_v, 86, D // 8,
                            D, 87, ph)

            wtmp = [e0(nc.sbuf_tensor(f"p0_wt{i}", [128, D], F32))
                    for i in range(2)]
            wcast = e0(nc.sbuf_tensor("p0_wc", [128, 2 * D], BF16))

            v = c.inc(nc.sync.dma_start(
                out=wtmp[0][:, 0:HKV * HD].rearrange(
                    "p (kv e) -> p kv e", e=HD),
                in_=m_in.rearrange("kv d e -> d kv e")), "d_in0", 16)
            c.wait(nc.gpsimd, "d_in0", v)
            nc.gpsimd.wait_ge(c.sems["s_pool"], c.val("s_pool"))
            vmcast = c.inc(nc.gpsimd.tensor_copy(
                out=M_sb[:], in_=wtmp[0][:, 0:HKV * HD]), "s_pool")
            v = c.inc(nc.sync.dma_start(out=wtmp[1][:, 0:HKV], in_=zT),
                      "d_in1", 16)
            c.wait(nc.gpsimd, "d_in1", v)
            vmz = c.inc(nc.gpsimd.tensor_copy(out=z_sb[:],
                                              in_=wtmp[1][:, 0:HKV]),
                        "s_pool")
            c.inc(nc.sync.dma_start(out=bq_sb[:], in_=bqT), "d_qz", 16)

            wcv = wcast[:, 0:HKV * 2 * HD].rearrange(
                "p (h two i) -> p h two i", two=2, i=128)
            wqT_r = wqT.rearrange("(k p) h -> k p h", p=128)
            wq_rd = [vmcast, vmz]
            wq_wr = 0
            for kc in range(32):
                parity = kc % 2
                c.wait(nc.sync, "s_pool", wq_rd[parity])
                v = c.inc(nc.sync.dma_start(out=wtmp[parity][:, 0:HKV * HD],
                                            in_=wqT_r[kc]),
                          "d_in%d" % parity, 16)
                c.wait(nc.gpsimd, "d_in%d" % parity, v)
                if wq_wr:
                    c.wait(nc.gpsimd, "d_scr", wq_wr)
                nc.gpsimd.wait_ge(c.sems["s_pool"], c.val("s_pool"))
                ins = nc.gpsimd.tensor_copy(
                    out=wcv[:, :, parity, :],
                    in_=wtmp[parity][:, 0:HKV * HD].rearrange(
                        "p (h i) -> p h i", i=128))
                wq_rd[parity] = c.inc(ins, "s_pool")
                if parity == 1:
                    c.wait(nc.scalar, "s_pool")
                    wq_wr = c.inc(nc.scalar.dma_start(
                        out=wq_v[:, :, kc - 1:kc + 1, :].rearrange(
                            "h p s i -> p h (s i)"),
                        in_=wcv.rearrange("p h two i -> p h (two i)")),
                        "d_scr", 16)

            wov = wcast[:].rearrange("p (o two i) -> p o two i", two=2, i=128)
            woT_r = woT.rearrange("(k p) o -> k p o", p=128)
            wo_wr = wq_wr
            wo_rd = c.val("s_pool")
            for kv in range(HKV):
                parity = kv % 2
                a, b = wtmp[0], wtmp[1]
                c.wait(nc.sync, "s_pool", wo_rd)
                v0 = c.inc(nc.sync.dma_start(out=a[:], in_=woT_r[4 * kv]),
                           "d_in0", 16)
                v1 = c.inc(nc.sync.dma_start(out=b[:], in_=woT_r[4 * kv + 1]),
                           "d_in1", 16)
                c.wait(nc.gpsimd, "d_in0", v0)
                c.wait(nc.gpsimd, "d_in1", v1)
                nc.gpsimd.wait_ge(c.sems["s_pool"], c.val("s_pool"))
                s1 = c.inc(nc.gpsimd.tensor_add(out=a[:], in0=a[:], in1=b[:]),
                           "s_pool")
                c.wait(nc.sync, "s_pool", s1)
                v2 = c.inc(nc.sync.dma_start(out=b[:], in_=woT_r[4 * kv + 2]),
                           "d_in1", 16)
                c.wait(nc.gpsimd, "d_in1", v2)
                nc.gpsimd.wait_ge(c.sems["s_pool"], s1)
                s2 = c.inc(nc.gpsimd.tensor_add(out=a[:], in0=a[:], in1=b[:]),
                           "s_pool")
                c.wait(nc.sync, "s_pool", s2)
                v3 = c.inc(nc.sync.dma_start(out=b[:], in_=woT_r[4 * kv + 3]),
                           "d_in1", 16)
                c.wait(nc.gpsimd, "d_in1", v3)
                nc.gpsimd.wait_ge(c.sems["s_pool"], s2)
                if wo_wr:
                    c.wait(nc.gpsimd, "d_scr", wo_wr)
                s3 = c.inc(nc.gpsimd.tensor_add(
                    out=wov[:, :, parity, :],
                    in0=a[:].rearrange("p (o i) -> p o i", i=128),
                    in1=b[:].rearrange("p (o i) -> p o i", i=128)),
                    "s_pool")
                wo_rd = s3
                if parity == 1:
                    c.wait(nc.scalar, "s_pool", s3)
                    wo_wr = c.inc(nc.scalar.dma_start(
                        out=wo_v[:, :, kv - 1:kv + 1, :].rearrange(
                            "o p s i -> p o (s i)"),
                        in_=wov.rearrange("p o two i -> p o (two i)")),
                        "d_scr", 16)

        y_sb = e(nc.sbuf_tensor("y_sb", [128, 32 * TT], F32))
        xn_sb = e(nc.sbuf_tensor("xn_sb", [128, 32 * TT], BF16))
        h_sb = [e(nc.sbuf_tensor(f"h_sb{i}", [128, MAXCH * TT], BF16))
                for i in range(2)]
        wg_sb = [e(nc.sbuf_tensor(f"wg_sb{i}", [128, 33 * 128], BF16))
                 for i in range(2)]
        wu_sb = [e(nc.sbuf_tensor(f"wu_sb{i}", [128, 33 * 128], BF16))
                 for i in range(2)]
        wd_sb = [e(nc.sbuf_tensor(f"wd_sb{i}", [128, (MAXCH + 1) * 128], BF16))
                 for i in range(4)]
        sig_sb = [e(nc.sbuf_tensor(f"sig_sb{i}", [128, TT], BF16))
                  for i in range(2)]
        ret_sb = e(nc.sbuf_tensor("ret_sb", [128, HKV * TT], BF16))
        sq_sb = [e(nc.sbuf_tensor(f"sq_sb{i}", [128, TT], BF16))
                 for i in range(4)]
        t1_sb = [e(nc.sbuf_tensor(f"t1_sb{i}", [128, TT], F32))
                 for i in range(2)]
        silu_sb = [e(nc.sbuf_tensor(f"silu_sb{i}", [128, TT], BF16))
                   for i in range(2)]
        rb_sb = [e(nc.sbuf_tensor(f"rb_sb{i}", [128, TT], F32))
                 for i in range(2)]
        tex_sb = e(nc.sbuf_tensor("tex_sb", [128, TT], F32))
        tmx_sb = e(nc.sbuf_tensor("tmx_sb", [128, TT], F32))
        row_sb = e(nc.sbuf_tensor("row_sb", [1, 5 * TT], F32))
        pA = e(nc.psum_tensor("pA", [128, TT], F32))
        pB = e(nc.psum_tensor("pB", [128, TT], F32))
        pC = e(nc.psum_tensor("pC", [128, TT], F32))
        pC2 = e(nc.psum_tensor("pC2", [128, TT], F32))
        pD = e(nc.psum_tensor("pD", [128, TT], F32))
        pF = e(nc.psum_tensor("pF", [128, TT], F32))
        pF2 = e(nc.psum_tensor("pF2", [128, TT], F32))
        pE = e(nc.psum_tensor("pE", [128, TT], F32))

        snap = {k: c.val(k) for k in c.cnt}
        for eng in [nc.sync, nc.scalar, nc.vector, nc.gpsimd, nc.tensor]:
            for s in ["d_in0", "d_in1", "d_sc", "d_qz", "d_qw0", "d_qw1",
                      "d_scr", "s_dve", "s_pool", "s_pe", "s_act"]:
                c.wait(eng, s, snap[s])

        xT_r = xT.rearrange("(k p) t -> p k t", p=128)
        yT_r = yT.rearrange("(k p) t -> p k t", p=128)
        y_r = y_sb[:].rearrange("p (k t) -> p k t", t=TT)
        xn_r = xn_sb[:].rearrange("p (k t) -> p k t", t=TT)

        free_gq = [0, 0]
        free_wu = [0, 0]
        free_dwo = [0, 0, 0, 0]
        free_h = [0, 0]
        y_store_v = 0

        for tt in range(NT):
            snap = {k: c.val(k) for k in ["s_pe", "s_dve", "s_act", "s_pool"]}
            for eng in [nc.tensor, nc.vector, nc.gpsimd, nc.scalar]:
                for s in ["s_pe", "s_dve", "s_act", "s_pool"]:
                    c.wait(eng, s, snap[s])

            c.wait(nc.gpsimd, "d_y", y_store_v)
            xload_v = 0
            for part in range(4):
                ins = nc.gpsimd.dma_start(
                    out=y_r[:, part * 8:(part + 1) * 8, :],
                    in_=xT_r[:, part * 8:(part + 1) * 8,
                             tt * TT:(tt + 1) * TT])
                xload_v = c.inc(ins, "d_x", 16)

            c.wait(nc.gpsimd, "d_x", xload_v)
            for kc in range(32):
                ins = nc.gpsimd.tensor_copy(out=xn_r[:, kc, :],
                                            in_=y_r[:, kc, :])
            vxt = c.inc(ins, "s_pool")
            c.wait(nc.tensor, "s_pool", vxt)
            for kc in range(32):
                nc.tensor.matmul(out=pA[0:1, :], lhsT=onesK[:],
                                 rhs=xn_r[:, kc, :], start=(kc == 0),
                                 stop=(kc == 31))
            c.wait(nc.scalar, "d_x", xload_v)
            sq_done = [0, 0, 0, 0]
            for kc in range(32):
                pr = kc % 4
                if sq_done[pr]:
                    c.wait(nc.scalar, "s_pe", sq_done[pr])
                va = c.inc(nc.scalar.activation(out=sq_sb[pr][:],
                                                in_=y_r[:, kc, :],
                                                func=AF.Square), "s_act")
                c.wait(nc.tensor, "s_act", va)
                ins = nc.tensor.matmul(out=pB[0:1, :], lhsT=onesK[:],
                                       rhs=sq_sb[pr][:], start=(kc == 0),
                                       stop=(kc == 31))
                sq_done[pr] = c.inc(ins, "s_pe")
            vstats = max(sq_done)
            mean = row_sb[0:1, 0:TT]
            ex2 = row_sb[0:1, TT:2 * TT]
            var = ex2
            m2 = row_sb[0:1, 2 * TT:3 * TT]
            rstd = row_sb[0:1, 3 * TT:4 * TT]
            mr = m2
            den = row_sb[0:1, 4 * TT:5 * TT]
            nc.vector.wait_ge(c.sems["s_pe"], vstats)
            nc.vector.tensor_copy(out=mean, in_=pA[0:1, :])
            vcp = c.inc(nc.vector.tensor_copy(out=ex2, in_=pB[0:1, :]),
                        "s_dve")
            nc.vector.wait_ge(c.sems["s_dve"], vcp)
            vmu = c.inc(nc.vector.tensor_mul(out=m2, in0=mean, in1=mean),
                        "s_dve")
            nc.vector.wait_ge(c.sems["s_dve"], vmu)
            vrow1 = c.inc(nc.vector.tensor_sub(out=var, in0=ex2, in1=m2),
                          "s_dve")
            c.wait(nc.scalar, "s_dve", vrow1)
            vln = c.inc(nc.scalar.activation(out=var, in_=var, func=AF.Ln,
                                             bias=cst[0:1, 0:1]), "s_act")
            nc.scalar.wait_ge(c.sems["s_act"], vln)
            vact2 = c.inc(nc.scalar.activation(out=rstd, in_=var, func=AF.Exp,
                                               scale=-0.5), "s_act")
            c.wait(nc.vector, "s_act", vact2)
            vrow2 = c.inc(nc.vector.tensor_mul(out=mr, in0=mean, in1=rstd),
                          "s_dve")
            c.wait(nc.tensor, "s_dve", vrow2)
            nc.tensor.matmul(out=pC[:, :], lhsT=ones1[:], rhs=rstd,
                             start=True, stop=True)
            vbc = c.inc(nc.tensor.matmul(out=pD[:, :], lhsT=ones1[:], rhs=mr,
                                         start=True, stop=True), "s_pe")
            nc.vector.wait_ge(c.sems["s_pe"], vbc)
            for kc in range(32):
                pr = kc % 2
                vt = c.inc(nc.vector.tensor_mul(out=t1_sb[pr][:],
                                                in0=y_r[:, kc, :],
                                                in1=pC[:, :]), "s_dve")
                nc.vector.wait_ge(c.sems["s_dve"], vt)
                ins = nc.vector.tensor_sub(out=xn_r[:, kc, :],
                                           in0=t1_sb[pr][:], in1=pD[:, :])
            vxn = c.inc(ins, "s_dve")

            if COMPUTE_FFN:
                c.wait(nc.tensor, "s_dve", vxn)
                for kc in range(32):
                    ins = nc.tensor.matmul(
                        out=pE[0:32, :], lhsT=D32[:, 32 - kc:64 - kc],
                        rhs=xn_r[:, kc, :], start=(kc == 0), stop=(kc == 31))
                vsg = c.inc(ins, "s_pe")
                nc.vector.wait_ge(c.sems["s_pe"], vsg)
                vsgp = c.inc(nc.vector.tensor_scalar(
                    out=sgp[0:32, :], in0=pE[0:32, :], scalar1=-1.0,
                    scalar2=None, op0=ALU.mult), "s_dve")

                pg = [pA, pB]
                pu = [pC, pC2]
                py = [pD, pF]
                pg_rd = [0, 0]
                pu_rd = [0, 0]
                py_rd = [0, 0]
                silu_rd = [0, 0]
                vsdp = 0
                sgp_mm = 0
                for q in range(8):
                    nic = CHUNKS[q]
                    hq = h_sb[q % 2]
                    hq_r = hq[:].rearrange("p (s t) -> p s t", t=TT)
                    h_chunk_wr = 0
                    for j in range(nic):
                        ic = CHOFF[q] + j
                        pr = ic % 2
                        c.wait(nc.sync, "s_pe", free_gq[pr])
                        vg = c.inc(nc.sync.dma_start(
                            out=wg_sb[pr][:],
                            in_=g_v[ic].rearrange("p s i -> p (s i)")),
                            "d_wg%d" % pr, 16)
                        c.wait(nc.sync, "s_pe", free_wu[pr])
                        vu = c.inc(nc.sync.dma_start(
                            out=wu_sb[pr][:],
                            in_=u_v[ic].rearrange("p s i -> p (s i)")),
                            "d_wu%d" % pr, 16)
                        c.wait(nc.tensor, "d_wg%d" % pr, vg)
                        c.wait(nc.tensor, "s_dve", vsgp)
                        if pg_rd[pr]:
                            c.wait(nc.tensor, "s_act", pg_rd[pr])
                        wg_r = wg_sb[pr][:].rearrange("p (s i) -> p s i",
                                                      i=128)
                        for kc in range(32):
                            nc.tensor.matmul(out=pg[pr][:, :],
                                             lhsT=wg_r[:, kc, :],
                                             rhs=xn_r[:, kc, :],
                                             start=(kc == 0), stop=False)
                        ins = nc.tensor.matmul(out=pg[pr][:, :],
                                               lhsT=wg_r[:, 32, :],
                                               rhs=sgp[:, :],
                                               start=False, stop=True)
                        free_gq[pr] = sgp_mm = c.inc(ins, "s_pe")
                        c.wait(nc.tensor, "d_wu%d" % pr, vu)
                        if pu_rd[pr]:
                            c.wait(nc.tensor, "s_dve", pu_rd[pr])
                        wu_r = wu_sb[pr][:].rearrange("p (s i) -> p s i",
                                                      i=128)
                        for kc in range(32):
                            nc.tensor.matmul(out=pu[pr][:, :],
                                             lhsT=wu_r[:, kc, :],
                                             rhs=xn_r[:, kc, :],
                                             start=(kc == 0), stop=False)
                        ins = nc.tensor.matmul(out=pu[pr][:, :],
                                               lhsT=wu_r[:, 32, :],
                                               rhs=sgp[:, :],
                                               start=False, stop=True)
                        free_wu[pr] = c.inc(ins, "s_pe")
                        c.wait(nc.scalar, "s_pe", free_gq[pr])
                        if silu_rd[pr]:
                            c.wait(nc.scalar, "s_dve", silu_rd[pr])
                        va = c.inc(nc.scalar.activation(
                            out=silu_sb[pr][:], in_=pg[pr][:, :],
                            func=AF.Silu), "s_act")
                        pg_rd[pr] = va
                        nc.vector.wait_ge(c.sems["s_act"], va)
                        nc.vector.wait_ge(c.sems["s_pe"], free_wu[pr])
                        if free_h[q % 2]:
                            c.wait(nc.vector, "s_pe", free_h[q % 2])
                        ins = nc.vector.scalar_tensor_tensor(
                            out=hq_r[:, j, :], in0=pu[pr][:, :], scalar=0.0,
                            in1=silu_sb[pr][:], op0=ALU.bypass, op1=ALU.mult)
                        h_chunk_wr = c.inc(ins, "s_dve")
                        pu_rd[pr] = h_chunk_wr
                        silu_rd[pr] = h_chunk_wr
                    c.wait(nc.tensor, "s_dve", h_chunk_wr)
                    for j in range(nic):
                        ic = CHOFF[q] + j
                        ins = nc.tensor.matmul(
                            out=pE[0:86, :], lhsT=D86[:, 86 - ic:172 - ic],
                            rhs=hq_r[:, j, :], start=(ic == 0),
                            stop=(ic == 85))
                    if q == 7:
                        vsd = c.inc(ins, "s_pe")
                        nc.vector.wait_ge(c.sems["s_pe"], vsd)
                        vsdp = c.inc(nc.vector.tensor_scalar(
                            out=sdp[0:86, :], in0=pE[0:86, :], scalar1=-1.0,
                            scalar2=None, op0=ALU.mult), "s_dve")
                    s0 = CHOFF[q]
                    nsl = nic + (1 if q == 7 else 0)
                    for oc in range(32):
                        pr = oc % 4
                        c.wait(nc.scalar, "s_pe", free_dwo[pr])
                        vd = c.inc(nc.scalar.dma_start(
                            out=wd_sb[pr][:, 0:nsl * 128],
                            in_=d_v[oc, :, s0:s0 + nsl, :].rearrange(
                                "p s i -> p (s i)")), "d_wd%d" % pr, 16)
                        c.wait(nc.tensor, "d_wd%d" % pr, vd)
                        if q == 7:
                            c.wait(nc.tensor, "s_dve", vsdp)
                        bk = oc % 2
                        if py_rd[bk]:
                            c.wait(nc.tensor, "s_dve", py_rd[bk])
                        wd_r = wd_sb[pr][:, 0:nsl * 128].rearrange(
                            "p (s i) -> p s i", i=128)
                        for j in range(nic):
                            ins = nc.tensor.matmul(
                                out=py[bk][:, :], lhsT=wd_r[:, j, :],
                                rhs=hq_r[:, j, :], start=(j == 0),
                                stop=(j == nic - 1 and q != 7))
                        if q == 7:
                            ins = nc.tensor.matmul(out=py[bk][:, :],
                                                   lhsT=wd_r[:, nic, :],
                                                   rhs=sdp[:, :],
                                                   start=False, stop=True)
                        free_dwo[pr] = c.inc(ins, "s_pe")
                        nc.vector.wait_ge(c.sems["s_pe"], free_dwo[pr])
                        ins = nc.vector.scalar_tensor_tensor(
                            out=y_r[:, oc, :], in0=py[bk][:, :],
                            scalar=0.0, in1=y_r[:, oc, :], op0=ALU.bypass,
                            op1=ALU.add)
                        py_rd[bk] = c.inc(ins, "s_dve")
                    free_h[q % 2] = free_dwo[(31) % 2]

            if COMPUTE_ATTN:
                ret_r = ret_sb[:].rearrange("p (h t) -> p h t", t=TT)
                pq = [pA, pB]
                pn = [pC, pC2]
                pb_ = [pF, pF2]
                pq_rd = [0, 0]
                pn_rd = [0, 0]
                pb_rd = [0, 0]
                sig_rd = [0, 0]
                rb_rd = [0, 0]
                den_wr = 0
                den_rd = 0
                va_exp = 0
                vret = 0
                ffn_dve = c.val("s_dve")
                ffn_act = c.val("s_act")
                c.wait(nc.tensor, "s_dve", ffn_dve)
                c.wait(nc.tensor, "s_act", ffn_act)
                for hc in range(HKV):
                    pr = hc % 2
                    c.wait(nc.sync, "s_pe", free_gq[pr])
                    vq = c.inc(nc.sync.dma_start(
                        out=wg_sb[pr][:, 0:32 * 128],
                        in_=wq_v[hc].rearrange("p s i -> p (s i)")),
                        "d_wg%d" % pr, 16)
                    c.wait(nc.tensor, "d_wg%d" % pr, vq)
                    if pq_rd[pr]:
                        c.wait(nc.tensor, "s_dve", pq_rd[pr])
                    wqr = wg_sb[pr][:, 0:32 * 128].rearrange(
                        "p (s i) -> p s i", i=128)
                    for kc in range(32):
                        ins = nc.tensor.matmul(out=pq[pr][:, :],
                                               lhsT=wqr[:, kc, :],
                                               rhs=xn_r[:, kc, :],
                                               start=(kc == 0),
                                               stop=(kc == 31))
                    free_gq[pr] = c.inc(ins, "s_pe")
                    nc.vector.wait_ge(c.sems["s_pe"], free_gq[pr])
                    if va_exp:
                        c.wait(nc.vector, "s_act", va_exp)
                    nc.vector.tensor_scalar(
                        out=tmx_sb[:], in0=pq[pr][:, :],
                        scalar1=bq_sb[:, hc:hc + 1], scalar2=0.0,
                        op0=ALU.add, op1=ALU.min)
                    v1 = c.inc(nc.vector.tensor_scalar(
                        out=tex_sb[:], in0=pq[pr][:, :],
                        scalar1=bq_sb[:, hc:hc + 1], scalar2=0.0,
                        op0=ALU.add, op1=ALU.max), "s_dve")
                    pq_rd[pr] = v1
                    c.wait(nc.scalar, "s_dve", v1)
                    va_exp = c.inc(nc.scalar.activation(out=tmx_sb[:],
                                                        in_=tmx_sb[:],
                                                        func=AF.Exp),
                                   "s_act")
                    c.wait(nc.vector, "s_act", va_exp)
                    nc.vector.wait_ge(c.sems["s_dve"], v1)
                    if sig_rd[pr]:
                        c.wait(nc.vector, "s_pe", sig_rd[pr])
                    vs = c.inc(nc.vector.tensor_add(
                        out=sig_sb[pr][:], in0=tmx_sb[:], in1=tex_sb[:]),
                        "s_dve")
                    c.wait(nc.tensor, "s_dve", vs)
                    if den_rd:
                        c.wait(nc.tensor, "s_dve", den_rd)
                    nc.tensor.matmul(out=pD[0:1, :],
                                     lhsT=z_sb[:, hc:hc + 1],
                                     rhs=sig_sb[pr][:], start=True,
                                     stop=True)
                    if pn_rd[pr]:
                        c.wait(nc.tensor, "s_dve", pn_rd[pr])
                    vmm = c.inc(nc.tensor.matmul(
                        out=pn[pr][:, :], lhsT=M_sb[:, hc * HD:(hc + 1) * HD],
                        rhs=sig_sb[pr][:], start=True, stop=True), "s_pe")
                    sig_rd[pr] = vmm
                    nc.vector.wait_ge(c.sems["s_pe"], vmm)
                    if den_wr:
                        nc.vector.wait_ge(c.sems["s_pe"], den_wr)
                    vds = c.inc(nc.vector.tensor_scalar(
                        out=den, in0=pD[0:1, :], scalar1=1e-6, scalar2=None,
                        op0=ALU.add), "s_dve")
                    nc.vector.wait_ge(c.sems["s_dve"], vds)
                    vr = c.inc(nc.vector.reciprocal(out=den, in_=den),
                               "s_dve")
                    den_rd = vr
                    c.wait(nc.tensor, "s_dve", vr)
                    if pb_rd[pr]:
                        c.wait(nc.tensor, "s_act", pb_rd[pr])
                    vb = c.inc(nc.tensor.matmul(out=pb_[pr][:, :],
                                                lhsT=ones1[:], rhs=den,
                                                start=True, stop=True),
                               "s_pe")
                    den_wr = vb
                    c.wait(nc.scalar, "s_pe", vb)
                    if rb_rd[pr]:
                        c.wait(nc.scalar, "s_dve", rb_rd[pr])
                    va = c.inc(nc.scalar.copy(out=rb_sb[pr][:],
                                              in_=pb_[pr][:, :]), "s_act")
                    pb_rd[pr] = va
                    nc.vector.wait_ge(c.sems["s_act"], va)
                    ins = nc.vector.scalar_tensor_tensor(
                        out=ret_r[:, hc, :], in0=pn[pr][:, :], scalar=0.0,
                        in1=rb_sb[pr][:], op0=ALU.bypass, op1=ALU.mult)
                    vret = c.inc(ins, "s_dve")
                    pn_rd[pr] = vret
                    rb_rd[pr] = vret
                c.wait(nc.tensor, "s_dve", vret)
                pyo = [pA, pB]
                pyo_rd = [pq_rd[0], pq_rd[1]]
                for oc in range(32):
                    pr = oc % 4
                    c.wait(nc.scalar, "s_pe", free_dwo[pr])
                    vo = c.inc(nc.scalar.dma_start(
                        out=wd_sb[pr][:, 0:8 * 128],
                        in_=wo_v[oc].rearrange("p s i -> p (s i)")),
                        "d_wd%d" % pr, 16)
                    c.wait(nc.tensor, "d_wd%d" % pr, vo)
                    bk = oc % 2
                    if pyo_rd[bk]:
                        c.wait(nc.tensor, "s_dve", pyo_rd[bk])
                    wor = wd_sb[pr][:, 0:8 * 128].rearrange(
                        "p (s i) -> p s i", i=128)
                    for cc in range(HKV):
                        ins = nc.tensor.matmul(out=pyo[bk][:, :],
                                               lhsT=wor[:, cc, :],
                                               rhs=ret_r[:, cc, :],
                                               start=(cc == 0),
                                               stop=(cc == 7))
                    free_dwo[pr] = c.inc(ins, "s_pe")
                    nc.vector.wait_ge(c.sems["s_pe"], free_dwo[pr])
                    ins = nc.vector.scalar_tensor_tensor(
                        out=y_r[:, oc, :], in0=pyo[bk][:, :], scalar=0.0,
                        in1=y_r[:, oc, :], op0=ALU.bypass, op1=ALU.add)
                    pyo_rd[bk] = c.inc(ins, "s_dve")
                    if oc % 8 == 7:
                        part = oc // 8
                        c.wait(nc.gpsimd, "s_dve", pyo_rd[bk])
                        y_store_v = c.inc(nc.gpsimd.dma_start(
                            out=yT_r[:, part * 8:(part + 1) * 8,
                                     tt * TT:(tt + 1) * TT],
                            in_=y_r[:, part * 8:(part + 1) * 8, :]),
                            "d_y", 16)

        c.wait(nc.gpsimd, "d_y", y_store_v)

    return nc


def make_in_maps(inputs, ncore=NCORE):
    x = np.asarray(inputs["L0_attn_hidden_states"])
    Wq = np.asarray(inputs["L0_attn_Wq"])
    bq = np.asarray(inputs["L0_attn_bq"])
    Wo = np.asarray(inputs["L0_attn_Wo"])
    M = np.asarray(inputs["L0_attn_M"])
    z = np.asarray(inputs["L0_attn_z"])
    T = TT * NT
    xt = np.ascontiguousarray(x.reshape(x.shape[0] * x.shape[1], D).T)
    common = dict(
        wqT=np.ascontiguousarray(
            Wq.reshape(32, HD, D)[::4].reshape(HKV * HD, D).T),
        bqT=np.ascontiguousarray(bq.reshape(32, HD)[::4].T),
        woT=np.ascontiguousarray(Wo.T),
        m=np.ascontiguousarray(M),
        zT=np.ascontiguousarray(z.T),
        g_qw=np.ascontiguousarray(inputs["L0_ffn_gate_qweight"]),
        g_sc=np.ascontiguousarray(inputs["L0_ffn_gate_scales"]),
        g_qz=np.ascontiguousarray(inputs["L0_ffn_gate_qzeros"]),
        u_qw=np.ascontiguousarray(inputs["L0_ffn_up_qweight"]),
        u_sc=np.ascontiguousarray(inputs["L0_ffn_up_scales"]),
        u_qz=np.ascontiguousarray(inputs["L0_ffn_up_qzeros"]),
        d_qw=np.ascontiguousarray(inputs["L0_ffn_down_qweight"]),
        d_sc=np.ascontiguousarray(inputs["L0_ffn_down_scales"]),
        d_qz=np.ascontiguousarray(inputs["L0_ffn_down_qzeros"]),
    )
    in_maps = []
    for cid in range(ncore):
        m_ = dict(common)
        m_["xT"] = np.ascontiguousarray(xt[:, cid * T:(cid + 1) * T])
        in_maps.append(m_)
    return in_maps


def kernel(**inputs):
    in_maps = make_in_maps(inputs)
    nc = build_program()
    res = run_bass_kernel_spmd(nc, in_maps, core_ids=list(range(NCORE)))
    T = TT * NT
    out = np.empty((16384, D), dtype=np.float32)
    for cid in range(NCORE):
        out[cid * T:(cid + 1) * T, :] = res.results[cid]["yT"].T
    return out.reshape(2, 8192, D)


# version 06
# speedup vs baseline: 1.0170x; 1.0170x over previous
import numpy as np
import concourse.bass as bass
import concourse.mybir as mybir
from concourse.bass_utils import run_bass_kernel_spmd
from contextlib import ExitStack

F32 = mybir.dt.float32
BF16 = mybir.dt.bfloat16
I32 = mybir.dt.int32
I8 = mybir.dt.int8
AF = mybir.ActivationFunctionType
ALU = mybir.AluOpType

D = 4096
HKV = 8
HD = 128
I_FFN = 11008
NCORE = 8

CHUNKS = [11, 11, 11, 11, 11, 11, 10, 10]
CHOFF = [0, 11, 22, 33, 44, 55, 66, 76]
MAXCH = 11

COMPUTE_FFN = True
COMPUTE_ATTN = True
TT = 512
NT = 4


class Ctl:
    def __init__(self, nc, stack):
        self.nc = nc
        self.stack = stack
        self.sems = {}
        self.cnt = {}

    def sem(self, name):
        self.sems[name] = self.stack.enter_context(self.nc.semaphore(name))
        self.cnt[name] = 0

    def inc(self, instr, name, k=1):
        instr.then_inc(self.sems[name], k)
        self.cnt[name] += k
        return self.cnt[name]

    def wait(self, eng, name, val=None):
        v = self.cnt[name] if val is None else val
        if v > 0:
            eng.wait_ge(self.sems[name], v)

    def val(self, name):
        return self.cnt[name]


def _dequant_matrix(nc, c, qw_d, sc_d, qz_d, sc_v, G, W, N, NSLOT, ph):
    MASK = 0x0F0F0F0F
    for eng in [nc.vector, nc.gpsimd]:
        c.wait(eng, "d_scr")
    c.wait(nc.gpsimd, "s_pe")
    c.wait(nc.gpsimd, "s_dve")
    c.wait(nc.sync, "s_dve")
    c.wait(nc.scalar, "s_pool")
    vp = 0
    st_rd = [0, 0]
    for si, st in enumerate(range(0, N, 1024)):
        en = min(N, st + 1024)
        spr = si % 2
        if st_rd[spr]:
            c.wait(nc.sync, "s_dve", st_rd[spr])
        v = c.inc(nc.sync.dma_start(out=ph["sc_st"][spr][0:G, 0:en - st],
                                    in_=sc_d[0:G, st:en]),
                  "d_sc%d" % spr, 16)
        c.wait(nc.vector, "d_sc%d" % spr, v)
        nc.vector.wait_ge(c.sems["s_dve"], c.val("s_dve"))
        vp = c.inc(nc.vector.tensor_copy(out=ph["sc_bf"][0:G, st:en],
                                         in_=ph["sc_st"][spr][0:G, 0:en - st]),
                   "s_dve")
        st_rd[spr] = vp
    vsc = vp
    v = c.inc(nc.sync.dma_start(out=ph["qz"][0:G, 0:W], in_=qz_d), "d_qz", 16)
    vms = c.inc(nc.gpsimd.memset(ph["negzs"][:, 0:N], 0.0), "s_pool")
    c.wait(nc.vector, "d_qz", v)
    nc.vector.wait_ge(c.sems["s_dve"], c.val("s_dve"))
    nc.vector.tensor_scalar(
        out=ph["qe"][0:G, 0:W], in0=ph["qz"][0:G, 0:W], scalar1=MASK,
        scalar2=None, op0=ALU.bitwise_and)
    vz = c.inc(nc.vector.tensor_scalar(
        out=ph["qo"][0:G, 0:W], in0=ph["qz"][0:G, 0:W], scalar1=4,
        scalar2=MASK, op0=ALU.logical_shift_right, op1=ALU.bitwise_and),
        "s_dve")
    nc.vector.wait_ge(c.sems["s_dve"], vz)
    c.wait(nc.vector, "s_pool", vms)
    qe8 = ph["qe"][0:G, 0:W].bitcast(I8)
    qo8 = ph["qo"][0:G, 0:W].bitcast(I8)
    nc.vector.scalar_tensor_tensor(
        out=ph["negzs"][0:G, 0:N:2], in0=qe8, scalar=-8.0,
        in1=ph["sc_bf"][0:G, 0:N:2], op0=ALU.add, op1=ALU.mult)
    vz2 = c.inc(nc.vector.scalar_tensor_tensor(
        out=ph["negzs"][0:G, 1:N:2], in0=qo8, scalar=-8.0,
        in1=ph["sc_bf"][0:G, 1:N:2], op0=ALU.add, op1=ALU.mult), "s_dve")
    c.wait(nc.scalar, "s_dve", vz2)
    c.inc(nc.scalar.dma_start(
        out=sc_v[:, :, NSLOT - 1, :].rearrange("t p i -> p t i"),
        in_=ph["negzs"][:, 0:N].rearrange("p (t i) -> p t i", i=128)),
        "d_scr", 16)

    dq = ph["deq"][:, 0:2 * N]
    dqv = dq.rearrange("p (t two i) -> p t two i", two=2, i=128)
    qw_r = qw_d.rearrange("(g p) w -> g p w", p=128)
    qw_rd = [0, 0]
    ps_rd = [0, 0]
    deq_wr = 0
    rs_built = 0
    for g in range(G):
        parity = g % 2
        qbuf = ph["qw"][parity]
        if qw_rd[parity]:
            c.wait(nc.sync, "s_dve", qw_rd[parity])
        v = c.inc(nc.sync.dma_start(out=qbuf[:, 0:W], in_=qw_r[g]),
                  "d_qw%d" % parity, 16)
        if rs_built:
            c.wait(nc.vector, "s_pe", rs_built)
        vrs = c.inc(nc.vector.tensor_scalar(
            out=ph["rs_t"][0:G, 0:128], in0=ph["iota"][0:G, :],
            scalar1=g, scalar2=None, op0=ALU.is_equal), "s_dve")
        c.wait(nc.vector, "d_qw%d" % parity, v)
        nc.vector.wait_ge(c.sems["s_dve"], c.val("s_dve"))
        nc.vector.tensor_scalar(
            out=ph["qe"][:, 0:W], in0=qbuf[:, 0:W], scalar1=MASK,
            scalar2=None, op0=ALU.bitwise_and)
        qw_rd[parity] = c.inc(nc.vector.tensor_scalar(
            out=ph["qo"][:, 0:W], in0=qbuf[:, 0:W], scalar1=4,
            scalar2=MASK, op0=ALU.logical_shift_right, op1=ALU.bitwise_and),
            "s_dve")
        vunp = qw_rd[parity]
        qe8 = ph["qe"][:, 0:W].bitcast(I8)
        qo8 = ph["qo"][:, 0:W].bitcast(I8)
        for ci, ch0 in enumerate(range(0, N, 2048)):
            cw = min(N, ch0 + 2048) - ch0
            bi = ci % 2
            pb = ph["p_s"][bi]
            if ps_rd[bi]:
                c.wait(nc.tensor, "s_dve", ps_rd[bi])
            c.wait(nc.tensor, "s_dve", max(vsc, vrs))
            for s0 in range(0, cw, 512):
                sw = min(cw, s0 + 512) - s0
                vmm = c.inc(nc.tensor.matmul(
                    out=pb[:, s0:s0 + sw], lhsT=ph["rs_t"][0:G, 0:128],
                    rhs=ph["sc_bf"][0:G, ch0 + s0:ch0 + s0 + sw],
                    start=True, stop=True), "s_pe")
            rs_built = vmm
            nc.vector.wait_ge(c.sems["s_pe"], vmm)
            nc.vector.wait_ge(c.sems["s_dve"], vunp)
            if deq_wr and ch0 == 0:
                c.wait(nc.vector, "d_scr", deq_wr)
            a0, a1 = ch0 // 128, (ch0 + cw) // 128
            ins = nc.vector.scalar_tensor_tensor(
                out=dqv[:, a0:a1, parity, 0:128:2],
                in0=qe8[:, ch0 // 2:(ch0 + cw) // 2].rearrange(
                    "p (a i) -> p a i", i=64),
                scalar=-8.0,
                in1=pb[:, 0:cw:2].rearrange("p (a i) -> p a i", i=64),
                op0=ALU.add, op1=ALU.mult)
            ins = nc.vector.scalar_tensor_tensor(
                out=dqv[:, a0:a1, parity, 1:128:2],
                in0=qo8[:, ch0 // 2:(ch0 + cw) // 2].rearrange(
                    "p (a i) -> p a i", i=64),
                scalar=-8.0,
                in1=pb[:, 1:cw:2].rearrange("p (a i) -> p a i", i=64),
                op0=ALU.add, op1=ALU.mult)
            ps_rd[bi] = c.inc(ins, "s_dve")
        if parity == 1:
            c.wait(nc.scalar, "s_dve")
            deq_wr = c.inc(nc.scalar.dma_start(
                out=sc_v[:, :, g - 1:g + 1, :].rearrange(
                    "t p s i -> p t (s i)"),
                in_=dqv.rearrange("p t two i -> p t (two i)")),
                "d_scr", 16)


def build_program():
    T = TT * NT
    nc = bass.Bass("TRN2", target_bir_lowering=False)

    xT = nc.dram_tensor("xT", [D, T], F32, kind="ExternalInput").ap()
    wqT = nc.dram_tensor("wqT", [D, HKV * HD], F32, kind="ExternalInput").ap()
    bqT = nc.dram_tensor("bqT", [HD, HKV], F32, kind="ExternalInput").ap()
    woT = nc.dram_tensor("woT", [D, D], F32, kind="ExternalInput").ap()
    m_in = nc.dram_tensor("m", [HKV, HD, HD], F32, kind="ExternalInput").ap()
    zT = nc.dram_tensor("zT", [HD, HKV], F32, kind="ExternalInput").ap()
    g_qw = nc.dram_tensor("g_qw", [D, I_FFN // 8], I32, kind="ExternalInput").ap()
    g_sc = nc.dram_tensor("g_sc", [32, I_FFN], F32, kind="ExternalInput").ap()
    g_qz = nc.dram_tensor("g_qz", [32, I_FFN // 8], I32, kind="ExternalInput").ap()
    u_qw = nc.dram_tensor("u_qw", [D, I_FFN // 8], I32, kind="ExternalInput").ap()
    u_sc = nc.dram_tensor("u_sc", [32, I_FFN], F32, kind="ExternalInput").ap()
    u_qz = nc.dram_tensor("u_qz", [32, I_FFN // 8], I32, kind="ExternalInput").ap()
    d_qw = nc.dram_tensor("d_qw", [I_FFN, D // 8], I32, kind="ExternalInput").ap()
    d_sc = nc.dram_tensor("d_sc", [86, D], F32, kind="ExternalInput").ap()
    d_qz = nc.dram_tensor("d_qz", [86, D // 8], I32, kind="ExternalInput").ap()
    yT = nc.dram_tensor("yT", [D, T], F32, kind="ExternalOutput").ap()

    sc_g = nc.dram_tensor("sc_g", [86 * 128 * 33 * 128], BF16)
    sc_u = nc.dram_tensor("sc_u", [86 * 128 * 33 * 128], BF16)
    sc_d = nc.dram_tensor("sc_d", [32 * 128 * 87 * 128], BF16)
    sc_wq = nc.dram_tensor("sc_wq", [8 * 128 * 32 * 128], BF16)
    sc_wo = nc.dram_tensor("sc_wo", [32 * 128 * 8 * 128], BF16)
    g_v = sc_g.ap().rearrange("(t p s i) -> t p s i", p=128, s=33, i=128)
    u_v = sc_u.ap().rearrange("(t p s i) -> t p s i", p=128, s=33, i=128)
    d_v = sc_d.ap().rearrange("(t p s i) -> t p s i", p=128, s=87, i=128)
    wq_v = sc_wq.ap().rearrange("(h p k i) -> h p k i", p=128, k=32, i=128)
    wo_v = sc_wo.ap().rearrange("(o p k i) -> o p k i", p=128, k=8, i=128)

    with ExitStack() as stack:
        e = stack.enter_context
        c = Ctl(nc, stack)
        DMA_SEMS = ["d_in0", "d_in1", "d_sc0", "d_sc1", "d_qz", "d_qw0", "d_qw1",
                    "d_scr", "d_wg0", "d_wg1", "d_wu0", "d_wu1",
                    "d_wd0", "d_wd1", "d_wd2", "d_wd3", "d_x", "d_y"]
        for s in DMA_SEMS + ["s_pe", "s_dve", "s_act", "s_pool"]:
            c.sem(s)
        c.dma_sems = DMA_SEMS

        M_sb = e(nc.sbuf_tensor("M_sb", [128, HKV * HD], BF16))
        z_sb = e(nc.sbuf_tensor("z_sb", [128, HKV], BF16))
        bq_sb = e(nc.sbuf_tensor("bq_sb", [128, HKV], F32))
        onesK = e(nc.sbuf_tensor("onesK", [128, 1], BF16))
        ones1 = e(nc.sbuf_tensor("ones1", [1, 128], F32))
        D32 = e(nc.sbuf_tensor("D32", [128, 64], BF16))
        D86 = e(nc.sbuf_tensor("D86", [128, 172], BF16))
        sgp = e(nc.sbuf_tensor("sgp", [128, TT], BF16))
        sdp = e(nc.sbuf_tensor("sdp", [128, TT], BF16))
        cst = e(nc.sbuf_tensor("cst", [128, 2], F32))

        nc.vector.memset(onesK[:], 1.0 / 4096.0)
        nc.vector.memset(ones1[:], 1.0)
        nc.vector.memset(D32[:], 0.0)
        nc.vector.memset(D86[:], 0.0)
        nc.vector.memset(sgp[:], 0.0)
        nc.vector.memset(sdp[:], 0.0)
        vm0 = c.inc(nc.vector.memset(cst[:, 0:1], 1e-5), "s_dve")
        nc.vector.wait_ge(c.sems["s_dve"], vm0)
        nc.vector.memset(D32[:, 32:33], 1.0)
        c.inc(nc.vector.memset(D86[:, 86:87], 1.0), "s_dve")

        with ExitStack() as p0a:
            e0 = p0a.enter_context
            ph = {}
            ph["sc_st"] = [e0(nc.sbuf_tensor(f"p0_scst{i}", [86, 1024], F32))
                           for i in range(2)]
            ph["sc_bf"] = e0(nc.sbuf_tensor("p0_scb", [86, I_FFN], BF16))
            ph["qz"] = e0(nc.sbuf_tensor("p0_qz", [86, I_FFN // 8], I32))
            ph["negzs"] = e0(nc.sbuf_tensor("p0_negzs", [128, I_FFN], BF16))
            ph["qw"] = [e0(nc.sbuf_tensor(f"p0_qw{i}", [128, I_FFN // 8], I32))
                        for i in range(2)]
            ph["qe"] = e0(nc.sbuf_tensor("p0_qe", [128, I_FFN // 8], I32))
            ph["qo"] = e0(nc.sbuf_tensor("p0_qo", [128, I_FFN // 8], I32))
            ph["deq"] = e0(nc.sbuf_tensor("p0_deq", [128, 2 * I_FFN], BF16))
            ph["p_s"] = [e0(nc.psum_tensor(f"p0_ps{i}", [128, 2048], F32))
                         for i in range(2)]
            ph["rs_t"] = e0(nc.sbuf_tensor("p0_rst", [96, 128], BF16))
            ph["iota"] = e0(nc.sbuf_tensor("p0_io96", [96, 128], I32))

            vio = c.inc(nc.gpsimd.iota(ph["iota"][:], pattern=[[0, 128]],
                                       base=0, channel_multiplier=1),
                        "s_pool")
            c.wait(nc.vector, "s_pool", vio)

            _dequant_matrix(nc, c, g_qw, g_sc, g_qz, g_v, 32, I_FFN // 8,
                            I_FFN, 33, ph)
            _dequant_matrix(nc, c, u_qw, u_sc, u_qz, u_v, 32, I_FFN // 8,
                            I_FFN, 33, ph)
            _dequant_matrix(nc, c, d_qw, d_sc, d_qz, d_v, 86, D // 8,
                            D, 87, ph)

            wtmp = [e0(nc.sbuf_tensor(f"p0_wt{i}", [128, D], F32))
                    for i in range(2)]
            wcast = e0(nc.sbuf_tensor("p0_wc", [128, 2 * D], BF16))

            v = c.inc(nc.sync.dma_start(
                out=wtmp[0][:, 0:HKV * HD].rearrange(
                    "p (kv e) -> p kv e", e=HD),
                in_=m_in.rearrange("kv d e -> d kv e")), "d_in0", 16)
            c.wait(nc.gpsimd, "d_in0", v)
            nc.gpsimd.wait_ge(c.sems["s_pool"], c.val("s_pool"))
            vmcast = c.inc(nc.gpsimd.tensor_copy(
                out=M_sb[:], in_=wtmp[0][:, 0:HKV * HD]), "s_pool")
            v = c.inc(nc.sync.dma_start(out=wtmp[1][:, 0:HKV], in_=zT),
                      "d_in1", 16)
            c.wait(nc.gpsimd, "d_in1", v)
            vmz = c.inc(nc.gpsimd.tensor_copy(out=z_sb[:],
                                              in_=wtmp[1][:, 0:HKV]),
                        "s_pool")
            c.inc(nc.sync.dma_start(out=bq_sb[:], in_=bqT), "d_qz", 16)

            wcv = wcast[:, 0:HKV * 2 * HD].rearrange(
                "p (h two i) -> p h two i", two=2, i=128)
            wqT_r = wqT.rearrange("(k p) h -> k p h", p=128)
            wq_rd = [vmcast, vmz]
            wq_wr = 0
            for kc in range(32):
                parity = kc % 2
                c.wait(nc.sync, "s_pool", wq_rd[parity])
                v = c.inc(nc.sync.dma_start(out=wtmp[parity][:, 0:HKV * HD],
                                            in_=wqT_r[kc]),
                          "d_in%d" % parity, 16)
                c.wait(nc.gpsimd, "d_in%d" % parity, v)
                if wq_wr:
                    c.wait(nc.gpsimd, "d_scr", wq_wr)
                nc.gpsimd.wait_ge(c.sems["s_pool"], c.val("s_pool"))
                ins = nc.gpsimd.tensor_copy(
                    out=wcv[:, :, parity, :],
                    in_=wtmp[parity][:, 0:HKV * HD].rearrange(
                        "p (h i) -> p h i", i=128))
                wq_rd[parity] = c.inc(ins, "s_pool")
                if parity == 1:
                    c.wait(nc.scalar, "s_pool")
                    wq_wr = c.inc(nc.scalar.dma_start(
                        out=wq_v[:, :, kc - 1:kc + 1, :].rearrange(
                            "h p s i -> p h (s i)"),
                        in_=wcv.rearrange("p h two i -> p h (two i)")),
                        "d_scr", 16)

            wov = wcast[:].rearrange("p (o two i) -> p o two i", two=2, i=128)
            woT_r = woT.rearrange("(k p) o -> k p o", p=128)
            wo_wr = wq_wr
            wo_rd = c.val("s_pool")
            for kv in range(HKV):
                parity = kv % 2
                a, b = wtmp[0], wtmp[1]
                c.wait(nc.sync, "s_pool", wo_rd)
                v0 = c.inc(nc.sync.dma_start(out=a[:], in_=woT_r[4 * kv]),
                           "d_in0", 16)
                v1 = c.inc(nc.sync.dma_start(out=b[:], in_=woT_r[4 * kv + 1]),
                           "d_in1", 16)
                c.wait(nc.gpsimd, "d_in0", v0)
                c.wait(nc.gpsimd, "d_in1", v1)
                nc.gpsimd.wait_ge(c.sems["s_pool"], c.val("s_pool"))
                s1 = c.inc(nc.gpsimd.tensor_add(out=a[:], in0=a[:], in1=b[:]),
                           "s_pool")
                c.wait(nc.sync, "s_pool", s1)
                v2 = c.inc(nc.sync.dma_start(out=b[:], in_=woT_r[4 * kv + 2]),
                           "d_in1", 16)
                c.wait(nc.gpsimd, "d_in1", v2)
                nc.gpsimd.wait_ge(c.sems["s_pool"], s1)
                s2 = c.inc(nc.gpsimd.tensor_add(out=a[:], in0=a[:], in1=b[:]),
                           "s_pool")
                c.wait(nc.sync, "s_pool", s2)
                v3 = c.inc(nc.sync.dma_start(out=b[:], in_=woT_r[4 * kv + 3]),
                           "d_in1", 16)
                c.wait(nc.gpsimd, "d_in1", v3)
                nc.gpsimd.wait_ge(c.sems["s_pool"], s2)
                if wo_wr:
                    c.wait(nc.gpsimd, "d_scr", wo_wr)
                s3 = c.inc(nc.gpsimd.tensor_add(
                    out=wov[:, :, parity, :],
                    in0=a[:].rearrange("p (o i) -> p o i", i=128),
                    in1=b[:].rearrange("p (o i) -> p o i", i=128)),
                    "s_pool")
                wo_rd = s3
                if parity == 1:
                    c.wait(nc.scalar, "s_pool", s3)
                    wo_wr = c.inc(nc.scalar.dma_start(
                        out=wo_v[:, :, kv - 1:kv + 1, :].rearrange(
                            "o p s i -> p o (s i)"),
                        in_=wov.rearrange("p o two i -> p o (two i)")),
                        "d_scr", 16)

        y_sb = e(nc.sbuf_tensor("y_sb", [128, 32 * TT], F32))
        xn_sb = e(nc.sbuf_tensor("xn_sb", [128, 32 * TT], BF16))
        h_sb = [e(nc.sbuf_tensor(f"h_sb{i}", [128, MAXCH * TT], BF16))
                for i in range(2)]
        wg_sb = [e(nc.sbuf_tensor(f"wg_sb{i}", [128, 33 * 128], BF16))
                 for i in range(2)]
        wu_sb = [e(nc.sbuf_tensor(f"wu_sb{i}", [128, 33 * 128], BF16))
                 for i in range(2)]
        wd_sb = [e(nc.sbuf_tensor(f"wd_sb{i}", [128, (MAXCH + 1) * 128], BF16))
                 for i in range(4)]
        sig_sb = [e(nc.sbuf_tensor(f"sig_sb{i}", [128, TT], BF16))
                  for i in range(2)]
        ret_sb = e(nc.sbuf_tensor("ret_sb", [128, HKV * TT], BF16))
        sq_sb = [e(nc.sbuf_tensor(f"sq_sb{i}", [128, TT], BF16))
                 for i in range(4)]
        t1_sb = [e(nc.sbuf_tensor(f"t1_sb{i}", [128, TT], F32))
                 for i in range(2)]
        silu_sb = [e(nc.sbuf_tensor(f"silu_sb{i}", [128, TT], BF16))
                   for i in range(2)]
        rb_sb = [e(nc.sbuf_tensor(f"rb_sb{i}", [128, TT], F32))
                 for i in range(2)]
        tex_sb = e(nc.sbuf_tensor("tex_sb", [128, TT], F32))
        tmx_sb = e(nc.sbuf_tensor("tmx_sb", [128, TT], F32))
        row_sb = e(nc.sbuf_tensor("row_sb", [1, 5 * TT], F32))
        pA = e(nc.psum_tensor("pA", [128, TT], F32))
        pB = e(nc.psum_tensor("pB", [128, TT], F32))
        pC = e(nc.psum_tensor("pC", [128, TT], F32))
        pC2 = e(nc.psum_tensor("pC2", [128, TT], F32))
        pD = e(nc.psum_tensor("pD", [128, TT], F32))
        pF = e(nc.psum_tensor("pF", [128, TT], F32))
        pF2 = e(nc.psum_tensor("pF2", [128, TT], F32))
        pE = e(nc.psum_tensor("pE", [128, TT], F32))

        snap = {k: c.val(k) for k in c.cnt}
        for eng in [nc.sync, nc.scalar, nc.vector, nc.gpsimd, nc.tensor]:
            for s in ["d_in0", "d_in1", "d_sc0", "d_sc1", "d_qz", "d_qw0", "d_qw1",
                      "d_scr", "s_dve", "s_pool", "s_pe", "s_act"]:
                c.wait(eng, s, snap[s])

        xT_r = xT.rearrange("(k p) t -> p k t", p=128)
        yT_r = yT.rearrange("(k p) t -> p k t", p=128)
        y_r = y_sb[:].rearrange("p (k t) -> p k t", t=TT)
        xn_r = xn_sb[:].rearrange("p (k t) -> p k t", t=TT)

        free_gq = [0, 0]
        free_wu = [0, 0]
        free_dwo = [0, 0, 0, 0]
        free_h = [0, 0]
        y_store_v = 0

        for tt in range(NT):
            snap = {k: c.val(k) for k in ["s_pe", "s_dve", "s_act", "s_pool"]}
            for eng in [nc.tensor, nc.vector, nc.gpsimd, nc.scalar]:
                for s in ["s_pe", "s_dve", "s_act", "s_pool"]:
                    c.wait(eng, s, snap[s])

            c.wait(nc.gpsimd, "d_y", y_store_v)
            xload_v = 0
            for part in range(4):
                ins = nc.gpsimd.dma_start(
                    out=y_r[:, part * 8:(part + 1) * 8, :],
                    in_=xT_r[:, part * 8:(part + 1) * 8,
                             tt * TT:(tt + 1) * TT])
                xload_v = c.inc(ins, "d_x", 16)

            c.wait(nc.gpsimd, "d_x", xload_v)
            for kc in range(32):
                ins = nc.gpsimd.tensor_copy(out=xn_r[:, kc, :],
                                            in_=y_r[:, kc, :])
            vxt = c.inc(ins, "s_pool")
            c.wait(nc.tensor, "s_pool", vxt)
            for kc in range(32):
                nc.tensor.matmul(out=pA[0:1, :], lhsT=onesK[:],
                                 rhs=xn_r[:, kc, :], start=(kc == 0),
                                 stop=(kc == 31))
            c.wait(nc.scalar, "d_x", xload_v)
            sq_done = [0, 0, 0, 0]
            for kc in range(32):
                pr = kc % 4
                if sq_done[pr]:
                    c.wait(nc.scalar, "s_pe", sq_done[pr])
                va = c.inc(nc.scalar.activation(out=sq_sb[pr][:],
                                                in_=y_r[:, kc, :],
                                                func=AF.Square), "s_act")
                c.wait(nc.tensor, "s_act", va)
                ins = nc.tensor.matmul(out=pB[0:1, :], lhsT=onesK[:],
                                       rhs=sq_sb[pr][:], start=(kc == 0),
                                       stop=(kc == 31))
                sq_done[pr] = c.inc(ins, "s_pe")
            vstats = max(sq_done)
            mean = row_sb[0:1, 0:TT]
            ex2 = row_sb[0:1, TT:2 * TT]
            var = ex2
            m2 = row_sb[0:1, 2 * TT:3 * TT]
            rstd = row_sb[0:1, 3 * TT:4 * TT]
            mr = m2
            den = row_sb[0:1, 4 * TT:5 * TT]
            nc.vector.wait_ge(c.sems["s_pe"], vstats)
            nc.vector.tensor_copy(out=mean, in_=pA[0:1, :])
            vcp = c.inc(nc.vector.tensor_copy(out=ex2, in_=pB[0:1, :]),
                        "s_dve")
            nc.vector.wait_ge(c.sems["s_dve"], vcp)
            vmu = c.inc(nc.vector.tensor_mul(out=m2, in0=mean, in1=mean),
                        "s_dve")
            nc.vector.wait_ge(c.sems["s_dve"], vmu)
            vrow1 = c.inc(nc.vector.tensor_sub(out=var, in0=ex2, in1=m2),
                          "s_dve")
            c.wait(nc.scalar, "s_dve", vrow1)
            vln = c.inc(nc.scalar.activation(out=var, in_=var, func=AF.Ln,
                                             bias=cst[0:1, 0:1]), "s_act")
            nc.scalar.wait_ge(c.sems["s_act"], vln)
            vact2 = c.inc(nc.scalar.activation(out=rstd, in_=var, func=AF.Exp,
                                               scale=-0.5), "s_act")
            c.wait(nc.vector, "s_act", vact2)
            vrow2 = c.inc(nc.vector.tensor_mul(out=mr, in0=mean, in1=rstd),
                          "s_dve")
            c.wait(nc.tensor, "s_dve", vrow2)
            nc.tensor.matmul(out=pC[:, :], lhsT=ones1[:], rhs=rstd,
                             start=True, stop=True)
            vbc = c.inc(nc.tensor.matmul(out=pD[:, :], lhsT=ones1[:], rhs=mr,
                                         start=True, stop=True), "s_pe")
            c.wait(nc.scalar, "s_pe", vbc)
            nc.scalar.copy(out=rb_sb[0][:], in_=pC[:, :])
            vev = c.inc(nc.scalar.copy(out=rb_sb[1][:], in_=pD[:, :]),
                        "s_act")
            c.wait(nc.vector, "s_act", vev)
            c.wait(nc.gpsimd, "s_act", vev)
            for kc in range(16):
                pr = kc % 2
                vt = c.inc(nc.vector.tensor_mul(out=t1_sb[pr][:],
                                                in0=y_r[:, kc, :],
                                                in1=rb_sb[0][:]), "s_dve")
                nc.vector.wait_ge(c.sems["s_dve"], vt)
                ins = nc.vector.tensor_sub(out=xn_r[:, kc, :],
                                           in0=t1_sb[pr][:], in1=rb_sb[1][:])
            vxn = c.inc(ins, "s_dve")
            tp = [tex_sb, tmx_sb]
            for kc in range(16, 32):
                pr = kc % 2
                vt = c.inc(nc.gpsimd.tensor_mul(out=tp[pr][:],
                                                in0=y_r[:, kc, :],
                                                in1=rb_sb[0][:]), "s_pool")
                nc.gpsimd.wait_ge(c.sems["s_pool"], vt)
                ins = nc.gpsimd.tensor_sub(out=xn_r[:, kc, :],
                                           in0=tp[pr][:], in1=rb_sb[1][:])
            vxn_p = c.inc(ins, "s_pool")

            if COMPUTE_FFN:
                c.wait(nc.tensor, "s_dve", vxn)
                c.wait(nc.tensor, "s_pool", vxn_p)
                for kc in range(32):
                    ins = nc.tensor.matmul(
                        out=pE[0:32, :], lhsT=D32[:, 32 - kc:64 - kc],
                        rhs=xn_r[:, kc, :], start=(kc == 0), stop=(kc == 31))
                vsg = c.inc(ins, "s_pe")
                nc.vector.wait_ge(c.sems["s_pe"], vsg)
                vsgp = c.inc(nc.vector.tensor_scalar(
                    out=sgp[0:32, :], in0=pE[0:32, :], scalar1=-1.0,
                    scalar2=None, op0=ALU.mult), "s_dve")

                pg = [pA, pB]
                pu = [pC, pC2]
                py = [pD, pF]
                pg_rd = [0, 0]
                pu_rd = [0, 0]
                py_rd = [0, 0]
                silu_rd = [0, 0]
                vsdp = 0
                sgp_mm = 0
                for q in range(8):
                    nic = CHUNKS[q]
                    hq = h_sb[q % 2]
                    hq_r = hq[:].rearrange("p (s t) -> p s t", t=TT)
                    h_chunk_wr = 0
                    for j in range(nic):
                        ic = CHOFF[q] + j
                        pr = ic % 2
                        c.wait(nc.sync, "s_pe", free_gq[pr])
                        vg = c.inc(nc.sync.dma_start(
                            out=wg_sb[pr][:],
                            in_=g_v[ic].rearrange("p s i -> p (s i)")),
                            "d_wg%d" % pr, 16)
                        c.wait(nc.sync, "s_pe", free_wu[pr])
                        vu = c.inc(nc.sync.dma_start(
                            out=wu_sb[pr][:],
                            in_=u_v[ic].rearrange("p s i -> p (s i)")),
                            "d_wu%d" % pr, 16)
                        c.wait(nc.tensor, "d_wg%d" % pr, vg)
                        c.wait(nc.tensor, "s_dve", vsgp)
                        if pg_rd[pr]:
                            c.wait(nc.tensor, "s_act", pg_rd[pr])
                        wg_r = wg_sb[pr][:].rearrange("p (s i) -> p s i",
                                                      i=128)
                        for kc in range(32):
                            nc.tensor.matmul(out=pg[pr][:, :],
                                             lhsT=wg_r[:, kc, :],
                                             rhs=xn_r[:, kc, :],
                                             start=(kc == 0), stop=False)
                        ins = nc.tensor.matmul(out=pg[pr][:, :],
                                               lhsT=wg_r[:, 32, :],
                                               rhs=sgp[:, :],
                                               start=False, stop=True)
                        free_gq[pr] = sgp_mm = c.inc(ins, "s_pe")
                        c.wait(nc.tensor, "d_wu%d" % pr, vu)
                        if pu_rd[pr]:
                            c.wait(nc.tensor, "s_dve", pu_rd[pr])
                        wu_r = wu_sb[pr][:].rearrange("p (s i) -> p s i",
                                                      i=128)
                        for kc in range(32):
                            nc.tensor.matmul(out=pu[pr][:, :],
                                             lhsT=wu_r[:, kc, :],
                                             rhs=xn_r[:, kc, :],
                                             start=(kc == 0), stop=False)
                        ins = nc.tensor.matmul(out=pu[pr][:, :],
                                               lhsT=wu_r[:, 32, :],
                                               rhs=sgp[:, :],
                                               start=False, stop=True)
                        free_wu[pr] = c.inc(ins, "s_pe")
                        c.wait(nc.scalar, "s_pe", free_gq[pr])
                        if silu_rd[pr]:
                            c.wait(nc.scalar, "s_dve", silu_rd[pr])
                        va = c.inc(nc.scalar.activation(
                            out=silu_sb[pr][:], in_=pg[pr][:, :],
                            func=AF.Silu), "s_act")
                        pg_rd[pr] = va
                        nc.vector.wait_ge(c.sems["s_act"], va)
                        nc.vector.wait_ge(c.sems["s_pe"], free_wu[pr])
                        if free_h[q % 2]:
                            c.wait(nc.vector, "s_pe", free_h[q % 2])
                        ins = nc.vector.scalar_tensor_tensor(
                            out=hq_r[:, j, :], in0=pu[pr][:, :], scalar=0.0,
                            in1=silu_sb[pr][:], op0=ALU.bypass, op1=ALU.mult)
                        h_chunk_wr = c.inc(ins, "s_dve")
                        pu_rd[pr] = h_chunk_wr
                        silu_rd[pr] = h_chunk_wr
                    c.wait(nc.tensor, "s_dve", h_chunk_wr)
                    for j in range(nic):
                        ic = CHOFF[q] + j
                        ins = nc.tensor.matmul(
                            out=pE[0:86, :], lhsT=D86[:, 86 - ic:172 - ic],
                            rhs=hq_r[:, j, :], start=(ic == 0),
                            stop=(ic == 85))
                    if q == 7:
                        vsd = c.inc(ins, "s_pe")
                        nc.vector.wait_ge(c.sems["s_pe"], vsd)
                        vsdp = c.inc(nc.vector.tensor_scalar(
                            out=sdp[0:86, :], in0=pE[0:86, :], scalar1=-1.0,
                            scalar2=None, op0=ALU.mult), "s_dve")
                    s0 = CHOFF[q]
                    nsl = nic + (1 if q == 7 else 0)
                    for oc in range(32):
                        pr = oc % 4
                        c.wait(nc.scalar, "s_pe", free_dwo[pr])
                        vd = c.inc(nc.scalar.dma_start(
                            out=wd_sb[pr][:, 0:nsl * 128],
                            in_=d_v[oc, :, s0:s0 + nsl, :].rearrange(
                                "p s i -> p (s i)")), "d_wd%d" % pr, 16)
                        c.wait(nc.tensor, "d_wd%d" % pr, vd)
                        if q == 7:
                            c.wait(nc.tensor, "s_dve", vsdp)
                        bk = oc % 2
                        if py_rd[bk]:
                            c.wait(nc.tensor, "s_dve", py_rd[bk])
                        wd_r = wd_sb[pr][:, 0:nsl * 128].rearrange(
                            "p (s i) -> p s i", i=128)
                        for j in range(nic):
                            ins = nc.tensor.matmul(
                                out=py[bk][:, :], lhsT=wd_r[:, j, :],
                                rhs=hq_r[:, j, :], start=(j == 0),
                                stop=(j == nic - 1 and q != 7))
                        if q == 7:
                            ins = nc.tensor.matmul(out=py[bk][:, :],
                                                   lhsT=wd_r[:, nic, :],
                                                   rhs=sdp[:, :],
                                                   start=False, stop=True)
                        free_dwo[pr] = c.inc(ins, "s_pe")
                        nc.vector.wait_ge(c.sems["s_pe"], free_dwo[pr])
                        ins = nc.vector.scalar_tensor_tensor(
                            out=y_r[:, oc, :], in0=py[bk][:, :],
                            scalar=0.0, in1=y_r[:, oc, :], op0=ALU.bypass,
                            op1=ALU.add)
                        py_rd[bk] = c.inc(ins, "s_dve")
                    free_h[q % 2] = free_dwo[(31) % 2]

            if COMPUTE_ATTN:
                ret_r = ret_sb[:].rearrange("p (h t) -> p h t", t=TT)
                pq = [pA, pB]
                pn = [pC, pC2]
                pb_ = [pF, pF2]
                pq_rd = [0, 0]
                pn_rd = [0, 0]
                pb_rd = [0, 0]
                sig_rd = [0, 0]
                rb_rd = [0, 0]
                den_wr = 0
                den_rd = 0
                va_exp = 0
                vret = 0
                ffn_dve = c.val("s_dve")
                ffn_act = c.val("s_act")
                c.wait(nc.tensor, "s_dve", ffn_dve)
                c.wait(nc.tensor, "s_act", ffn_act)
                for hc in range(HKV):
                    pr = hc % 2
                    c.wait(nc.sync, "s_pe", free_gq[pr])
                    vq = c.inc(nc.sync.dma_start(
                        out=wg_sb[pr][:, 0:32 * 128],
                        in_=wq_v[hc].rearrange("p s i -> p (s i)")),
                        "d_wg%d" % pr, 16)
                    c.wait(nc.tensor, "d_wg%d" % pr, vq)
                    c.wait(nc.tensor, "s_pool", vxn_p)
                    if pq_rd[pr]:
                        c.wait(nc.tensor, "s_dve", pq_rd[pr])
                    wqr = wg_sb[pr][:, 0:32 * 128].rearrange(
                        "p (s i) -> p s i", i=128)
                    for kc in range(32):
                        ins = nc.tensor.matmul(out=pq[pr][:, :],
                                               lhsT=wqr[:, kc, :],
                                               rhs=xn_r[:, kc, :],
                                               start=(kc == 0),
                                               stop=(kc == 31))
                    free_gq[pr] = c.inc(ins, "s_pe")
                    nc.vector.wait_ge(c.sems["s_pe"], free_gq[pr])
                    if va_exp:
                        c.wait(nc.vector, "s_act", va_exp)
                    nc.vector.tensor_scalar(
                        out=tmx_sb[:], in0=pq[pr][:, :],
                        scalar1=bq_sb[:, hc:hc + 1], scalar2=0.0,
                        op0=ALU.add, op1=ALU.min)
                    v1 = c.inc(nc.vector.tensor_scalar(
                        out=tex_sb[:], in0=pq[pr][:, :],
                        scalar1=bq_sb[:, hc:hc + 1], scalar2=0.0,
                        op0=ALU.add, op1=ALU.max), "s_dve")
                    pq_rd[pr] = v1
                    c.wait(nc.scalar, "s_dve", v1)
                    va_exp = c.inc(nc.scalar.activation(out=tmx_sb[:],
                                                        in_=tmx_sb[:],
                                                        func=AF.Exp),
                                   "s_act")
                    c.wait(nc.vector, "s_act", va_exp)
                    nc.vector.wait_ge(c.sems["s_dve"], v1)
                    if sig_rd[pr]:
                        c.wait(nc.vector, "s_pe", sig_rd[pr])
                    vs = c.inc(nc.vector.tensor_add(
                        out=sig_sb[pr][:], in0=tmx_sb[:], in1=tex_sb[:]),
                        "s_dve")
                    c.wait(nc.tensor, "s_dve", vs)
                    if den_rd:
                        c.wait(nc.tensor, "s_dve", den_rd)
                    nc.tensor.matmul(out=pD[0:1, :],
                                     lhsT=z_sb[:, hc:hc + 1],
                                     rhs=sig_sb[pr][:], start=True,
                                     stop=True)
                    if pn_rd[pr]:
                        c.wait(nc.tensor, "s_dve", pn_rd[pr])
                    vmm = c.inc(nc.tensor.matmul(
                        out=pn[pr][:, :], lhsT=M_sb[:, hc * HD:(hc + 1) * HD],
                        rhs=sig_sb[pr][:], start=True, stop=True), "s_pe")
                    sig_rd[pr] = vmm
                    nc.vector.wait_ge(c.sems["s_pe"], vmm)
                    if den_wr:
                        nc.vector.wait_ge(c.sems["s_pe"], den_wr)
                    vds = c.inc(nc.vector.tensor_scalar(
                        out=den, in0=pD[0:1, :], scalar1=1e-6, scalar2=None,
                        op0=ALU.add), "s_dve")
                    nc.vector.wait_ge(c.sems["s_dve"], vds)
                    vr = c.inc(nc.vector.reciprocal(out=den, in_=den),
                               "s_dve")
                    den_rd = vr
                    c.wait(nc.tensor, "s_dve", vr)
                    if pb_rd[pr]:
                        c.wait(nc.tensor, "s_act", pb_rd[pr])
                    vb = c.inc(nc.tensor.matmul(out=pb_[pr][:, :],
                                                lhsT=ones1[:], rhs=den,
                                                start=True, stop=True),
                               "s_pe")
                    den_wr = vb
                    c.wait(nc.scalar, "s_pe", vb)
                    if rb_rd[pr]:
                        c.wait(nc.scalar, "s_dve", rb_rd[pr])
                    va = c.inc(nc.scalar.copy(out=rb_sb[pr][:],
                                              in_=pb_[pr][:, :]), "s_act")
                    pb_rd[pr] = va
                    nc.vector.wait_ge(c.sems["s_act"], va)
                    ins = nc.vector.scalar_tensor_tensor(
                        out=ret_r[:, hc, :], in0=pn[pr][:, :], scalar=0.0,
                        in1=rb_sb[pr][:], op0=ALU.bypass, op1=ALU.mult)
                    vret = c.inc(ins, "s_dve")
                    pn_rd[pr] = vret
                    rb_rd[pr] = vret
                c.wait(nc.tensor, "s_dve", vret)
                pyo = [pA, pB]
                pyo_rd = [pq_rd[0], pq_rd[1]]
                for oc in range(32):
                    pr = oc % 4
                    c.wait(nc.scalar, "s_pe", free_dwo[pr])
                    vo = c.inc(nc.scalar.dma_start(
                        out=wd_sb[pr][:, 0:8 * 128],
                        in_=wo_v[oc].rearrange("p s i -> p (s i)")),
                        "d_wd%d" % pr, 16)
                    c.wait(nc.tensor, "d_wd%d" % pr, vo)
                    bk = oc % 2
                    if pyo_rd[bk]:
                        c.wait(nc.tensor, "s_dve", pyo_rd[bk])
                    wor = wd_sb[pr][:, 0:8 * 128].rearrange(
                        "p (s i) -> p s i", i=128)
                    for cc in range(HKV):
                        ins = nc.tensor.matmul(out=pyo[bk][:, :],
                                               lhsT=wor[:, cc, :],
                                               rhs=ret_r[:, cc, :],
                                               start=(cc == 0),
                                               stop=(cc == 7))
                    free_dwo[pr] = c.inc(ins, "s_pe")
                    nc.vector.wait_ge(c.sems["s_pe"], free_dwo[pr])
                    ins = nc.vector.scalar_tensor_tensor(
                        out=y_r[:, oc, :], in0=pyo[bk][:, :], scalar=0.0,
                        in1=y_r[:, oc, :], op0=ALU.bypass, op1=ALU.add)
                    pyo_rd[bk] = c.inc(ins, "s_dve")
                    if oc % 8 == 7:
                        part = oc // 8
                        c.wait(nc.gpsimd, "s_dve", pyo_rd[bk])
                        y_store_v = c.inc(nc.gpsimd.dma_start(
                            out=yT_r[:, part * 8:(part + 1) * 8,
                                     tt * TT:(tt + 1) * TT],
                            in_=y_r[:, part * 8:(part + 1) * 8, :]),
                            "d_y", 16)

        c.wait(nc.gpsimd, "d_y", y_store_v)

    return nc


def make_in_maps(inputs, ncore=NCORE):
    x = np.asarray(inputs["L0_attn_hidden_states"])
    Wq = np.asarray(inputs["L0_attn_Wq"])
    bq = np.asarray(inputs["L0_attn_bq"])
    Wo = np.asarray(inputs["L0_attn_Wo"])
    M = np.asarray(inputs["L0_attn_M"])
    z = np.asarray(inputs["L0_attn_z"])
    T = TT * NT
    xt = np.ascontiguousarray(x.reshape(x.shape[0] * x.shape[1], D).T)
    common = dict(
        wqT=np.ascontiguousarray(
            Wq.reshape(32, HD, D)[::4].reshape(HKV * HD, D).T),
        bqT=np.ascontiguousarray(bq.reshape(32, HD)[::4].T),
        woT=np.ascontiguousarray(Wo.T),
        m=np.ascontiguousarray(M),
        zT=np.ascontiguousarray(z.T),
        g_qw=np.ascontiguousarray(inputs["L0_ffn_gate_qweight"]),
        g_sc=np.ascontiguousarray(inputs["L0_ffn_gate_scales"]),
        g_qz=np.ascontiguousarray(inputs["L0_ffn_gate_qzeros"]),
        u_qw=np.ascontiguousarray(inputs["L0_ffn_up_qweight"]),
        u_sc=np.ascontiguousarray(inputs["L0_ffn_up_scales"]),
        u_qz=np.ascontiguousarray(inputs["L0_ffn_up_qzeros"]),
        d_qw=np.ascontiguousarray(inputs["L0_ffn_down_qweight"]),
        d_sc=np.ascontiguousarray(inputs["L0_ffn_down_scales"]),
        d_qz=np.ascontiguousarray(inputs["L0_ffn_down_qzeros"]),
    )
    in_maps = []
    for cid in range(ncore):
        m_ = dict(common)
        m_["xT"] = np.ascontiguousarray(xt[:, cid * T:(cid + 1) * T])
        in_maps.append(m_)
    return in_maps


def kernel(**inputs):
    in_maps = make_in_maps(inputs)
    nc = build_program()
    res = run_bass_kernel_spmd(nc, in_maps, core_ids=list(range(NCORE)))
    T = TT * NT
    out = np.empty((16384, D), dtype=np.float32)
    for cid in range(NCORE):
        out[cid * T:(cid + 1) * T, :] = res.results[cid]["yT"].T
    return out.reshape(2, 8192, D)


# version 07
# speedup vs baseline: 1.0628x; 1.0451x over previous
import numpy as np
import concourse.bass as bass
import concourse.mybir as mybir
from concourse.bass_utils import run_bass_kernel_spmd
from contextlib import ExitStack

F32 = mybir.dt.float32
BF16 = mybir.dt.bfloat16
I32 = mybir.dt.int32
I8 = mybir.dt.int8
AF = mybir.ActivationFunctionType
ALU = mybir.AluOpType

D = 4096
HKV = 8
HD = 128
I_FFN = 11008
NCORE = 8

CHUNKS = [11, 11, 11, 11, 11, 11, 10, 10]
CHOFF = [0, 11, 22, 33, 44, 55, 66, 76]
MAXCH = 11

COMPUTE_FFN = True
COMPUTE_ATTN = True
TT = 512
NT = 4


class Ctl:
    def __init__(self, nc, stack):
        self.nc = nc
        self.stack = stack
        self.sems = {}
        self.cnt = {}

    def sem(self, name):
        self.sems[name] = self.stack.enter_context(self.nc.semaphore(name))
        self.cnt[name] = 0

    def inc(self, instr, name, k=1):
        instr.then_inc(self.sems[name], k)
        self.cnt[name] += k
        return self.cnt[name]

    def wait(self, eng, name, val=None):
        v = self.cnt[name] if val is None else val
        if v > 0:
            eng.wait_ge(self.sems[name], v)

    def val(self, name):
        return self.cnt[name]


def _dequant_matrix(nc, c, qw_d, sc_d, qz_d, sc_v, G, W, N, NSLOT, ph):
    MASK = 0x0F0F0F0F
    for eng in [nc.vector, nc.gpsimd]:
        c.wait(eng, "d_scr")
    c.wait(nc.gpsimd, "s_pe")
    c.wait(nc.gpsimd, "s_dve")
    c.wait(nc.sync, "s_dve")
    vp = 0
    st_rd = [0, 0]
    for si, st in enumerate(range(0, N, 1024)):
        en = min(N, st + 1024)
        spr = si % 2
        if st_rd[spr]:
            c.wait(nc.sync, "s_dve", st_rd[spr])
        v = c.inc(nc.sync.dma_start(out=ph["sc_st"][spr][0:G, 0:en - st],
                                    in_=sc_d[0:G, st:en]),
                  "d_sc%d" % spr, 16)
        c.wait(nc.vector, "d_sc%d" % spr, v)
        nc.vector.wait_ge(c.sems["s_dve"], c.val("s_dve"))
        vp = c.inc(nc.vector.tensor_copy(out=ph["sc_bf"][0:G, st:en],
                                         in_=ph["sc_st"][spr][0:G, 0:en - st]),
                   "s_dve")
        st_rd[spr] = vp
    vsc = vp
    v = c.inc(nc.sync.dma_start(out=ph["qz"][0:G, 0:W], in_=qz_d), "d_qz", 16)
    vms = c.inc(nc.gpsimd.memset(ph["negzs"][:, 0:N], 0.0), "s_pool")
    c.wait(nc.vector, "d_qz", v)
    nc.vector.wait_ge(c.sems["s_dve"], c.val("s_dve"))
    nc.vector.tensor_scalar(
        out=ph["qe"][0:G, 0:W], in0=ph["qz"][0:G, 0:W], scalar1=MASK,
        scalar2=None, op0=ALU.bitwise_and)
    vz = c.inc(nc.vector.tensor_scalar(
        out=ph["qo"][0:G, 0:W], in0=ph["qz"][0:G, 0:W], scalar1=4,
        scalar2=MASK, op0=ALU.logical_shift_right, op1=ALU.bitwise_and),
        "s_dve")
    nc.vector.wait_ge(c.sems["s_dve"], vz)
    c.wait(nc.vector, "s_pool", vms)
    qe8 = ph["qe"][0:G, 0:W].bitcast(I8)
    qo8 = ph["qo"][0:G, 0:W].bitcast(I8)
    nc.vector.scalar_tensor_tensor(
        out=ph["negzs"][0:G, 0:N:2], in0=qe8, scalar=-8.0,
        in1=ph["sc_bf"][0:G, 0:N:2], op0=ALU.add, op1=ALU.mult)
    vz2 = c.inc(nc.vector.scalar_tensor_tensor(
        out=ph["negzs"][0:G, 1:N:2], in0=qo8, scalar=-8.0,
        in1=ph["sc_bf"][0:G, 1:N:2], op0=ALU.add, op1=ALU.mult), "s_dve")
    c.wait(nc.scalar, "s_dve", vz2)
    c.inc(nc.scalar.dma_start(
        out=sc_v[:, :, NSLOT - 1, :].rearrange("t p i -> p t i"),
        in_=ph["negzs"][:, 0:N].rearrange("p (t i) -> p t i", i=128)),
        "d_scr", 16)

    dq = ph["deq"][:, 0:2 * N]
    dqv = dq.rearrange("p (t two i) -> p t two i", two=2, i=128)
    qw_r = qw_d.rearrange("(g p) w -> g p w", p=128)
    qw_rd = [0, 0]
    ps_rd = [0, 0]
    deq_wr = 0
    rs_built = 0
    for g in range(G):
        parity = g % 2
        qbuf = ph["qw"][parity]
        if qw_rd[parity]:
            c.wait(nc.sync, "s_dve", qw_rd[parity])
        v = c.inc(nc.sync.dma_start(out=qbuf[:, 0:W], in_=qw_r[g]),
                  "d_qw%d" % parity, 16)
        if rs_built:
            c.wait(nc.vector, "s_pe", rs_built)
        vrs = c.inc(nc.vector.tensor_scalar(
            out=ph["rs_t"][0:G, 0:128], in0=ph["iota"][0:G, :],
            scalar1=g, scalar2=None, op0=ALU.is_equal), "s_dve")
        c.wait(nc.vector, "d_qw%d" % parity, v)
        nc.vector.wait_ge(c.sems["s_dve"], c.val("s_dve"))
        nc.vector.tensor_scalar(
            out=ph["qe"][:, 0:W], in0=qbuf[:, 0:W], scalar1=MASK,
            scalar2=None, op0=ALU.bitwise_and)
        qw_rd[parity] = c.inc(nc.vector.tensor_scalar(
            out=ph["qo"][:, 0:W], in0=qbuf[:, 0:W], scalar1=4,
            scalar2=MASK, op0=ALU.logical_shift_right, op1=ALU.bitwise_and),
            "s_dve")
        vunp = qw_rd[parity]
        qe8 = ph["qe"][:, 0:W].bitcast(I8)
        qo8 = ph["qo"][:, 0:W].bitcast(I8)
        for ci, ch0 in enumerate(range(0, N, 2048)):
            cw = min(N, ch0 + 2048) - ch0
            bi = ci % 2
            pb = ph["p_s"][bi]
            if ps_rd[bi]:
                c.wait(nc.tensor, "s_dve", ps_rd[bi])
            c.wait(nc.tensor, "s_dve", max(vsc, vrs))
            for s0 in range(0, cw, 512):
                sw = min(cw, s0 + 512) - s0
                vmm = c.inc(nc.tensor.matmul(
                    out=pb[:, s0:s0 + sw], lhsT=ph["rs_t"][0:G, 0:128],
                    rhs=ph["sc_bf"][0:G, ch0 + s0:ch0 + s0 + sw],
                    start=True, stop=True), "s_pe")
            rs_built = vmm
            nc.vector.wait_ge(c.sems["s_pe"], vmm)
            nc.vector.wait_ge(c.sems["s_dve"], vunp)
            if deq_wr and ch0 == 0:
                c.wait(nc.vector, "d_scr", deq_wr)
            a0, a1 = ch0 // 128, (ch0 + cw) // 128
            ins = nc.vector.scalar_tensor_tensor(
                out=dqv[:, a0:a1, parity, 0:128:2],
                in0=qe8[:, ch0 // 2:(ch0 + cw) // 2].rearrange(
                    "p (a i) -> p a i", i=64),
                scalar=-8.0,
                in1=pb[:, 0:cw:2].rearrange("p (a i) -> p a i", i=64),
                op0=ALU.add, op1=ALU.mult)
            ins = nc.vector.scalar_tensor_tensor(
                out=dqv[:, a0:a1, parity, 1:128:2],
                in0=qo8[:, ch0 // 2:(ch0 + cw) // 2].rearrange(
                    "p (a i) -> p a i", i=64),
                scalar=-8.0,
                in1=pb[:, 1:cw:2].rearrange("p (a i) -> p a i", i=64),
                op0=ALU.add, op1=ALU.mult)
            ps_rd[bi] = c.inc(ins, "s_dve")
        if parity == 1:
            c.wait(nc.scalar, "s_dve")
            deq_wr = c.inc(nc.scalar.dma_start(
                out=sc_v[:, :, g - 1:g + 1, :].rearrange(
                    "t p s i -> p t (s i)"),
                in_=dqv.rearrange("p t two i -> p t (two i)")),
                "d_scr", 16)


def build_program():
    T = TT * NT
    nc = bass.Bass("TRN2", target_bir_lowering=False)

    xT = nc.dram_tensor("xT", [D, T], F32, kind="ExternalInput").ap()
    wqT = nc.dram_tensor("wqT", [D, HKV * HD], F32, kind="ExternalInput").ap()
    bqT = nc.dram_tensor("bqT", [HD, HKV], F32, kind="ExternalInput").ap()
    woT = nc.dram_tensor("woT", [D, D], F32, kind="ExternalInput").ap()
    m_in = nc.dram_tensor("m", [HKV, HD, HD], F32, kind="ExternalInput").ap()
    zT = nc.dram_tensor("zT", [HD, HKV], F32, kind="ExternalInput").ap()
    g_qw = nc.dram_tensor("g_qw", [D, I_FFN // 8], I32, kind="ExternalInput").ap()
    g_sc = nc.dram_tensor("g_sc", [32, I_FFN], F32, kind="ExternalInput").ap()
    g_qz = nc.dram_tensor("g_qz", [32, I_FFN // 8], I32, kind="ExternalInput").ap()
    u_qw = nc.dram_tensor("u_qw", [D, I_FFN // 8], I32, kind="ExternalInput").ap()
    u_sc = nc.dram_tensor("u_sc", [32, I_FFN], F32, kind="ExternalInput").ap()
    u_qz = nc.dram_tensor("u_qz", [32, I_FFN // 8], I32, kind="ExternalInput").ap()
    d_qw = nc.dram_tensor("d_qw", [I_FFN, D // 8], I32, kind="ExternalInput").ap()
    d_sc = nc.dram_tensor("d_sc", [86, D], F32, kind="ExternalInput").ap()
    d_qz = nc.dram_tensor("d_qz", [86, D // 8], I32, kind="ExternalInput").ap()
    yT = nc.dram_tensor("yT", [D, T], F32, kind="ExternalOutput").ap()

    sc_g = nc.dram_tensor("sc_g", [86 * 128 * 33 * 128], BF16)
    sc_u = nc.dram_tensor("sc_u", [86 * 128 * 33 * 128], BF16)
    sc_d = nc.dram_tensor("sc_d", [32 * 128 * 87 * 128], BF16)
    sc_wq = nc.dram_tensor("sc_wq", [8 * 128 * 32 * 128], BF16)
    sc_wo = nc.dram_tensor("sc_wo", [32 * 128 * 8 * 128], BF16)
    g_v = sc_g.ap().rearrange("(t p s i) -> t p s i", p=128, s=33, i=128)
    u_v = sc_u.ap().rearrange("(t p s i) -> t p s i", p=128, s=33, i=128)
    d_v = sc_d.ap().rearrange("(t p s i) -> t p s i", p=128, s=87, i=128)
    wq_v = sc_wq.ap().rearrange("(h p k i) -> h p k i", p=128, k=32, i=128)
    wo_v = sc_wo.ap().rearrange("(o p k i) -> o p k i", p=128, k=8, i=128)

    with ExitStack() as stack:
        e = stack.enter_context
        c = Ctl(nc, stack)
        DMA_SEMS = ["d_in0", "d_in1", "d_sc0", "d_sc1", "d_qz", "d_qw0", "d_qw1",
                    "d_scr", "d_scr2", "d_wg0", "d_wg1", "d_wu0", "d_wu1",
                    "d_wd0", "d_wd1", "d_wd2", "d_wd3", "d_x", "d_y"]
        for s in DMA_SEMS + ["s_pe", "s_dve", "s_act", "s_pool"]:
            c.sem(s)
        c.dma_sems = DMA_SEMS

        M_sb = e(nc.sbuf_tensor("M_sb", [128, HKV * HD], BF16))
        z_sb = e(nc.sbuf_tensor("z_sb", [128, HKV], BF16))
        bq_sb = e(nc.sbuf_tensor("bq_sb", [128, HKV], F32))
        onesK = e(nc.sbuf_tensor("onesK", [128, 1], BF16))
        ones1 = e(nc.sbuf_tensor("ones1", [1, 128], F32))
        D32 = e(nc.sbuf_tensor("D32", [128, 64], BF16))
        D86 = e(nc.sbuf_tensor("D86", [128, 172], BF16))
        sgp = e(nc.sbuf_tensor("sgp", [128, TT], BF16))
        sdp = e(nc.sbuf_tensor("sdp", [128, TT], BF16))
        cst = e(nc.sbuf_tensor("cst", [128, 2], F32))

        nc.vector.memset(onesK[:], 1.0 / 4096.0)
        nc.vector.memset(ones1[:], 1.0)
        nc.vector.memset(D32[:], 0.0)
        nc.vector.memset(D86[:], 0.0)
        nc.vector.memset(sgp[:], 0.0)
        nc.vector.memset(sdp[:], 0.0)
        vm0 = c.inc(nc.vector.memset(cst[:, 0:1], 1e-5), "s_dve")
        nc.vector.wait_ge(c.sems["s_dve"], vm0)
        nc.vector.memset(D32[:, 32:33], 1.0)
        c.inc(nc.vector.memset(D86[:, 86:87], 1.0), "s_dve")

        with ExitStack() as p0a:
            e0 = p0a.enter_context
            ph = {}
            ph["sc_st"] = [e0(nc.sbuf_tensor(f"p0_scst{i}", [86, 1024], F32))
                           for i in range(2)]
            ph["sc_bf"] = e0(nc.sbuf_tensor("p0_scb", [86, I_FFN], BF16))
            ph["qz"] = e0(nc.sbuf_tensor("p0_qz", [86, I_FFN // 8], I32))
            ph["negzs"] = e0(nc.sbuf_tensor("p0_negzs", [128, I_FFN], BF16))
            ph["qw"] = [e0(nc.sbuf_tensor(f"p0_qw{i}", [128, I_FFN // 8], I32))
                        for i in range(2)]
            ph["qe"] = e0(nc.sbuf_tensor("p0_qe", [128, I_FFN // 8], I32))
            ph["qo"] = e0(nc.sbuf_tensor("p0_qo", [128, I_FFN // 8], I32))
            ph["deq"] = e0(nc.sbuf_tensor("p0_deq", [128, 2 * I_FFN], BF16))
            ph["p_s"] = [e0(nc.psum_tensor(f"p0_ps{i}", [128, 2048], F32))
                         for i in range(2)]
            ph["rs_t"] = e0(nc.sbuf_tensor("p0_rst", [96, 128], BF16))
            ph["iota"] = e0(nc.sbuf_tensor("p0_io96", [96, 128], I32))

            vio = c.inc(nc.gpsimd.iota(ph["iota"][:], pattern=[[0, 128]],
                                       base=0, channel_multiplier=1),
                        "s_pool")
            c.wait(nc.vector, "s_pool", vio)

            _dequant_matrix(nc, c, g_qw, g_sc, g_qz, g_v, 32, I_FFN // 8,
                            I_FFN, 33, ph)

            wtmp = [e0(nc.sbuf_tensor(f"p0_wt{i}", [128, D], F32))
                    for i in range(2)]
            wcast = e0(nc.sbuf_tensor("p0_wc", [128, 2 * D], BF16))

            v = c.inc(nc.sync.dma_start(
                out=wtmp[0][:, 0:HKV * HD].rearrange(
                    "p (kv e) -> p kv e", e=HD),
                in_=m_in.rearrange("kv d e -> d kv e")), "d_in0", 16)
            c.wait(nc.gpsimd, "d_in0", v)
            nc.gpsimd.wait_ge(c.sems["s_pool"], c.val("s_pool"))
            vmcast = c.inc(nc.gpsimd.tensor_copy(
                out=M_sb[:], in_=wtmp[0][:, 0:HKV * HD]), "s_pool")
            v = c.inc(nc.sync.dma_start(out=wtmp[1][:, 0:HKV], in_=zT),
                      "d_in1", 16)
            c.wait(nc.gpsimd, "d_in1", v)
            vmz = c.inc(nc.gpsimd.tensor_copy(out=z_sb[:],
                                              in_=wtmp[1][:, 0:HKV]),
                        "s_pool")
            c.inc(nc.sync.dma_start(out=bq_sb[:], in_=bqT), "d_qz", 16)

            wcv = wcast[:, 0:HKV * 2 * HD].rearrange(
                "p (h two i) -> p h two i", two=2, i=128)
            wqT_r = wqT.rearrange("(k p) h -> k p h", p=128)
            wq_rd = [vmcast, vmz]
            wq_wr = 0
            for kc in range(32):
                parity = kc % 2
                c.wait(nc.sync, "s_pool", wq_rd[parity])
                v = c.inc(nc.sync.dma_start(out=wtmp[parity][:, 0:HKV * HD],
                                            in_=wqT_r[kc]),
                          "d_in%d" % parity, 16)
                c.wait(nc.gpsimd, "d_in%d" % parity, v)
                if wq_wr:
                    c.wait(nc.gpsimd, "d_scr2", wq_wr)
                nc.gpsimd.wait_ge(c.sems["s_pool"], c.val("s_pool"))
                ins = nc.gpsimd.tensor_copy(
                    out=wcv[:, :, parity, :],
                    in_=wtmp[parity][:, 0:HKV * HD].rearrange(
                        "p (h i) -> p h i", i=128))
                wq_rd[parity] = c.inc(ins, "s_pool")
                if parity == 1:
                    nc.gpsimd.wait_ge(c.sems["s_pool"], c.val("s_pool"))
                    wq_wr = c.inc(nc.gpsimd.dma_start(
                        out=wq_v[:, :, kc - 1:kc + 1, :].rearrange(
                            "h p s i -> p h (s i)"),
                        in_=wcv.rearrange("p h two i -> p h (two i)")),
                        "d_scr2", 16)

            wov = wcast[:].rearrange("p (o two i) -> p o two i", two=2, i=128)
            woT_r = woT.rearrange("(k p) o -> k p o", p=128)
            wo_wr = wq_wr
            wo_rd = c.val("s_pool")
            for kv in range(HKV):
                parity = kv % 2
                a, b = wtmp[0], wtmp[1]
                c.wait(nc.sync, "s_pool", wo_rd)
                v0 = c.inc(nc.sync.dma_start(out=a[:], in_=woT_r[4 * kv]),
                           "d_in0", 16)
                v1 = c.inc(nc.sync.dma_start(out=b[:], in_=woT_r[4 * kv + 1]),
                           "d_in1", 16)
                c.wait(nc.gpsimd, "d_in0", v0)
                c.wait(nc.gpsimd, "d_in1", v1)
                nc.gpsimd.wait_ge(c.sems["s_pool"], c.val("s_pool"))
                s1 = c.inc(nc.gpsimd.tensor_add(out=a[:], in0=a[:], in1=b[:]),
                           "s_pool")
                c.wait(nc.sync, "s_pool", s1)
                v2 = c.inc(nc.sync.dma_start(out=b[:], in_=woT_r[4 * kv + 2]),
                           "d_in1", 16)
                c.wait(nc.gpsimd, "d_in1", v2)
                nc.gpsimd.wait_ge(c.sems["s_pool"], s1)
                s2 = c.inc(nc.gpsimd.tensor_add(out=a[:], in0=a[:], in1=b[:]),
                           "s_pool")
                c.wait(nc.sync, "s_pool", s2)
                v3 = c.inc(nc.sync.dma_start(out=b[:], in_=woT_r[4 * kv + 3]),
                           "d_in1", 16)
                c.wait(nc.gpsimd, "d_in1", v3)
                nc.gpsimd.wait_ge(c.sems["s_pool"], s2)
                if wo_wr:
                    c.wait(nc.gpsimd, "d_scr2", wo_wr)
                s3 = c.inc(nc.gpsimd.tensor_add(
                    out=wov[:, :, parity, :],
                    in0=a[:].rearrange("p (o i) -> p o i", i=128),
                    in1=b[:].rearrange("p (o i) -> p o i", i=128)),
                    "s_pool")
                wo_rd = s3
                if parity == 1:
                    nc.gpsimd.wait_ge(c.sems["s_pool"], s3)
                    wo_wr = c.inc(nc.gpsimd.dma_start(
                        out=wo_v[:, :, kv - 1:kv + 1, :].rearrange(
                            "o p s i -> p o (s i)"),
                        in_=wov.rearrange("p o two i -> p o (two i)")),
                        "d_scr2", 16)


            _dequant_matrix(nc, c, u_qw, u_sc, u_qz, u_v, 32, I_FFN // 8,
                            I_FFN, 33, ph)
            _dequant_matrix(nc, c, d_qw, d_sc, d_qz, d_v, 86, D // 8,
                            D, 87, ph)

        y_sb = e(nc.sbuf_tensor("y_sb", [128, 32 * TT], F32))
        xn_sb = e(nc.sbuf_tensor("xn_sb", [128, 32 * TT], BF16))
        h_sb = [e(nc.sbuf_tensor(f"h_sb{i}", [128, MAXCH * TT], BF16))
                for i in range(2)]
        wg_sb = [e(nc.sbuf_tensor(f"wg_sb{i}", [128, 33 * 128], BF16))
                 for i in range(2)]
        wu_sb = [e(nc.sbuf_tensor(f"wu_sb{i}", [128, 33 * 128], BF16))
                 for i in range(2)]
        wd_sb = [e(nc.sbuf_tensor(f"wd_sb{i}", [128, (MAXCH + 1) * 128], BF16))
                 for i in range(4)]
        sig_sb = [e(nc.sbuf_tensor(f"sig_sb{i}", [128, TT], BF16))
                  for i in range(2)]
        ret_sb = e(nc.sbuf_tensor("ret_sb", [128, HKV * TT], BF16))
        sq_sb = [e(nc.sbuf_tensor(f"sq_sb{i}", [128, TT], BF16))
                 for i in range(4)]
        t1_sb = [e(nc.sbuf_tensor(f"t1_sb{i}", [128, TT], F32))
                 for i in range(2)]
        silu_sb = [e(nc.sbuf_tensor(f"silu_sb{i}", [128, TT], BF16))
                   for i in range(2)]
        rb_sb = [e(nc.sbuf_tensor(f"rb_sb{i}", [128, TT], F32))
                 for i in range(2)]
        tex_sb = e(nc.sbuf_tensor("tex_sb", [128, TT], F32))
        tmx_sb = e(nc.sbuf_tensor("tmx_sb", [128, TT], F32))
        row_sb = e(nc.sbuf_tensor("row_sb", [1, 5 * TT], F32))
        pA = e(nc.psum_tensor("pA", [128, TT], F32))
        pB = e(nc.psum_tensor("pB", [128, TT], F32))
        pC = e(nc.psum_tensor("pC", [128, TT], F32))
        pC2 = e(nc.psum_tensor("pC2", [128, TT], F32))
        pD = e(nc.psum_tensor("pD", [128, TT], F32))
        pF = e(nc.psum_tensor("pF", [128, TT], F32))
        pF2 = e(nc.psum_tensor("pF2", [128, TT], F32))
        pE = e(nc.psum_tensor("pE", [128, TT], F32))

        snap = {k: c.val(k) for k in c.cnt}
        for eng in [nc.sync, nc.scalar, nc.vector, nc.gpsimd, nc.tensor]:
            for s in ["d_in0", "d_in1", "d_sc0", "d_sc1", "d_qz", "d_qw0",
                      "d_qw1", "d_scr", "d_scr2", "s_dve", "s_pool", "s_pe",
                      "s_act"]:
                c.wait(eng, s, snap[s])

        xT_r = xT.rearrange("(k p) t -> p k t", p=128)
        yT_r = yT.rearrange("(k p) t -> p k t", p=128)
        y_r = y_sb[:].rearrange("p (k t) -> p k t", t=TT)
        xn_r = xn_sb[:].rearrange("p (k t) -> p k t", t=TT)

        free_gq = [0, 0]
        free_wu = [0, 0]
        free_dwo = [0, 0, 0, 0]
        free_h = [0, 0]
        y_store_v = 0

        for tt in range(NT):
            snap = {k: c.val(k) for k in ["s_pe", "s_dve", "s_act", "s_pool"]}
            for eng in [nc.tensor, nc.vector, nc.gpsimd, nc.scalar]:
                for s in ["s_pe", "s_dve", "s_act", "s_pool"]:
                    c.wait(eng, s, snap[s])

            c.wait(nc.sync, "d_y", y_store_v)
            xload_v = 0
            for part in range(4):
                ins = nc.sync.dma_start(
                    out=y_r[:, part * 8:(part + 1) * 8, :],
                    in_=xT_r[:, part * 8:(part + 1) * 8,
                             tt * TT:(tt + 1) * TT])
                xload_v = c.inc(ins, "d_x", 16)

            c.wait(nc.gpsimd, "d_x", xload_v)
            c.wait(nc.vector, "d_x", xload_v)
            for kc in range(6):
                ins = nc.gpsimd.tensor_copy(out=xn_r[:, kc, :],
                                            in_=y_r[:, kc, :])
            vxt_p = c.inc(ins, "s_pool")
            for kc in range(6, 32):
                ins = nc.vector.tensor_copy(out=xn_r[:, kc, :],
                                            in_=y_r[:, kc, :])
            vxt = c.inc(ins, "s_dve")
            c.wait(nc.tensor, "s_pool", vxt_p)
            c.wait(nc.tensor, "s_dve", vxt)
            for kc in range(32):
                nc.tensor.matmul(out=pA[0:1, :], lhsT=onesK[:],
                                 rhs=xn_r[:, kc, :], start=(kc == 0),
                                 stop=(kc == 31))
            c.wait(nc.scalar, "d_x", xload_v)
            sq_done = [0, 0, 0, 0]
            for kc in range(32):
                pr = kc % 4
                if sq_done[pr]:
                    c.wait(nc.scalar, "s_pe", sq_done[pr])
                va = c.inc(nc.scalar.activation(out=sq_sb[pr][:],
                                                in_=y_r[:, kc, :],
                                                func=AF.Square), "s_act")
                c.wait(nc.tensor, "s_act", va)
                ins = nc.tensor.matmul(out=pB[0:1, :], lhsT=onesK[:],
                                       rhs=sq_sb[pr][:], start=(kc == 0),
                                       stop=(kc == 31))
                sq_done[pr] = c.inc(ins, "s_pe")
            vstats = max(sq_done)
            mean = row_sb[0:1, 0:TT]
            ex2 = row_sb[0:1, TT:2 * TT]
            var = ex2
            m2 = row_sb[0:1, 2 * TT:3 * TT]
            rstd = row_sb[0:1, 3 * TT:4 * TT]
            mr = m2
            den = row_sb[0:1, 4 * TT:5 * TT]
            nc.vector.wait_ge(c.sems["s_pe"], vstats)
            nc.vector.tensor_copy(out=mean, in_=pA[0:1, :])
            vcp = c.inc(nc.vector.tensor_copy(out=ex2, in_=pB[0:1, :]),
                        "s_dve")
            nc.vector.wait_ge(c.sems["s_dve"], vcp)
            vmu = c.inc(nc.vector.tensor_mul(out=m2, in0=mean, in1=mean),
                        "s_dve")
            nc.vector.wait_ge(c.sems["s_dve"], vmu)
            vrow1 = c.inc(nc.vector.tensor_sub(out=var, in0=ex2, in1=m2),
                          "s_dve")
            c.wait(nc.scalar, "s_dve", vrow1)
            vln = c.inc(nc.scalar.activation(out=var, in_=var, func=AF.Ln,
                                             bias=cst[0:1, 0:1]), "s_act")
            nc.scalar.wait_ge(c.sems["s_act"], vln)
            vact2 = c.inc(nc.scalar.activation(out=rstd, in_=var, func=AF.Exp,
                                               scale=-0.5), "s_act")
            c.wait(nc.vector, "s_act", vact2)
            vrow2 = c.inc(nc.vector.tensor_mul(out=mr, in0=mean, in1=rstd),
                          "s_dve")
            c.wait(nc.tensor, "s_dve", vrow2)
            nc.tensor.matmul(out=pC[:, :], lhsT=ones1[:], rhs=rstd,
                             start=True, stop=True)
            vbc = c.inc(nc.tensor.matmul(out=pD[:, :], lhsT=ones1[:], rhs=mr,
                                         start=True, stop=True), "s_pe")
            c.wait(nc.scalar, "s_pe", vbc)
            nc.scalar.copy(out=rb_sb[0][:], in_=pC[:, :])
            vev = c.inc(nc.scalar.copy(out=rb_sb[1][:], in_=pD[:, :]),
                        "s_act")
            c.wait(nc.vector, "s_act", vev)
            c.wait(nc.gpsimd, "s_act", vev)
            for kc in range(24):
                pr = kc % 2
                vt = c.inc(nc.vector.tensor_mul(out=t1_sb[pr][:],
                                                in0=y_r[:, kc, :],
                                                in1=rb_sb[0][:]), "s_dve")
                nc.vector.wait_ge(c.sems["s_dve"], vt)
                ins = nc.vector.tensor_sub(out=xn_r[:, kc, :],
                                           in0=t1_sb[pr][:], in1=rb_sb[1][:])
            vxn = c.inc(ins, "s_dve")
            tp = [tex_sb, tmx_sb]
            for kc in range(24, 32):
                pr = kc % 2
                vt = c.inc(nc.gpsimd.tensor_mul(out=tp[pr][:],
                                                in0=y_r[:, kc, :],
                                                in1=rb_sb[0][:]), "s_pool")
                nc.gpsimd.wait_ge(c.sems["s_pool"], vt)
                ins = nc.gpsimd.tensor_sub(out=xn_r[:, kc, :],
                                           in0=tp[pr][:], in1=rb_sb[1][:])
            vxn_p = c.inc(ins, "s_pool")

            if COMPUTE_FFN:
                c.wait(nc.tensor, "s_dve", vxn)
                c.wait(nc.tensor, "s_pool", vxn_p)
                for kc in range(32):
                    ins = nc.tensor.matmul(
                        out=pE[0:32, :], lhsT=D32[:, 32 - kc:64 - kc],
                        rhs=xn_r[:, kc, :], start=(kc == 0), stop=(kc == 31))
                vsg = c.inc(ins, "s_pe")
                nc.vector.wait_ge(c.sems["s_pe"], vsg)
                vsgp = c.inc(nc.vector.tensor_scalar(
                    out=sgp[0:32, :], in0=pE[0:32, :], scalar1=-1.0,
                    scalar2=None, op0=ALU.mult), "s_dve")

                pg = [pA, pB]
                pu = [pC, pC2]
                py = [pD, pF]
                pg_rd = [0, 0]
                pu_rd = [0, 0]
                py_rd = [0, 0]
                silu_rd = [0, 0]
                vsdp = 0
                sgp_mm = 0
                for q in range(8):
                    nic = CHUNKS[q]
                    hq = h_sb[q % 2]
                    hq_r = hq[:].rearrange("p (s t) -> p s t", t=TT)
                    h_chunk_wr = 0
                    for j in range(nic):
                        ic = CHOFF[q] + j
                        pr = ic % 2
                        c.wait(nc.sync, "s_pe", free_gq[pr])
                        vg = c.inc(nc.sync.dma_start(
                            out=wg_sb[pr][:],
                            in_=g_v[ic].rearrange("p s i -> p (s i)")),
                            "d_wg%d" % pr, 16)
                        c.wait(nc.sync, "s_pe", free_wu[pr])
                        vu = c.inc(nc.sync.dma_start(
                            out=wu_sb[pr][:],
                            in_=u_v[ic].rearrange("p s i -> p (s i)")),
                            "d_wu%d" % pr, 16)
                        c.wait(nc.tensor, "d_wg%d" % pr, vg)
                        c.wait(nc.tensor, "s_dve", vsgp)
                        if pg_rd[pr]:
                            c.wait(nc.tensor, "s_act", pg_rd[pr])
                        wg_r = wg_sb[pr][:].rearrange("p (s i) -> p s i",
                                                      i=128)
                        for kc in range(32):
                            nc.tensor.matmul(out=pg[pr][:, :],
                                             lhsT=wg_r[:, kc, :],
                                             rhs=xn_r[:, kc, :],
                                             start=(kc == 0), stop=False)
                        ins = nc.tensor.matmul(out=pg[pr][:, :],
                                               lhsT=wg_r[:, 32, :],
                                               rhs=sgp[:, :],
                                               start=False, stop=True)
                        free_gq[pr] = sgp_mm = c.inc(ins, "s_pe")
                        c.wait(nc.tensor, "d_wu%d" % pr, vu)
                        if pu_rd[pr]:
                            c.wait(nc.tensor, "s_dve", pu_rd[pr])
                        wu_r = wu_sb[pr][:].rearrange("p (s i) -> p s i",
                                                      i=128)
                        for kc in range(32):
                            nc.tensor.matmul(out=pu[pr][:, :],
                                             lhsT=wu_r[:, kc, :],
                                             rhs=xn_r[:, kc, :],
                                             start=(kc == 0), stop=False)
                        ins = nc.tensor.matmul(out=pu[pr][:, :],
                                               lhsT=wu_r[:, 32, :],
                                               rhs=sgp[:, :],
                                               start=False, stop=True)
                        free_wu[pr] = c.inc(ins, "s_pe")
                        c.wait(nc.scalar, "s_pe", free_gq[pr])
                        if silu_rd[pr]:
                            c.wait(nc.scalar, "s_dve", silu_rd[pr])
                        va = c.inc(nc.scalar.activation(
                            out=silu_sb[pr][:], in_=pg[pr][:, :],
                            func=AF.Silu), "s_act")
                        pg_rd[pr] = va
                        nc.vector.wait_ge(c.sems["s_act"], va)
                        nc.vector.wait_ge(c.sems["s_pe"], free_wu[pr])
                        if free_h[q % 2]:
                            c.wait(nc.vector, "s_pe", free_h[q % 2])
                        ins = nc.vector.scalar_tensor_tensor(
                            out=hq_r[:, j, :], in0=pu[pr][:, :], scalar=0.0,
                            in1=silu_sb[pr][:], op0=ALU.bypass, op1=ALU.mult)
                        h_chunk_wr = c.inc(ins, "s_dve")
                        pu_rd[pr] = h_chunk_wr
                        silu_rd[pr] = h_chunk_wr
                    c.wait(nc.tensor, "s_dve", h_chunk_wr)
                    for j in range(nic):
                        ic = CHOFF[q] + j
                        ins = nc.tensor.matmul(
                            out=pE[0:86, :], lhsT=D86[:, 86 - ic:172 - ic],
                            rhs=hq_r[:, j, :], start=(ic == 0),
                            stop=(ic == 85))
                    if q == 7:
                        vsd = c.inc(ins, "s_pe")
                        nc.vector.wait_ge(c.sems["s_pe"], vsd)
                        vsdp = c.inc(nc.vector.tensor_scalar(
                            out=sdp[0:86, :], in0=pE[0:86, :], scalar1=-1.0,
                            scalar2=None, op0=ALU.mult), "s_dve")
                    s0 = CHOFF[q]
                    nsl = nic + (1 if q == 7 else 0)
                    for oc in range(32):
                        pr = oc % 4
                        c.wait(nc.scalar, "s_pe", free_dwo[pr])
                        vd = c.inc(nc.scalar.dma_start(
                            out=wd_sb[pr][:, 0:nsl * 128],
                            in_=d_v[oc, :, s0:s0 + nsl, :].rearrange(
                                "p s i -> p (s i)")), "d_wd%d" % pr, 16)
                        c.wait(nc.tensor, "d_wd%d" % pr, vd)
                        if q == 7:
                            c.wait(nc.tensor, "s_dve", vsdp)
                        bk = oc % 2
                        if py_rd[bk]:
                            c.wait(nc.tensor, "s_dve", py_rd[bk])
                        wd_r = wd_sb[pr][:, 0:nsl * 128].rearrange(
                            "p (s i) -> p s i", i=128)
                        for j in range(nic):
                            ins = nc.tensor.matmul(
                                out=py[bk][:, :], lhsT=wd_r[:, j, :],
                                rhs=hq_r[:, j, :], start=(j == 0),
                                stop=(j == nic - 1 and q != 7))
                        if q == 7:
                            ins = nc.tensor.matmul(out=py[bk][:, :],
                                                   lhsT=wd_r[:, nic, :],
                                                   rhs=sdp[:, :],
                                                   start=False, stop=True)
                        free_dwo[pr] = c.inc(ins, "s_pe")
                        nc.vector.wait_ge(c.sems["s_pe"], free_dwo[pr])
                        ins = nc.vector.scalar_tensor_tensor(
                            out=y_r[:, oc, :], in0=py[bk][:, :],
                            scalar=0.0, in1=y_r[:, oc, :], op0=ALU.bypass,
                            op1=ALU.add)
                        py_rd[bk] = c.inc(ins, "s_dve")
                    free_h[q % 2] = free_dwo[(31) % 2]

            if COMPUTE_ATTN:
                ret_r = ret_sb[:].rearrange("p (h t) -> p h t", t=TT)
                pq = [pA, pB]
                pn = [pC, pC2]
                pb_ = [pF, pF2]
                pq_rd = [0, 0]
                pn_rd = [0, 0]
                pb_rd = [0, 0]
                sig_rd = [0, 0]
                rb_rd = [0, 0]
                den_wr = 0
                den_rd = 0
                va_exp = 0
                vret = 0
                ffn_dve = c.val("s_dve")
                ffn_act = c.val("s_act")
                c.wait(nc.tensor, "s_dve", ffn_dve)
                c.wait(nc.tensor, "s_act", ffn_act)
                for hc in range(HKV):
                    pr = hc % 2
                    c.wait(nc.sync, "s_pe", free_gq[pr])
                    vq = c.inc(nc.sync.dma_start(
                        out=wg_sb[pr][:, 0:32 * 128],
                        in_=wq_v[hc].rearrange("p s i -> p (s i)")),
                        "d_wg%d" % pr, 16)
                    c.wait(nc.tensor, "d_wg%d" % pr, vq)
                    c.wait(nc.tensor, "s_pool", vxn_p)
                    if pq_rd[pr]:
                        c.wait(nc.tensor, "s_dve", pq_rd[pr])
                    wqr = wg_sb[pr][:, 0:32 * 128].rearrange(
                        "p (s i) -> p s i", i=128)
                    for kc in range(32):
                        ins = nc.tensor.matmul(out=pq[pr][:, :],
                                               lhsT=wqr[:, kc, :],
                                               rhs=xn_r[:, kc, :],
                                               start=(kc == 0),
                                               stop=(kc == 31))
                    free_gq[pr] = c.inc(ins, "s_pe")
                    nc.vector.wait_ge(c.sems["s_pe"], free_gq[pr])
                    if va_exp:
                        c.wait(nc.vector, "s_act", va_exp)
                    nc.vector.tensor_scalar(
                        out=tmx_sb[:], in0=pq[pr][:, :],
                        scalar1=bq_sb[:, hc:hc + 1], scalar2=0.0,
                        op0=ALU.add, op1=ALU.min)
                    v1 = c.inc(nc.vector.tensor_scalar(
                        out=tex_sb[:], in0=pq[pr][:, :],
                        scalar1=bq_sb[:, hc:hc + 1], scalar2=0.0,
                        op0=ALU.add, op1=ALU.max), "s_dve")
                    pq_rd[pr] = v1
                    c.wait(nc.scalar, "s_dve", v1)
                    va_exp = c.inc(nc.scalar.activation(out=tmx_sb[:],
                                                        in_=tmx_sb[:],
                                                        func=AF.Exp),
                                   "s_act")
                    c.wait(nc.vector, "s_act", va_exp)
                    nc.vector.wait_ge(c.sems["s_dve"], v1)
                    if sig_rd[pr]:
                        c.wait(nc.vector, "s_pe", sig_rd[pr])
                    vs = c.inc(nc.vector.tensor_add(
                        out=sig_sb[pr][:], in0=tmx_sb[:], in1=tex_sb[:]),
                        "s_dve")
                    c.wait(nc.tensor, "s_dve", vs)
                    if den_rd:
                        c.wait(nc.tensor, "s_dve", den_rd)
                    nc.tensor.matmul(out=pD[0:1, :],
                                     lhsT=z_sb[:, hc:hc + 1],
                                     rhs=sig_sb[pr][:], start=True,
                                     stop=True)
                    if pn_rd[pr]:
                        c.wait(nc.tensor, "s_dve", pn_rd[pr])
                    vmm = c.inc(nc.tensor.matmul(
                        out=pn[pr][:, :], lhsT=M_sb[:, hc * HD:(hc + 1) * HD],
                        rhs=sig_sb[pr][:], start=True, stop=True), "s_pe")
                    sig_rd[pr] = vmm
                    nc.vector.wait_ge(c.sems["s_pe"], vmm)
                    if den_wr:
                        nc.vector.wait_ge(c.sems["s_pe"], den_wr)
                    vds = c.inc(nc.vector.tensor_scalar(
                        out=den, in0=pD[0:1, :], scalar1=1e-6, scalar2=None,
                        op0=ALU.add), "s_dve")
                    nc.vector.wait_ge(c.sems["s_dve"], vds)
                    vr = c.inc(nc.vector.reciprocal(out=den, in_=den),
                               "s_dve")
                    den_rd = vr
                    c.wait(nc.tensor, "s_dve", vr)
                    if pb_rd[pr]:
                        c.wait(nc.tensor, "s_act", pb_rd[pr])
                    vb = c.inc(nc.tensor.matmul(out=pb_[pr][:, :],
                                                lhsT=ones1[:], rhs=den,
                                                start=True, stop=True),
                               "s_pe")
                    den_wr = vb
                    c.wait(nc.scalar, "s_pe", vb)
                    if rb_rd[pr]:
                        c.wait(nc.scalar, "s_dve", rb_rd[pr])
                    va = c.inc(nc.scalar.copy(out=rb_sb[pr][:],
                                              in_=pb_[pr][:, :]), "s_act")
                    pb_rd[pr] = va
                    nc.vector.wait_ge(c.sems["s_act"], va)
                    ins = nc.vector.scalar_tensor_tensor(
                        out=ret_r[:, hc, :], in0=pn[pr][:, :], scalar=0.0,
                        in1=rb_sb[pr][:], op0=ALU.bypass, op1=ALU.mult)
                    vret = c.inc(ins, "s_dve")
                    pn_rd[pr] = vret
                    rb_rd[pr] = vret
                c.wait(nc.tensor, "s_dve", vret)
                pyo = [pA, pB]
                pyo_rd = [pq_rd[0], pq_rd[1]]
                for oc in range(32):
                    pr = oc % 4
                    c.wait(nc.scalar, "s_pe", free_dwo[pr])
                    vo = c.inc(nc.scalar.dma_start(
                        out=wd_sb[pr][:, 0:8 * 128],
                        in_=wo_v[oc].rearrange("p s i -> p (s i)")),
                        "d_wd%d" % pr, 16)
                    c.wait(nc.tensor, "d_wd%d" % pr, vo)
                    bk = oc % 2
                    if pyo_rd[bk]:
                        c.wait(nc.tensor, "s_dve", pyo_rd[bk])
                    wor = wd_sb[pr][:, 0:8 * 128].rearrange(
                        "p (s i) -> p s i", i=128)
                    for cc in range(HKV):
                        ins = nc.tensor.matmul(out=pyo[bk][:, :],
                                               lhsT=wor[:, cc, :],
                                               rhs=ret_r[:, cc, :],
                                               start=(cc == 0),
                                               stop=(cc == 7))
                    free_dwo[pr] = c.inc(ins, "s_pe")
                    nc.vector.wait_ge(c.sems["s_pe"], free_dwo[pr])
                    ins = nc.vector.scalar_tensor_tensor(
                        out=y_r[:, oc, :], in0=pyo[bk][:, :], scalar=0.0,
                        in1=y_r[:, oc, :], op0=ALU.bypass, op1=ALU.add)
                    pyo_rd[bk] = c.inc(ins, "s_dve")
                    if oc % 8 == 7:
                        part = oc // 8
                        c.wait(nc.sync, "s_dve", pyo_rd[bk])
                        y_store_v = c.inc(nc.sync.dma_start(
                            out=yT_r[:, part * 8:(part + 1) * 8,
                                     tt * TT:(tt + 1) * TT],
                            in_=y_r[:, part * 8:(part + 1) * 8, :]),
                            "d_y", 16)

        c.wait(nc.sync, "d_y", y_store_v)

    return nc


def make_in_maps(inputs, ncore=NCORE):
    x = np.asarray(inputs["L0_attn_hidden_states"])
    Wq = np.asarray(inputs["L0_attn_Wq"])
    bq = np.asarray(inputs["L0_attn_bq"])
    Wo = np.asarray(inputs["L0_attn_Wo"])
    M = np.asarray(inputs["L0_attn_M"])
    z = np.asarray(inputs["L0_attn_z"])
    T = TT * NT
    xt = np.ascontiguousarray(x.reshape(x.shape[0] * x.shape[1], D).T)
    common = dict(
        wqT=np.ascontiguousarray(
            Wq.reshape(32, HD, D)[::4].reshape(HKV * HD, D).T),
        bqT=np.ascontiguousarray(bq.reshape(32, HD)[::4].T),
        woT=np.ascontiguousarray(Wo.T),
        m=np.ascontiguousarray(M),
        zT=np.ascontiguousarray(z.T),
        g_qw=np.ascontiguousarray(inputs["L0_ffn_gate_qweight"]),
        g_sc=np.ascontiguousarray(inputs["L0_ffn_gate_scales"]),
        g_qz=np.ascontiguousarray(inputs["L0_ffn_gate_qzeros"]),
        u_qw=np.ascontiguousarray(inputs["L0_ffn_up_qweight"]),
        u_sc=np.ascontiguousarray(inputs["L0_ffn_up_scales"]),
        u_qz=np.ascontiguousarray(inputs["L0_ffn_up_qzeros"]),
        d_qw=np.ascontiguousarray(inputs["L0_ffn_down_qweight"]),
        d_sc=np.ascontiguousarray(inputs["L0_ffn_down_scales"]),
        d_qz=np.ascontiguousarray(inputs["L0_ffn_down_qzeros"]),
    )
    in_maps = []
    for cid in range(ncore):
        m_ = dict(common)
        m_["xT"] = np.ascontiguousarray(xt[:, cid * T:(cid + 1) * T])
        in_maps.append(m_)
    return in_maps


def kernel(**inputs):
    in_maps = make_in_maps(inputs)
    nc = build_program()
    res = run_bass_kernel_spmd(nc, in_maps, core_ids=list(range(NCORE)))
    T = TT * NT
    out = np.empty((16384, D), dtype=np.float32)
    for cid in range(NCORE):
        out[cid * T:(cid + 1) * T, :] = res.results[cid]["yT"].T
    return out.reshape(2, 8192, D)


# version 08
# speedup vs baseline: 1.1048x; 1.0395x over previous
import numpy as np
import concourse.bass as bass
import concourse.mybir as mybir
from concourse.bass_utils import run_bass_kernel_spmd
from contextlib import ExitStack

F32 = mybir.dt.float32
BF16 = mybir.dt.bfloat16
I32 = mybir.dt.int32
I8 = mybir.dt.int8
AF = mybir.ActivationFunctionType
ALU = mybir.AluOpType

D = 4096
HKV = 8
HD = 128
I_FFN = 11008
NCORE = 8

CHUNKS = [11, 11, 11, 11, 11, 11, 10, 10]
CHOFF = [0, 11, 22, 33, 44, 55, 66, 76]
MAXCH = 11

COMPUTE_FFN = True
COMPUTE_ATTN = True
TT = 512
NT = 4


class Ctl:
    def __init__(self, nc, stack):
        self.nc = nc
        self.stack = stack
        self.sems = {}
        self.cnt = {}

    def sem(self, name):
        self.sems[name] = self.stack.enter_context(self.nc.semaphore(name))
        self.cnt[name] = 0

    def inc(self, instr, name, k=1):
        instr.then_inc(self.sems[name], k)
        self.cnt[name] += k
        return self.cnt[name]

    def wait(self, eng, name, val=None):
        v = self.cnt[name] if val is None else val
        if v > 0:
            eng.wait_ge(self.sems[name], v)

    def val(self, name):
        return self.cnt[name]


def _dequant_matrix(nc, c, qw_d, sc_d, qz_d, sc_v, G, W, N, NSLOT, ph):
    MASK = 0x0F0F0F0F
    for eng in [nc.vector, nc.gpsimd]:
        c.wait(eng, "d_scr")
    c.wait(nc.gpsimd, "s_pe")
    c.wait(nc.gpsimd, "s_dve")
    c.wait(nc.sync, "s_dve")
    vp = 0
    st_rd = [0, 0]
    for si, st in enumerate(range(0, N, 1024)):
        en = min(N, st + 1024)
        spr = si % 2
        if st_rd[spr]:
            c.wait(nc.sync, "s_dve", st_rd[spr])
        v = c.inc(nc.sync.dma_start(out=ph["sc_st"][spr][0:G, 0:en - st],
                                    in_=sc_d[0:G, st:en]),
                  "d_sc%d" % spr, 16)
        c.wait(nc.vector, "d_sc%d" % spr, v)
        nc.vector.wait_ge(c.sems["s_dve"], c.val("s_dve"))
        vp = c.inc(nc.vector.tensor_copy(out=ph["sc_bf"][0:G, st:en],
                                         in_=ph["sc_st"][spr][0:G, 0:en - st]),
                   "s_dve")
        st_rd[spr] = vp
    vsc = vp
    v = c.inc(nc.sync.dma_start(out=ph["qz"][0:G, 0:W], in_=qz_d), "d_qz", 16)
    vms = c.inc(nc.gpsimd.memset(ph["negzs"][:, 0:N], 0.0), "s_pool")
    c.wait(nc.vector, "d_qz", v)
    nc.vector.wait_ge(c.sems["s_dve"], c.val("s_dve"))
    nc.vector.tensor_scalar(
        out=ph["qe"][0:G, 0:W], in0=ph["qz"][0:G, 0:W], scalar1=MASK,
        scalar2=None, op0=ALU.bitwise_and)
    vz = c.inc(nc.vector.tensor_scalar(
        out=ph["qo"][0:G, 0:W], in0=ph["qz"][0:G, 0:W], scalar1=4,
        scalar2=MASK, op0=ALU.logical_shift_right, op1=ALU.bitwise_and),
        "s_dve")
    nc.vector.wait_ge(c.sems["s_dve"], vz)
    c.wait(nc.vector, "s_pool", vms)
    qe8 = ph["qe"][0:G, 0:W].bitcast(I8)
    qo8 = ph["qo"][0:G, 0:W].bitcast(I8)
    nc.vector.scalar_tensor_tensor(
        out=ph["negzs"][0:G, 0:N:2], in0=qe8, scalar=-8.0,
        in1=ph["sc_bf"][0:G, 0:N:2], op0=ALU.add, op1=ALU.mult)
    vz2 = c.inc(nc.vector.scalar_tensor_tensor(
        out=ph["negzs"][0:G, 1:N:2], in0=qo8, scalar=-8.0,
        in1=ph["sc_bf"][0:G, 1:N:2], op0=ALU.add, op1=ALU.mult), "s_dve")
    c.wait(nc.scalar, "s_dve", vz2)
    c.inc(nc.scalar.dma_start(
        out=sc_v[:, :, NSLOT - 1, :].rearrange("t p i -> p t i"),
        in_=ph["negzs"][:, 0:N].rearrange("p (t i) -> p t i", i=128)),
        "d_scr", 16)

    dq = ph["deq"][:, 0:2 * N]
    dqv = dq.rearrange("p (t two i) -> p t two i", two=2, i=128)
    qw_r = qw_d.rearrange("(g p) w -> g p w", p=128)
    qw_rd = [0, 0]
    ps_rd = [0, 0]
    deq_wr = 0
    rs_built = 0
    for g in range(G):
        parity = g % 2
        qbuf = ph["qw"][parity]
        if qw_rd[parity]:
            c.wait(nc.sync, "s_dve", qw_rd[parity])
        v = c.inc(nc.sync.dma_start(out=qbuf[:, 0:W], in_=qw_r[g]),
                  "d_qw%d" % parity, 16)
        if rs_built:
            c.wait(nc.vector, "s_pe", rs_built)
        vrs = c.inc(nc.vector.tensor_scalar(
            out=ph["rs_t"][0:G, 0:128], in0=ph["iota"][0:G, :],
            scalar1=g, scalar2=None, op0=ALU.is_equal), "s_dve")
        c.wait(nc.vector, "d_qw%d" % parity, v)
        nc.vector.wait_ge(c.sems["s_dve"], c.val("s_dve"))
        nc.vector.tensor_scalar(
            out=ph["qe"][:, 0:W], in0=qbuf[:, 0:W], scalar1=MASK,
            scalar2=None, op0=ALU.bitwise_and)
        qw_rd[parity] = c.inc(nc.vector.tensor_scalar(
            out=ph["qo"][:, 0:W], in0=qbuf[:, 0:W], scalar1=4,
            scalar2=MASK, op0=ALU.logical_shift_right, op1=ALU.bitwise_and),
            "s_dve")
        vunp = qw_rd[parity]
        qe8 = ph["qe"][:, 0:W].bitcast(I8)
        qo8 = ph["qo"][:, 0:W].bitcast(I8)
        for ci, ch0 in enumerate(range(0, N, 2048)):
            cw = min(N, ch0 + 2048) - ch0
            bi = ci % 2
            pb = ph["p_s"][bi]
            if ps_rd[bi]:
                c.wait(nc.tensor, "s_dve", ps_rd[bi])
            c.wait(nc.tensor, "s_dve", max(vsc, vrs))
            for s0 in range(0, cw, 512):
                sw = min(cw, s0 + 512) - s0
                vmm = c.inc(nc.tensor.matmul(
                    out=pb[:, s0:s0 + sw], lhsT=ph["rs_t"][0:G, 0:128],
                    rhs=ph["sc_bf"][0:G, ch0 + s0:ch0 + s0 + sw],
                    start=True, stop=True), "s_pe")
            rs_built = vmm
            nc.vector.wait_ge(c.sems["s_pe"], vmm)
            nc.vector.wait_ge(c.sems["s_dve"], vunp)
            if deq_wr and ch0 == 0:
                c.wait(nc.vector, "d_scr", deq_wr)
            a0, a1 = ch0 // 128, (ch0 + cw) // 128
            ins = nc.vector.scalar_tensor_tensor(
                out=dqv[:, a0:a1, parity, 0:128:2],
                in0=qe8[:, ch0 // 2:(ch0 + cw) // 2].rearrange(
                    "p (a i) -> p a i", i=64),
                scalar=-8.0,
                in1=pb[:, 0:cw:2].rearrange("p (a i) -> p a i", i=64),
                op0=ALU.add, op1=ALU.mult)
            ins = nc.vector.scalar_tensor_tensor(
                out=dqv[:, a0:a1, parity, 1:128:2],
                in0=qo8[:, ch0 // 2:(ch0 + cw) // 2].rearrange(
                    "p (a i) -> p a i", i=64),
                scalar=-8.0,
                in1=pb[:, 1:cw:2].rearrange("p (a i) -> p a i", i=64),
                op0=ALU.add, op1=ALU.mult)
            ps_rd[bi] = c.inc(ins, "s_dve")
        if parity == 1:
            c.wait(nc.scalar, "s_dve")
            deq_wr = c.inc(nc.scalar.dma_start(
                out=sc_v[:, :, g - 1:g + 1, :].rearrange(
                    "t p s i -> p t (s i)"),
                in_=dqv.rearrange("p t two i -> p t (two i)")),
                "d_scr", 16)


def build_program():
    T = TT * NT
    nc = bass.Bass("TRN2", target_bir_lowering=False)

    xT = nc.dram_tensor("xT", [D, T], F32, kind="ExternalInput").ap()
    wqT = nc.dram_tensor("wqT", [D, HKV * HD], F32, kind="ExternalInput").ap()
    bqT = nc.dram_tensor("bqT", [HD, HKV], F32, kind="ExternalInput").ap()
    woT = nc.dram_tensor("woT", [D, D], F32, kind="ExternalInput").ap()
    m_in = nc.dram_tensor("m", [HKV, HD, HD], F32, kind="ExternalInput").ap()
    zT = nc.dram_tensor("zT", [HD, HKV], F32, kind="ExternalInput").ap()
    g_qw = nc.dram_tensor("g_qw", [D, I_FFN // 8], I32, kind="ExternalInput").ap()
    g_sc = nc.dram_tensor("g_sc", [32, I_FFN], F32, kind="ExternalInput").ap()
    g_qz = nc.dram_tensor("g_qz", [32, I_FFN // 8], I32, kind="ExternalInput").ap()
    u_qw = nc.dram_tensor("u_qw", [D, I_FFN // 8], I32, kind="ExternalInput").ap()
    u_sc = nc.dram_tensor("u_sc", [32, I_FFN], F32, kind="ExternalInput").ap()
    u_qz = nc.dram_tensor("u_qz", [32, I_FFN // 8], I32, kind="ExternalInput").ap()
    d_qw = nc.dram_tensor("d_qw", [I_FFN, D // 8], I32, kind="ExternalInput").ap()
    d_sc = nc.dram_tensor("d_sc", [86, D], F32, kind="ExternalInput").ap()
    d_qz = nc.dram_tensor("d_qz", [86, D // 8], I32, kind="ExternalInput").ap()
    yT = nc.dram_tensor("yT", [D, T], F32, kind="ExternalOutput").ap()

    sc_g = nc.dram_tensor("sc_g", [86 * 128 * 33 * 128], BF16)
    sc_u = nc.dram_tensor("sc_u", [86 * 128 * 33 * 128], BF16)
    sc_d = nc.dram_tensor("sc_d", [32 * 128 * 87 * 128], BF16)
    sc_wq = nc.dram_tensor("sc_wq", [8 * 128 * 32 * 128], BF16)
    sc_wo = nc.dram_tensor("sc_wo", [32 * 128 * 8 * 128], BF16)
    g_v = sc_g.ap().rearrange("(t p s i) -> t p s i", p=128, s=33, i=128)
    u_v = sc_u.ap().rearrange("(t p s i) -> t p s i", p=128, s=33, i=128)
    d_v = sc_d.ap().rearrange("(t p s i) -> t p s i", p=128, s=87, i=128)
    wq_v = sc_wq.ap().rearrange("(h p k i) -> h p k i", p=128, k=32, i=128)
    wo_v = sc_wo.ap().rearrange("(o p k i) -> o p k i", p=128, k=8, i=128)

    with ExitStack() as stack:
        e = stack.enter_context
        c = Ctl(nc, stack)
        DMA_SEMS = ["d_in0", "d_in1", "d_bq", "d_sc0", "d_sc1", "d_qz", "d_qw0", "d_qw1",
                    "d_scr", "d_scr2", "d_wg0", "d_wg1", "d_wu0", "d_wu1",
                    "d_wd0", "d_wd1", "d_wd2", "d_wd3", "d_x", "d_y"]
        for s in DMA_SEMS + ["s_pe", "s_dve", "s_act", "s_pool"]:
            c.sem(s)
        c.dma_sems = DMA_SEMS

        M_sb = e(nc.sbuf_tensor("M_sb", [128, HKV * HD], BF16))
        z_sb = e(nc.sbuf_tensor("z_sb", [128, HKV], BF16))
        bq_sb = e(nc.sbuf_tensor("bq_sb", [128, HKV], F32))
        onesK = e(nc.sbuf_tensor("onesK", [128, 1], BF16))
        ones1 = e(nc.sbuf_tensor("ones1", [1, 128], F32))
        D32 = e(nc.sbuf_tensor("D32", [128, 64], BF16))
        D86 = e(nc.sbuf_tensor("D86", [128, 172], BF16))
        sgp = e(nc.sbuf_tensor("sgp", [128, TT], BF16))
        sdp = e(nc.sbuf_tensor("sdp", [128, TT], BF16))
        cst = e(nc.sbuf_tensor("cst", [128, 2], F32))

        nc.vector.memset(onesK[:], 1.0 / 4096.0)
        nc.vector.memset(ones1[:], 1.0)
        nc.vector.memset(D32[:], 0.0)
        nc.vector.memset(D86[:], 0.0)
        nc.vector.memset(sgp[:], 0.0)
        nc.vector.memset(sdp[:], 0.0)
        vm0 = c.inc(nc.vector.memset(cst[:, 0:1], 1e-5), "s_dve")
        nc.vector.wait_ge(c.sems["s_dve"], vm0)
        nc.vector.memset(D32[:, 32:33], 1.0)
        c.inc(nc.vector.memset(D86[:, 86:87], 1.0), "s_dve")

        with ExitStack() as p0a:
            e0 = p0a.enter_context
            ph = {}
            ph["sc_st"] = [e0(nc.sbuf_tensor(f"p0_scst{i}", [86, 1024], F32))
                           for i in range(2)]
            ph["sc_bf"] = e0(nc.sbuf_tensor("p0_scb", [86, I_FFN], BF16))
            ph["qz"] = e0(nc.sbuf_tensor("p0_qz", [86, I_FFN // 8], I32))
            ph["negzs"] = e0(nc.sbuf_tensor("p0_negzs", [128, I_FFN], BF16))
            ph["qw"] = [e0(nc.sbuf_tensor(f"p0_qw{i}", [128, I_FFN // 8], I32))
                        for i in range(2)]
            ph["qe"] = e0(nc.sbuf_tensor("p0_qe", [128, I_FFN // 8], I32))
            ph["qo"] = e0(nc.sbuf_tensor("p0_qo", [128, I_FFN // 8], I32))
            ph["deq"] = e0(nc.sbuf_tensor("p0_deq", [128, 2 * I_FFN], BF16))
            ph["p_s"] = [e0(nc.psum_tensor(f"p0_ps{i}", [128, 2048], F32))
                         for i in range(2)]
            ph["rs_t"] = e0(nc.sbuf_tensor("p0_rst", [96, 128], BF16))
            ph["iota"] = e0(nc.sbuf_tensor("p0_io96", [96, 128], I32))

            vio = c.inc(nc.gpsimd.iota(ph["iota"][:], pattern=[[0, 128]],
                                       base=0, channel_multiplier=1),
                        "s_pool")
            c.wait(nc.vector, "s_pool", vio)

            _dequant_matrix(nc, c, g_qw, g_sc, g_qz, g_v, 32, I_FFN // 8,
                            I_FFN, 33, ph)

            wtmp = [e0(nc.sbuf_tensor(f"p0_wt{i}", [128, D], F32))
                    for i in range(2)]
            wcast = e0(nc.sbuf_tensor("p0_wc", [128, 2 * D], BF16))

            v = c.inc(nc.gpsimd.dma_start(
                out=wtmp[0][:, 0:HKV * HD].rearrange(
                    "p (kv e) -> p kv e", e=HD),
                in_=m_in.rearrange("kv d e -> d kv e")), "d_in0", 16)
            c.wait(nc.gpsimd, "d_in0", v)
            nc.gpsimd.wait_ge(c.sems["s_pool"], c.val("s_pool"))
            vmcast = c.inc(nc.gpsimd.tensor_copy(
                out=M_sb[:], in_=wtmp[0][:, 0:HKV * HD]), "s_pool")
            v = c.inc(nc.gpsimd.dma_start(out=wtmp[1][:, 0:HKV], in_=zT),
                      "d_in1", 16)
            c.wait(nc.gpsimd, "d_in1", v)
            vmz = c.inc(nc.gpsimd.tensor_copy(out=z_sb[:],
                                              in_=wtmp[1][:, 0:HKV]),
                        "s_pool")
            c.inc(nc.gpsimd.dma_start(out=bq_sb[:], in_=bqT), "d_bq", 16)

            wcv = wcast[:, 0:HKV * 2 * HD].rearrange(
                "p (h two i) -> p h two i", two=2, i=128)
            wqT_r = wqT.rearrange("(k p) h -> k p h", p=128)
            wq_rd = [vmcast, vmz]
            wq_wr = 0
            for kc in range(32):
                parity = kc % 2
                nc.gpsimd.wait_ge(c.sems["s_pool"], wq_rd[parity])
                v = c.inc(nc.gpsimd.dma_start(out=wtmp[parity][:, 0:HKV * HD],
                                            in_=wqT_r[kc]),
                          "d_in%d" % parity, 16)
                c.wait(nc.gpsimd, "d_in%d" % parity, v)
                if wq_wr:
                    c.wait(nc.gpsimd, "d_scr2", wq_wr)
                nc.gpsimd.wait_ge(c.sems["s_pool"], c.val("s_pool"))
                ins = nc.gpsimd.tensor_copy(
                    out=wcv[:, :, parity, :],
                    in_=wtmp[parity][:, 0:HKV * HD].rearrange(
                        "p (h i) -> p h i", i=128))
                wq_rd[parity] = c.inc(ins, "s_pool")
                if parity == 1:
                    nc.gpsimd.wait_ge(c.sems["s_pool"], c.val("s_pool"))
                    wq_wr = c.inc(nc.gpsimd.dma_start(
                        out=wq_v[:, :, kc - 1:kc + 1, :].rearrange(
                            "h p s i -> p h (s i)"),
                        in_=wcv.rearrange("p h two i -> p h (two i)")),
                        "d_scr2", 16)

            wov = wcast[:].rearrange("p (o two i) -> p o two i", two=2, i=128)
            woT_r = woT.rearrange("(k p) o -> k p o", p=128)
            wo_wr = wq_wr
            wo_rd = c.val("s_pool")
            for kv in range(HKV):
                parity = kv % 2
                a, b = wtmp[0], wtmp[1]
                nc.gpsimd.wait_ge(c.sems["s_pool"], wo_rd)
                v0 = c.inc(nc.gpsimd.dma_start(out=a[:], in_=woT_r[4 * kv]),
                           "d_in0", 16)
                v1 = c.inc(nc.gpsimd.dma_start(out=b[:], in_=woT_r[4 * kv + 1]),
                           "d_in1", 16)
                c.wait(nc.gpsimd, "d_in0", v0)
                c.wait(nc.gpsimd, "d_in1", v1)
                nc.gpsimd.wait_ge(c.sems["s_pool"], c.val("s_pool"))
                s1 = c.inc(nc.gpsimd.tensor_add(out=a[:], in0=a[:], in1=b[:]),
                           "s_pool")
                nc.gpsimd.wait_ge(c.sems["s_pool"], s1)
                v2 = c.inc(nc.gpsimd.dma_start(out=b[:], in_=woT_r[4 * kv + 2]),
                           "d_in1", 16)
                c.wait(nc.gpsimd, "d_in1", v2)
                nc.gpsimd.wait_ge(c.sems["s_pool"], s1)
                s2 = c.inc(nc.gpsimd.tensor_add(out=a[:], in0=a[:], in1=b[:]),
                           "s_pool")
                nc.gpsimd.wait_ge(c.sems["s_pool"], s2)
                v3 = c.inc(nc.gpsimd.dma_start(out=b[:], in_=woT_r[4 * kv + 3]),
                           "d_in1", 16)
                c.wait(nc.gpsimd, "d_in1", v3)
                nc.gpsimd.wait_ge(c.sems["s_pool"], s2)
                if wo_wr:
                    c.wait(nc.gpsimd, "d_scr2", wo_wr)
                s3 = c.inc(nc.gpsimd.tensor_add(
                    out=wov[:, :, parity, :],
                    in0=a[:].rearrange("p (o i) -> p o i", i=128),
                    in1=b[:].rearrange("p (o i) -> p o i", i=128)),
                    "s_pool")
                wo_rd = s3
                if parity == 1:
                    nc.gpsimd.wait_ge(c.sems["s_pool"], s3)
                    wo_wr = c.inc(nc.gpsimd.dma_start(
                        out=wo_v[:, :, kv - 1:kv + 1, :].rearrange(
                            "o p s i -> p o (s i)"),
                        in_=wov.rearrange("p o two i -> p o (two i)")),
                        "d_scr2", 16)


            _dequant_matrix(nc, c, u_qw, u_sc, u_qz, u_v, 32, I_FFN // 8,
                            I_FFN, 33, ph)
            _dequant_matrix(nc, c, d_qw, d_sc, d_qz, d_v, 86, D // 8,
                            D, 87, ph)

        y_sb = e(nc.sbuf_tensor("y_sb", [128, 32 * TT], F32))
        xn_sb = e(nc.sbuf_tensor("xn_sb", [128, 32 * TT], BF16))
        h_sb = [e(nc.sbuf_tensor(f"h_sb{i}", [128, MAXCH * TT], BF16))
                for i in range(2)]
        wg_sb = [e(nc.sbuf_tensor(f"wg_sb{i}", [128, 33 * 128], BF16))
                 for i in range(2)]
        wu_sb = [e(nc.sbuf_tensor(f"wu_sb{i}", [128, 33 * 128], BF16))
                 for i in range(2)]
        wd_sb = [e(nc.sbuf_tensor(f"wd_sb{i}", [128, (MAXCH + 1) * 128], BF16))
                 for i in range(4)]
        sig_sb = [e(nc.sbuf_tensor(f"sig_sb{i}", [128, TT], BF16))
                  for i in range(2)]
        ret_sb = e(nc.sbuf_tensor("ret_sb", [128, HKV * TT], BF16))
        sq_sb = [e(nc.sbuf_tensor(f"sq_sb{i}", [128, TT], BF16))
                 for i in range(4)]
        t1_sb = [e(nc.sbuf_tensor(f"t1_sb{i}", [128, TT], F32))
                 for i in range(2)]
        silu_sb = [e(nc.sbuf_tensor(f"silu_sb{i}", [128, TT], BF16))
                   for i in range(2)]
        rb_sb = [e(nc.sbuf_tensor(f"rb_sb{i}", [128, TT], F32))
                 for i in range(2)]
        tex_sb = e(nc.sbuf_tensor("tex_sb", [128, TT], F32))
        tmx_sb = e(nc.sbuf_tensor("tmx_sb", [128, TT], F32))
        row_sb = e(nc.sbuf_tensor("row_sb", [1, 5 * TT], F32))
        pA = e(nc.psum_tensor("pA", [128, TT], F32))
        pB = e(nc.psum_tensor("pB", [128, TT], F32))
        pC = e(nc.psum_tensor("pC", [128, TT], F32))
        pC2 = e(nc.psum_tensor("pC2", [128, TT], F32))
        pD = e(nc.psum_tensor("pD", [128, TT], F32))
        pF = e(nc.psum_tensor("pF", [128, TT], F32))
        pF2 = e(nc.psum_tensor("pF2", [128, TT], F32))
        pE = e(nc.psum_tensor("pE", [128, TT], F32))

        snap = {k: c.val(k) for k in c.cnt}
        for eng in [nc.sync, nc.scalar, nc.vector, nc.gpsimd, nc.tensor]:
            for s in ["d_in0", "d_in1", "d_bq", "d_sc0", "d_sc1", "d_qz",
                      "d_qw0", "d_qw1", "d_scr", "d_scr2", "s_dve", "s_pool",
                      "s_pe", "s_act"]:
                c.wait(eng, s, snap[s])

        xT_r = xT.rearrange("(k p) t -> p k t", p=128)
        yT_r = yT.rearrange("(k p) t -> p k t", p=128)
        y_r = y_sb[:].rearrange("p (k t) -> p k t", t=TT)
        xn_r = xn_sb[:].rearrange("p (k t) -> p k t", t=TT)

        free_gq = [0, 0]
        free_wu = [0, 0]
        free_dwo = [0, 0, 0, 0]
        free_h = [0, 0]
        y_store_v = 0

        for tt in range(NT):
            snap = {k: c.val(k) for k in ["s_pe", "s_dve", "s_act", "s_pool"]}
            for eng in [nc.tensor, nc.vector, nc.gpsimd, nc.scalar]:
                for s in ["s_pe", "s_dve", "s_act", "s_pool"]:
                    c.wait(eng, s, snap[s])

            c.wait(nc.sync, "d_y", y_store_v)
            xload_v = 0
            for part in range(4):
                ins = nc.sync.dma_start(
                    out=y_r[:, part * 8:(part + 1) * 8, :],
                    in_=xT_r[:, part * 8:(part + 1) * 8,
                             tt * TT:(tt + 1) * TT])
                xload_v = c.inc(ins, "d_x", 16)

            c.wait(nc.gpsimd, "d_x", xload_v)
            c.wait(nc.vector, "d_x", xload_v)
            for kc in range(6):
                ins = nc.gpsimd.tensor_copy(out=xn_r[:, kc, :],
                                            in_=y_r[:, kc, :])
            vxt_p = c.inc(ins, "s_pool")
            for kc in range(6, 32):
                ins = nc.vector.tensor_copy(out=xn_r[:, kc, :],
                                            in_=y_r[:, kc, :])
            vxt = c.inc(ins, "s_dve")
            c.wait(nc.tensor, "s_pool", vxt_p)
            c.wait(nc.tensor, "s_dve", vxt)
            for kc in range(32):
                nc.tensor.matmul(out=pA[0:1, :], lhsT=onesK[:],
                                 rhs=xn_r[:, kc, :], start=(kc == 0),
                                 stop=(kc == 31))
            c.wait(nc.scalar, "d_x", xload_v)
            sq_done = [0, 0, 0, 0]
            for kc in range(32):
                pr = kc % 4
                if sq_done[pr]:
                    c.wait(nc.scalar, "s_pe", sq_done[pr])
                va = c.inc(nc.scalar.activation(out=sq_sb[pr][:],
                                                in_=y_r[:, kc, :],
                                                func=AF.Square), "s_act")
                c.wait(nc.tensor, "s_act", va)
                ins = nc.tensor.matmul(out=pB[0:1, :], lhsT=onesK[:],
                                       rhs=sq_sb[pr][:], start=(kc == 0),
                                       stop=(kc == 31))
                sq_done[pr] = c.inc(ins, "s_pe")
            vstats = max(sq_done)
            mean = row_sb[0:1, 0:TT]
            ex2 = row_sb[0:1, TT:2 * TT]
            var = ex2
            m2 = row_sb[0:1, 2 * TT:3 * TT]
            rstd = row_sb[0:1, 3 * TT:4 * TT]
            mr = m2
            den = row_sb[0:1, 4 * TT:5 * TT]
            nc.vector.wait_ge(c.sems["s_pe"], vstats)
            nc.vector.tensor_copy(out=mean, in_=pA[0:1, :])
            vcp = c.inc(nc.vector.tensor_copy(out=ex2, in_=pB[0:1, :]),
                        "s_dve")
            nc.vector.wait_ge(c.sems["s_dve"], vcp)
            vmu = c.inc(nc.vector.tensor_mul(out=m2, in0=mean, in1=mean),
                        "s_dve")
            nc.vector.wait_ge(c.sems["s_dve"], vmu)
            vrow1 = c.inc(nc.vector.tensor_sub(out=var, in0=ex2, in1=m2),
                          "s_dve")
            c.wait(nc.scalar, "s_dve", vrow1)
            vln = c.inc(nc.scalar.activation(out=var, in_=var, func=AF.Ln,
                                             bias=cst[0:1, 0:1]), "s_act")
            nc.scalar.wait_ge(c.sems["s_act"], vln)
            vact2 = c.inc(nc.scalar.activation(out=rstd, in_=var, func=AF.Exp,
                                               scale=-0.5), "s_act")
            c.wait(nc.vector, "s_act", vact2)
            vrow2 = c.inc(nc.vector.tensor_mul(out=mr, in0=mean, in1=rstd),
                          "s_dve")
            c.wait(nc.tensor, "s_dve", vrow2)
            nc.tensor.matmul(out=pC[:, :], lhsT=ones1[:], rhs=rstd,
                             start=True, stop=True)
            vbc = c.inc(nc.tensor.matmul(out=pD[:, :], lhsT=ones1[:], rhs=mr,
                                         start=True, stop=True), "s_pe")
            c.wait(nc.scalar, "s_pe", vbc)
            nc.scalar.copy(out=rb_sb[0][:], in_=pC[:, :])
            vev = c.inc(nc.scalar.copy(out=rb_sb[1][:], in_=pD[:, :]),
                        "s_act")
            c.wait(nc.vector, "s_act", vev)
            c.wait(nc.gpsimd, "s_act", vev)
            for kc in range(24):
                pr = kc % 2
                vt = c.inc(nc.vector.tensor_mul(out=t1_sb[pr][:],
                                                in0=y_r[:, kc, :],
                                                in1=rb_sb[0][:]), "s_dve")
                nc.vector.wait_ge(c.sems["s_dve"], vt)
                ins = nc.vector.tensor_sub(out=xn_r[:, kc, :],
                                           in0=t1_sb[pr][:], in1=rb_sb[1][:])
            vxn = c.inc(ins, "s_dve")
            tp = [tex_sb, tmx_sb]
            for kc in range(24, 32):
                pr = kc % 2
                vt = c.inc(nc.gpsimd.tensor_mul(out=tp[pr][:],
                                                in0=y_r[:, kc, :],
                                                in1=rb_sb[0][:]), "s_pool")
                nc.gpsimd.wait_ge(c.sems["s_pool"], vt)
                ins = nc.gpsimd.tensor_sub(out=xn_r[:, kc, :],
                                           in0=tp[pr][:], in1=rb_sb[1][:])
            vxn_p = c.inc(ins, "s_pool")

            if COMPUTE_FFN:
                c.wait(nc.tensor, "s_dve", vxn)
                c.wait(nc.tensor, "s_pool", vxn_p)
                for kc in range(32):
                    ins = nc.tensor.matmul(
                        out=pE[0:32, :], lhsT=D32[:, 32 - kc:64 - kc],
                        rhs=xn_r[:, kc, :], start=(kc == 0), stop=(kc == 31))
                vsg = c.inc(ins, "s_pe")
                nc.vector.wait_ge(c.sems["s_pe"], vsg)
                vsgp = c.inc(nc.vector.tensor_scalar(
                    out=sgp[0:32, :], in0=pE[0:32, :], scalar1=-1.0,
                    scalar2=None, op0=ALU.mult), "s_dve")

                pg = [pA, pB]
                pu = [pC, pC2]
                py = [pD, pF]
                pg_rd = [0, 0]
                pu_rd = [0, 0]
                py_rd = [0, 0]
                silu_rd = [0, 0]
                vsdp = 0
                sgp_mm = 0
                for q in range(8):
                    nic = CHUNKS[q]
                    hq = h_sb[q % 2]
                    hq_r = hq[:].rearrange("p (s t) -> p s t", t=TT)
                    h_chunk_wr = 0
                    for j in range(nic):
                        ic = CHOFF[q] + j
                        pr = ic % 2
                        c.wait(nc.sync, "s_pe", free_gq[pr])
                        vg = c.inc(nc.sync.dma_start(
                            out=wg_sb[pr][:],
                            in_=g_v[ic].rearrange("p s i -> p (s i)")),
                            "d_wg%d" % pr, 16)
                        c.wait(nc.sync, "s_pe", free_wu[pr])
                        vu = c.inc(nc.sync.dma_start(
                            out=wu_sb[pr][:],
                            in_=u_v[ic].rearrange("p s i -> p (s i)")),
                            "d_wu%d" % pr, 16)
                        c.wait(nc.tensor, "d_wg%d" % pr, vg)
                        c.wait(nc.tensor, "s_dve", vsgp)
                        if pg_rd[pr]:
                            c.wait(nc.tensor, "s_act", pg_rd[pr])
                        wg_r = wg_sb[pr][:].rearrange("p (s i) -> p s i",
                                                      i=128)
                        for kc in range(32):
                            nc.tensor.matmul(out=pg[pr][:, :],
                                             lhsT=wg_r[:, kc, :],
                                             rhs=xn_r[:, kc, :],
                                             start=(kc == 0), stop=False)
                        ins = nc.tensor.matmul(out=pg[pr][:, :],
                                               lhsT=wg_r[:, 32, :],
                                               rhs=sgp[:, :],
                                               start=False, stop=True)
                        free_gq[pr] = sgp_mm = c.inc(ins, "s_pe")
                        c.wait(nc.tensor, "d_wu%d" % pr, vu)
                        if pu_rd[pr]:
                            c.wait(nc.tensor, "s_dve", pu_rd[pr])
                        wu_r = wu_sb[pr][:].rearrange("p (s i) -> p s i",
                                                      i=128)
                        for kc in range(32):
                            nc.tensor.matmul(out=pu[pr][:, :],
                                             lhsT=wu_r[:, kc, :],
                                             rhs=xn_r[:, kc, :],
                                             start=(kc == 0), stop=False)
                        ins = nc.tensor.matmul(out=pu[pr][:, :],
                                               lhsT=wu_r[:, 32, :],
                                               rhs=sgp[:, :],
                                               start=False, stop=True)
                        free_wu[pr] = c.inc(ins, "s_pe")
                        c.wait(nc.scalar, "s_pe", free_gq[pr])
                        if silu_rd[pr]:
                            c.wait(nc.scalar, "s_dve", silu_rd[pr])
                        va = c.inc(nc.scalar.activation(
                            out=silu_sb[pr][:], in_=pg[pr][:, :],
                            func=AF.Silu), "s_act")
                        pg_rd[pr] = va
                        nc.vector.wait_ge(c.sems["s_act"], va)
                        nc.vector.wait_ge(c.sems["s_pe"], free_wu[pr])
                        if free_h[q % 2]:
                            c.wait(nc.vector, "s_pe", free_h[q % 2])
                        ins = nc.vector.scalar_tensor_tensor(
                            out=hq_r[:, j, :], in0=pu[pr][:, :], scalar=0.0,
                            in1=silu_sb[pr][:], op0=ALU.bypass, op1=ALU.mult)
                        h_chunk_wr = c.inc(ins, "s_dve")
                        pu_rd[pr] = h_chunk_wr
                        silu_rd[pr] = h_chunk_wr
                    c.wait(nc.tensor, "s_dve", h_chunk_wr)
                    for j in range(nic):
                        ic = CHOFF[q] + j
                        ins = nc.tensor.matmul(
                            out=pE[0:86, :], lhsT=D86[:, 86 - ic:172 - ic],
                            rhs=hq_r[:, j, :], start=(ic == 0),
                            stop=(ic == 85))
                    if q == 7:
                        vsd = c.inc(ins, "s_pe")
                        nc.vector.wait_ge(c.sems["s_pe"], vsd)
                        vsdp = c.inc(nc.vector.tensor_scalar(
                            out=sdp[0:86, :], in0=pE[0:86, :], scalar1=-1.0,
                            scalar2=None, op0=ALU.mult), "s_dve")
                    s0 = CHOFF[q]
                    nsl = nic + (1 if q == 7 else 0)
                    for oc in range(32):
                        pr = oc % 4
                        c.wait(nc.scalar, "s_pe", free_dwo[pr])
                        vd = c.inc(nc.scalar.dma_start(
                            out=wd_sb[pr][:, 0:nsl * 128],
                            in_=d_v[oc, :, s0:s0 + nsl, :].rearrange(
                                "p s i -> p (s i)")), "d_wd%d" % pr, 16)
                        c.wait(nc.tensor, "d_wd%d" % pr, vd)
                        if q == 7:
                            c.wait(nc.tensor, "s_dve", vsdp)
                        bk = oc % 2
                        if py_rd[bk]:
                            c.wait(nc.tensor, "s_dve", py_rd[bk])
                        wd_r = wd_sb[pr][:, 0:nsl * 128].rearrange(
                            "p (s i) -> p s i", i=128)
                        for j in range(nic):
                            ins = nc.tensor.matmul(
                                out=py[bk][:, :], lhsT=wd_r[:, j, :],
                                rhs=hq_r[:, j, :], start=(j == 0),
                                stop=(j == nic - 1 and q != 7))
                        if q == 7:
                            ins = nc.tensor.matmul(out=py[bk][:, :],
                                                   lhsT=wd_r[:, nic, :],
                                                   rhs=sdp[:, :],
                                                   start=False, stop=True)
                        free_dwo[pr] = c.inc(ins, "s_pe")
                        nc.vector.wait_ge(c.sems["s_pe"], free_dwo[pr])
                        ins = nc.vector.scalar_tensor_tensor(
                            out=y_r[:, oc, :], in0=py[bk][:, :],
                            scalar=0.0, in1=y_r[:, oc, :], op0=ALU.bypass,
                            op1=ALU.add)
                        py_rd[bk] = c.inc(ins, "s_dve")
                    free_h[q % 2] = free_dwo[(31) % 2]

            if COMPUTE_ATTN:
                ret_r = ret_sb[:].rearrange("p (h t) -> p h t", t=TT)
                pq = [pA, pB]
                pn = [pC, pC2]
                pb_ = [pF, pF2]
                pq_rd = [0, 0]
                pn_rd = [0, 0]
                pb_rd = [0, 0]
                sig_rd = [0, 0]
                rb_rd = [0, 0]
                den_wr = 0
                den_rd = 0
                va_exp = 0
                vret = 0
                ffn_dve = c.val("s_dve")
                ffn_act = c.val("s_act")
                c.wait(nc.tensor, "s_dve", ffn_dve)
                c.wait(nc.tensor, "s_act", ffn_act)
                for hc in range(HKV):
                    pr = hc % 2
                    c.wait(nc.sync, "s_pe", free_gq[pr])
                    vq = c.inc(nc.sync.dma_start(
                        out=wg_sb[pr][:, 0:32 * 128],
                        in_=wq_v[hc].rearrange("p s i -> p (s i)")),
                        "d_wg%d" % pr, 16)
                    c.wait(nc.tensor, "d_wg%d" % pr, vq)
                    c.wait(nc.tensor, "s_pool", vxn_p)
                    if pq_rd[pr]:
                        c.wait(nc.tensor, "s_dve", pq_rd[pr])
                    wqr = wg_sb[pr][:, 0:32 * 128].rearrange(
                        "p (s i) -> p s i", i=128)
                    for kc in range(32):
                        ins = nc.tensor.matmul(out=pq[pr][:, :],
                                               lhsT=wqr[:, kc, :],
                                               rhs=xn_r[:, kc, :],
                                               start=(kc == 0),
                                               stop=(kc == 31))
                    free_gq[pr] = c.inc(ins, "s_pe")
                    nc.vector.wait_ge(c.sems["s_pe"], free_gq[pr])
                    if va_exp:
                        c.wait(nc.vector, "s_act", va_exp)
                    nc.vector.tensor_scalar(
                        out=tmx_sb[:], in0=pq[pr][:, :],
                        scalar1=bq_sb[:, hc:hc + 1], scalar2=0.0,
                        op0=ALU.add, op1=ALU.min)
                    v1 = c.inc(nc.vector.tensor_scalar(
                        out=tex_sb[:], in0=pq[pr][:, :],
                        scalar1=bq_sb[:, hc:hc + 1], scalar2=0.0,
                        op0=ALU.add, op1=ALU.max), "s_dve")
                    pq_rd[pr] = v1
                    c.wait(nc.scalar, "s_dve", v1)
                    va_exp = c.inc(nc.scalar.activation(out=tmx_sb[:],
                                                        in_=tmx_sb[:],
                                                        func=AF.Exp),
                                   "s_act")
                    c.wait(nc.vector, "s_act", va_exp)
                    nc.vector.wait_ge(c.sems["s_dve"], v1)
                    if sig_rd[pr]:
                        c.wait(nc.vector, "s_pe", sig_rd[pr])
                    vs = c.inc(nc.vector.tensor_add(
                        out=sig_sb[pr][:], in0=tmx_sb[:], in1=tex_sb[:]),
                        "s_dve")
                    c.wait(nc.tensor, "s_dve", vs)
                    if den_rd:
                        c.wait(nc.tensor, "s_dve", den_rd)
                    nc.tensor.matmul(out=pD[0:1, :],
                                     lhsT=z_sb[:, hc:hc + 1],
                                     rhs=sig_sb[pr][:], start=True,
                                     stop=True)
                    if pn_rd[pr]:
                        c.wait(nc.tensor, "s_dve", pn_rd[pr])
                    vmm = c.inc(nc.tensor.matmul(
                        out=pn[pr][:, :], lhsT=M_sb[:, hc * HD:(hc + 1) * HD],
                        rhs=sig_sb[pr][:], start=True, stop=True), "s_pe")
                    sig_rd[pr] = vmm
                    nc.vector.wait_ge(c.sems["s_pe"], vmm)
                    if den_wr:
                        nc.vector.wait_ge(c.sems["s_pe"], den_wr)
                    vds = c.inc(nc.vector.tensor_scalar(
                        out=den, in0=pD[0:1, :], scalar1=1e-6, scalar2=None,
                        op0=ALU.add), "s_dve")
                    nc.vector.wait_ge(c.sems["s_dve"], vds)
                    vr = c.inc(nc.vector.reciprocal(out=den, in_=den),
                               "s_dve")
                    den_rd = vr
                    c.wait(nc.tensor, "s_dve", vr)
                    if pb_rd[pr]:
                        c.wait(nc.tensor, "s_act", pb_rd[pr])
                    vb = c.inc(nc.tensor.matmul(out=pb_[pr][:, :],
                                                lhsT=ones1[:], rhs=den,
                                                start=True, stop=True),
                               "s_pe")
                    den_wr = vb
                    c.wait(nc.scalar, "s_pe", vb)
                    if rb_rd[pr]:
                        c.wait(nc.scalar, "s_dve", rb_rd[pr])
                    va = c.inc(nc.scalar.copy(out=rb_sb[pr][:],
                                              in_=pb_[pr][:, :]), "s_act")
                    pb_rd[pr] = va
                    nc.vector.wait_ge(c.sems["s_act"], va)
                    ins = nc.vector.scalar_tensor_tensor(
                        out=ret_r[:, hc, :], in0=pn[pr][:, :], scalar=0.0,
                        in1=rb_sb[pr][:], op0=ALU.bypass, op1=ALU.mult)
                    vret = c.inc(ins, "s_dve")
                    pn_rd[pr] = vret
                    rb_rd[pr] = vret
                c.wait(nc.tensor, "s_dve", vret)
                pyo = [pA, pB]
                pyo_rd = [pq_rd[0], pq_rd[1]]
                for oc in range(32):
                    pr = oc % 4
                    c.wait(nc.scalar, "s_pe", free_dwo[pr])
                    vo = c.inc(nc.scalar.dma_start(
                        out=wd_sb[pr][:, 0:8 * 128],
                        in_=wo_v[oc].rearrange("p s i -> p (s i)")),
                        "d_wd%d" % pr, 16)
                    c.wait(nc.tensor, "d_wd%d" % pr, vo)
                    bk = oc % 2
                    if pyo_rd[bk]:
                        c.wait(nc.tensor, "s_dve", pyo_rd[bk])
                    wor = wd_sb[pr][:, 0:8 * 128].rearrange(
                        "p (s i) -> p s i", i=128)
                    for cc in range(HKV):
                        ins = nc.tensor.matmul(out=pyo[bk][:, :],
                                               lhsT=wor[:, cc, :],
                                               rhs=ret_r[:, cc, :],
                                               start=(cc == 0),
                                               stop=(cc == 7))
                    free_dwo[pr] = c.inc(ins, "s_pe")
                    nc.vector.wait_ge(c.sems["s_pe"], free_dwo[pr])
                    ins = nc.vector.scalar_tensor_tensor(
                        out=y_r[:, oc, :], in0=pyo[bk][:, :], scalar=0.0,
                        in1=y_r[:, oc, :], op0=ALU.bypass, op1=ALU.add)
                    pyo_rd[bk] = c.inc(ins, "s_dve")
                    if oc % 8 == 7:
                        part = oc // 8
                        c.wait(nc.sync, "s_dve", pyo_rd[bk])
                        y_store_v = c.inc(nc.sync.dma_start(
                            out=yT_r[:, part * 8:(part + 1) * 8,
                                     tt * TT:(tt + 1) * TT],
                            in_=y_r[:, part * 8:(part + 1) * 8, :]),
                            "d_y", 16)

        c.wait(nc.sync, "d_y", y_store_v)

    return nc


def make_in_maps(inputs, ncore=NCORE):
    x = np.asarray(inputs["L0_attn_hidden_states"])
    Wq = np.asarray(inputs["L0_attn_Wq"])
    bq = np.asarray(inputs["L0_attn_bq"])
    Wo = np.asarray(inputs["L0_attn_Wo"])
    M = np.asarray(inputs["L0_attn_M"])
    z = np.asarray(inputs["L0_attn_z"])
    T = TT * NT
    xt = np.ascontiguousarray(x.reshape(x.shape[0] * x.shape[1], D).T)
    common = dict(
        wqT=np.ascontiguousarray(
            Wq.reshape(32, HD, D)[::4].reshape(HKV * HD, D).T),
        bqT=np.ascontiguousarray(bq.reshape(32, HD)[::4].T),
        woT=np.ascontiguousarray(Wo.T),
        m=np.ascontiguousarray(M),
        zT=np.ascontiguousarray(z.T),
        g_qw=np.ascontiguousarray(inputs["L0_ffn_gate_qweight"]),
        g_sc=np.ascontiguousarray(inputs["L0_ffn_gate_scales"]),
        g_qz=np.ascontiguousarray(inputs["L0_ffn_gate_qzeros"]),
        u_qw=np.ascontiguousarray(inputs["L0_ffn_up_qweight"]),
        u_sc=np.ascontiguousarray(inputs["L0_ffn_up_scales"]),
        u_qz=np.ascontiguousarray(inputs["L0_ffn_up_qzeros"]),
        d_qw=np.ascontiguousarray(inputs["L0_ffn_down_qweight"]),
        d_sc=np.ascontiguousarray(inputs["L0_ffn_down_scales"]),
        d_qz=np.ascontiguousarray(inputs["L0_ffn_down_qzeros"]),
    )
    in_maps = []
    for cid in range(ncore):
        m_ = dict(common)
        m_["xT"] = np.ascontiguousarray(xt[:, cid * T:(cid + 1) * T])
        in_maps.append(m_)
    return in_maps


def kernel(**inputs):
    in_maps = make_in_maps(inputs)
    nc = build_program()
    res = run_bass_kernel_spmd(nc, in_maps, core_ids=list(range(NCORE)))
    T = TT * NT
    out = np.empty((16384, D), dtype=np.float32)
    for cid in range(NCORE):
        out[cid * T:(cid + 1) * T, :] = res.results[cid]["yT"].T
    return out.reshape(2, 8192, D)


# version 09 (final)
# speedup vs baseline: 1.1172x; 1.0112x over previous
import numpy as np
import concourse.bass as bass
import concourse.mybir as mybir
from concourse.bass_utils import run_bass_kernel_spmd
from contextlib import ExitStack

F32 = mybir.dt.float32
BF16 = mybir.dt.bfloat16
I32 = mybir.dt.int32
I8 = mybir.dt.int8
AF = mybir.ActivationFunctionType
ALU = mybir.AluOpType

D = 4096
HKV = 8
HD = 128
I_FFN = 11008
NCORE = 8

CHUNKS = [11, 11, 11, 11, 11, 11, 10, 10]
CHOFF = [0, 11, 22, 33, 44, 55, 66, 76]
MAXCH = 11

COMPUTE_FFN = True
COMPUTE_ATTN = True
TT = 512
NT = 4


class Ctl:
    def __init__(self, nc, stack):
        self.nc = nc
        self.stack = stack
        self.sems = {}
        self.cnt = {}

    def sem(self, name):
        self.sems[name] = self.stack.enter_context(self.nc.semaphore(name))
        self.cnt[name] = 0

    def inc(self, instr, name, k=1):
        instr.then_inc(self.sems[name], k)
        self.cnt[name] += k
        return self.cnt[name]

    def wait(self, eng, name, val=None):
        v = self.cnt[name] if val is None else val
        if v > 0:
            eng.wait_ge(self.sems[name], v)

    def val(self, name):
        return self.cnt[name]


def _dequant_matrix(nc, c, qw_d, sc_d, qz_d, sc_v, G, W, N, NSLOT, ph):
    MASK = 0x0F0F0F0F
    for eng in [nc.vector, nc.gpsimd]:
        c.wait(eng, "d_scr")
        c.wait(eng, "d_scrB")
        c.wait(eng, "d_nz")
    c.wait(nc.gpsimd, "s_pe")
    c.wait(nc.gpsimd, "s_dve")
    c.wait(nc.sync, "s_dve")
    vp = 0
    st_rd = [0, 0]
    for si, st in enumerate(range(0, N, 1024)):
        en = min(N, st + 1024)
        spr = si % 2
        if st_rd[spr]:
            c.wait(nc.sync, "s_dve", st_rd[spr])
        v = c.inc(nc.sync.dma_start(out=ph["sc_st"][spr][0:G, 0:en - st],
                                    in_=sc_d[0:G, st:en]),
                  "d_sc%d" % spr, 16)
        c.wait(nc.vector, "d_sc%d" % spr, v)
        nc.vector.wait_ge(c.sems["s_dve"], c.val("s_dve"))
        vp = c.inc(nc.vector.tensor_copy(out=ph["sc_bf"][0:G, st:en],
                                         in_=ph["sc_st"][spr][0:G, 0:en - st]),
                   "s_dve")
        st_rd[spr] = vp
    vsc = vp
    v = c.inc(nc.sync.dma_start(out=ph["qz"][0:G, 0:W], in_=qz_d), "d_qz", 16)
    c.wait(nc.vector, "d_qz", v)
    nc.vector.wait_ge(c.sems["s_dve"], c.val("s_dve"))
    nc.vector.tensor_scalar(
        out=ph["qe"][0:G, 0:W], in0=ph["qz"][0:G, 0:W], scalar1=MASK,
        scalar2=None, op0=ALU.bitwise_and)
    vz = c.inc(nc.vector.tensor_scalar(
        out=ph["qo"][0:G, 0:W], in0=ph["qz"][0:G, 0:W], scalar1=4,
        scalar2=MASK, op0=ALU.logical_shift_right, op1=ALU.bitwise_and),
        "s_dve")
    nc.vector.wait_ge(c.sems["s_dve"], vz)
    NHZ = N // 2
    vz2 = 0
    vms = 0
    for hf in range(2):
        if hf:
            c.wait(nc.gpsimd, "d_nz")
            c.wait(nc.gpsimd, "s_dve", vz2)
            nc.gpsimd.wait_ge(c.sems["s_pool"], vms)
        vms = c.inc(nc.gpsimd.memset(ph["negzs"][:, 0:NHZ], 0.0), "s_pool")
        qe8 = ph["qe"][0:G, hf * W // 2:(hf + 1) * W // 2].bitcast(I8)
        qo8 = ph["qo"][0:G, hf * W // 2:(hf + 1) * W // 2].bitcast(I8)
        c.wait(nc.vector, "s_pool", vms)
        if hf:
            c.wait(nc.vector, "d_nz")
        nc.vector.scalar_tensor_tensor(
            out=ph["negzs"][0:G, 0:NHZ:2], in0=qe8, scalar=-8.0,
            in1=ph["sc_bf"][0:G, hf * NHZ:hf * NHZ + NHZ:2],
            op0=ALU.add, op1=ALU.mult)
        vz2 = c.inc(nc.vector.scalar_tensor_tensor(
            out=ph["negzs"][0:G, 1:NHZ:2], in0=qo8, scalar=-8.0,
            in1=ph["sc_bf"][0:G, hf * NHZ + 1:hf * NHZ + NHZ:2],
            op0=ALU.add, op1=ALU.mult), "s_dve")
        c.wait(nc.scalar, "s_dve", vz2)
        if hf:
            c.wait(nc.scalar, "d_nz")
        t0 = hf * (NHZ // 128)
        c.inc(nc.scalar.dma_start(
            out=sc_v[t0:t0 + NHZ // 128, :, NSLOT - 1, :].rearrange(
                "t p i -> p t i"),
            in_=ph["negzs"][:, 0:NHZ].rearrange("p (t i) -> p t i", i=128)),
            "d_nz", 16)

    dqvs = [ph["deq"][i][:, 0:2 * N].rearrange(
        "p (t two i) -> p t two i", two=2, i=128) for i in range(2)]
    qw_r = qw_d.rearrange("(g p) w -> g p w", p=128)
    qw_rd = [0, 0]
    ps_rd = [0, 0]
    deq_wr = [0, 0]
    rs_built = 0
    for g in range(G):
        parity = g % 2
        dqv = dqvs[(g // 2) % 2]
        qbuf = ph["qw"][parity]
        if qw_rd[parity]:
            c.wait(nc.sync, "s_dve", qw_rd[parity])
        v = c.inc(nc.sync.dma_start(out=qbuf[:, 0:W], in_=qw_r[g]),
                  "d_qw%d" % parity, 16)
        if rs_built:
            c.wait(nc.vector, "s_pe", rs_built)
        vrs = c.inc(nc.vector.tensor_scalar(
            out=ph["rs_t"][0:G, 0:128], in0=ph["iota"][0:G, :],
            scalar1=g, scalar2=None, op0=ALU.is_equal), "s_dve")
        c.wait(nc.vector, "d_qw%d" % parity, v)
        nc.vector.wait_ge(c.sems["s_dve"], c.val("s_dve"))
        nc.vector.tensor_scalar(
            out=ph["qe"][:, 0:W], in0=qbuf[:, 0:W], scalar1=MASK,
            scalar2=None, op0=ALU.bitwise_and)
        qw_rd[parity] = c.inc(nc.vector.tensor_scalar(
            out=ph["qo"][:, 0:W], in0=qbuf[:, 0:W], scalar1=4,
            scalar2=MASK, op0=ALU.logical_shift_right, op1=ALU.bitwise_and),
            "s_dve")
        vunp = qw_rd[parity]
        qe8 = ph["qe"][:, 0:W].bitcast(I8)
        qo8 = ph["qo"][:, 0:W].bitcast(I8)
        for ci, ch0 in enumerate(range(0, N, 2048)):
            cw = min(N, ch0 + 2048) - ch0
            bi = ci % 2
            pb = ph["p_s"][bi]
            if ps_rd[bi]:
                c.wait(nc.tensor, "s_dve", ps_rd[bi])
            c.wait(nc.tensor, "s_dve", max(vsc, vrs))
            for s0 in range(0, cw, 512):
                sw = min(cw, s0 + 512) - s0
                vmm = c.inc(nc.tensor.matmul(
                    out=pb[:, s0:s0 + sw], lhsT=ph["rs_t"][0:G, 0:128],
                    rhs=ph["sc_bf"][0:G, ch0 + s0:ch0 + s0 + sw],
                    start=True, stop=True), "s_pe")
            rs_built = vmm
            nc.vector.wait_ge(c.sems["s_pe"], vmm)
            nc.vector.wait_ge(c.sems["s_dve"], vunp)
            if deq_wr[(g // 2) % 2] and ch0 == 0:
                c.wait(nc.vector, ("d_scr", "d_scrB")[(g // 2) % 2],
                       deq_wr[(g // 2) % 2])
            a0, a1 = ch0 // 128, (ch0 + cw) // 128
            ins = nc.vector.scalar_tensor_tensor(
                out=dqv[:, a0:a1, parity, 0:128:2],
                in0=qe8[:, ch0 // 2:(ch0 + cw) // 2].rearrange(
                    "p (a i) -> p a i", i=64),
                scalar=-8.0,
                in1=pb[:, 0:cw:2].rearrange("p (a i) -> p a i", i=64),
                op0=ALU.add, op1=ALU.mult)
            ins = nc.vector.scalar_tensor_tensor(
                out=dqv[:, a0:a1, parity, 1:128:2],
                in0=qo8[:, ch0 // 2:(ch0 + cw) // 2].rearrange(
                    "p (a i) -> p a i", i=64),
                scalar=-8.0,
                in1=pb[:, 1:cw:2].rearrange("p (a i) -> p a i", i=64),
                op0=ALU.add, op1=ALU.mult)
            ps_rd[bi] = c.inc(ins, "s_dve")
        if parity == 1:
            c.wait(nc.scalar, "s_dve")
            deq_wr[(g // 2) % 2] = c.inc(nc.scalar.dma_start(
                out=sc_v[:, :, g - 1:g + 1, :].rearrange(
                    "t p s i -> p t (s i)"),
                in_=dqv.rearrange("p t two i -> p t (two i)")),
                ("d_scr", "d_scrB")[(g // 2) % 2], 16)


def build_program():
    T = TT * NT
    nc = bass.Bass("TRN2", target_bir_lowering=False)

    xT = nc.dram_tensor("xT", [D, T], F32, kind="ExternalInput").ap()
    wqT = nc.dram_tensor("wqT", [D, HKV * HD], F32, kind="ExternalInput").ap()
    bqT = nc.dram_tensor("bqT", [HD, HKV], F32, kind="ExternalInput").ap()
    woT = nc.dram_tensor("woT", [D, D], F32, kind="ExternalInput").ap()
    m_in = nc.dram_tensor("m", [HKV, HD, HD], F32, kind="ExternalInput").ap()
    zT = nc.dram_tensor("zT", [HD, HKV], F32, kind="ExternalInput").ap()
    g_qw = nc.dram_tensor("g_qw", [D, I_FFN // 8], I32, kind="ExternalInput").ap()
    g_sc = nc.dram_tensor("g_sc", [32, I_FFN], F32, kind="ExternalInput").ap()
    g_qz = nc.dram_tensor("g_qz", [32, I_FFN // 8], I32, kind="ExternalInput").ap()
    u_qw = nc.dram_tensor("u_qw", [D, I_FFN // 8], I32, kind="ExternalInput").ap()
    u_sc = nc.dram_tensor("u_sc", [32, I_FFN], F32, kind="ExternalInput").ap()
    u_qz = nc.dram_tensor("u_qz", [32, I_FFN // 8], I32, kind="ExternalInput").ap()
    d_qw = nc.dram_tensor("d_qw", [I_FFN, D // 8], I32, kind="ExternalInput").ap()
    d_sc = nc.dram_tensor("d_sc", [86, D], F32, kind="ExternalInput").ap()
    d_qz = nc.dram_tensor("d_qz", [86, D // 8], I32, kind="ExternalInput").ap()
    yT = nc.dram_tensor("yT", [D, T], F32, kind="ExternalOutput").ap()

    sc_g = nc.dram_tensor("sc_g", [86 * 128 * 33 * 128], BF16)
    sc_u = nc.dram_tensor("sc_u", [86 * 128 * 33 * 128], BF16)
    sc_d = nc.dram_tensor("sc_d", [32 * 128 * 87 * 128], BF16)
    sc_wq = nc.dram_tensor("sc_wq", [8 * 128 * 32 * 128], BF16)
    sc_wo = nc.dram_tensor("sc_wo", [32 * 128 * 8 * 128], BF16)
    g_v = sc_g.ap().rearrange("(t p s i) -> t p s i", p=128, s=33, i=128)
    u_v = sc_u.ap().rearrange("(t p s i) -> t p s i", p=128, s=33, i=128)
    d_v = sc_d.ap().rearrange("(t p s i) -> t p s i", p=128, s=87, i=128)
    wq_v = sc_wq.ap().rearrange("(h p k i) -> h p k i", p=128, k=32, i=128)
    wo_v = sc_wo.ap().rearrange("(o p k i) -> o p k i", p=128, k=8, i=128)

    with ExitStack() as stack:
        e = stack.enter_context
        c = Ctl(nc, stack)
        DMA_SEMS = ["d_in0", "d_in1", "d_bq", "d_sc0", "d_sc1", "d_qz", "d_qw0", "d_qw1",
                    "d_scr", "d_scrB", "d_nz", "d_scr2", "d_wg0", "d_wg1", "d_wu0", "d_wu1",
                    "d_wd0", "d_wd1", "d_wd2", "d_wd3", "d_x", "d_y"]
        for s in DMA_SEMS + ["s_pe", "s_dve", "s_act", "s_pool"]:
            c.sem(s)
        c.dma_sems = DMA_SEMS

        M_sb = e(nc.sbuf_tensor("M_sb", [128, HKV * HD], BF16))
        z_sb = e(nc.sbuf_tensor("z_sb", [128, HKV], BF16))
        bq_sb = e(nc.sbuf_tensor("bq_sb", [128, HKV], F32))
        onesK = e(nc.sbuf_tensor("onesK", [128, 1], BF16))
        ones1 = e(nc.sbuf_tensor("ones1", [1, 128], F32))
        D32 = e(nc.sbuf_tensor("D32", [128, 64], BF16))
        D86 = e(nc.sbuf_tensor("D86", [128, 172], BF16))
        sgp = e(nc.sbuf_tensor("sgp", [128, TT], BF16))
        sdp = e(nc.sbuf_tensor("sdp", [128, TT], BF16))
        cst = e(nc.sbuf_tensor("cst", [128, 2], F32))

        nc.vector.memset(onesK[:], 1.0 / 4096.0)
        nc.vector.memset(ones1[:], 1.0)
        nc.vector.memset(D32[:], 0.0)
        nc.vector.memset(D86[:], 0.0)
        nc.vector.memset(sgp[:], 0.0)
        nc.vector.memset(sdp[:], 0.0)
        vm0 = c.inc(nc.vector.memset(cst[:, 0:1], 1e-5), "s_dve")
        nc.vector.wait_ge(c.sems["s_dve"], vm0)
        nc.vector.memset(D32[:, 32:33], 1.0)
        c.inc(nc.vector.memset(D86[:, 86:87], 1.0), "s_dve")

        with ExitStack() as p0a:
            e0 = p0a.enter_context
            ph = {}
            ph["sc_st"] = [e0(nc.sbuf_tensor(f"p0_scst{i}", [86, 1024], F32))
                           for i in range(2)]
            ph["sc_bf"] = e0(nc.sbuf_tensor("p0_scb", [86, I_FFN], BF16))
            ph["qz"] = e0(nc.sbuf_tensor("p0_qz", [86, I_FFN // 8], I32))
            ph["negzs"] = e0(nc.sbuf_tensor("p0_negzs", [128, I_FFN // 2],
                                            BF16))
            ph["qw"] = [e0(nc.sbuf_tensor(f"p0_qw{i}", [128, I_FFN // 8], I32))
                        for i in range(2)]
            ph["qe"] = e0(nc.sbuf_tensor("p0_qe", [128, I_FFN // 8], I32))
            ph["qo"] = e0(nc.sbuf_tensor("p0_qo", [128, I_FFN // 8], I32))
            ph["deq"] = [e0(nc.sbuf_tensor(f"p0_deq{i}", [128, 2 * I_FFN],
                                           BF16)) for i in range(2)]
            ph["p_s"] = [e0(nc.psum_tensor(f"p0_ps{i}", [128, 2048], F32))
                         for i in range(2)]
            ph["rs_t"] = e0(nc.sbuf_tensor("p0_rst", [96, 128], BF16))
            ph["iota"] = e0(nc.sbuf_tensor("p0_io96", [96, 128], I32))

            vio = c.inc(nc.gpsimd.iota(ph["iota"][:], pattern=[[0, 128]],
                                       base=0, channel_multiplier=1),
                        "s_pool")
            c.wait(nc.vector, "s_pool", vio)

            _dequant_matrix(nc, c, g_qw, g_sc, g_qz, g_v, 32, I_FFN // 8,
                            I_FFN, 33, ph)

            wtmp = [e0(nc.sbuf_tensor(f"p0_wt{i}", [128, D], F32))
                    for i in range(2)]
            wcast = e0(nc.sbuf_tensor("p0_wc", [128, 2 * D], BF16))

            v = c.inc(nc.gpsimd.dma_start(
                out=wtmp[0][:, 0:HKV * HD].rearrange(
                    "p (kv e) -> p kv e", e=HD),
                in_=m_in.rearrange("kv d e -> d kv e")), "d_in0", 16)
            c.wait(nc.gpsimd, "d_in0", v)
            nc.gpsimd.wait_ge(c.sems["s_pool"], c.val("s_pool"))
            vmcast = c.inc(nc.gpsimd.tensor_copy(
                out=M_sb[:], in_=wtmp[0][:, 0:HKV * HD]), "s_pool")
            v = c.inc(nc.gpsimd.dma_start(out=wtmp[1][:, 0:HKV], in_=zT),
                      "d_in1", 16)
            c.wait(nc.gpsimd, "d_in1", v)
            vmz = c.inc(nc.gpsimd.tensor_copy(out=z_sb[:],
                                              in_=wtmp[1][:, 0:HKV]),
                        "s_pool")
            c.inc(nc.gpsimd.dma_start(out=bq_sb[:], in_=bqT), "d_bq", 16)

            wcv = wcast[:, 0:HKV * 2 * HD].rearrange(
                "p (h two i) -> p h two i", two=2, i=128)
            wqT_r = wqT.rearrange("(k p) h -> k p h", p=128)
            wq_rd = [vmcast, vmz]
            wq_wr = 0
            for kc in range(32):
                parity = kc % 2
                nc.gpsimd.wait_ge(c.sems["s_pool"], wq_rd[parity])
                v = c.inc(nc.gpsimd.dma_start(out=wtmp[parity][:, 0:HKV * HD],
                                            in_=wqT_r[kc]),
                          "d_in%d" % parity, 16)
                c.wait(nc.gpsimd, "d_in%d" % parity, v)
                if wq_wr:
                    c.wait(nc.gpsimd, "d_scr2", wq_wr)
                nc.gpsimd.wait_ge(c.sems["s_pool"], c.val("s_pool"))
                ins = nc.gpsimd.tensor_copy(
                    out=wcv[:, :, parity, :],
                    in_=wtmp[parity][:, 0:HKV * HD].rearrange(
                        "p (h i) -> p h i", i=128))
                wq_rd[parity] = c.inc(ins, "s_pool")
                if parity == 1:
                    nc.gpsimd.wait_ge(c.sems["s_pool"], c.val("s_pool"))
                    wq_wr = c.inc(nc.gpsimd.dma_start(
                        out=wq_v[:, :, kc - 1:kc + 1, :].rearrange(
                            "h p s i -> p h (s i)"),
                        in_=wcv.rearrange("p h two i -> p h (two i)")),
                        "d_scr2", 16)

            wov = wcast[:].rearrange("p (o two i) -> p o two i", two=2, i=128)
            woT_r = woT.rearrange("(k p) o -> k p o", p=128)
            wo_wr = wq_wr
            wo_rd = c.val("s_pool")
            for kv in range(HKV):
                parity = kv % 2
                a, b = wtmp[0], wtmp[1]
                nc.gpsimd.wait_ge(c.sems["s_pool"], wo_rd)
                v0 = c.inc(nc.gpsimd.dma_start(out=a[:], in_=woT_r[4 * kv]),
                           "d_in0", 16)
                v1 = c.inc(nc.gpsimd.dma_start(out=b[:], in_=woT_r[4 * kv + 1]),
                           "d_in1", 16)
                c.wait(nc.gpsimd, "d_in0", v0)
                c.wait(nc.gpsimd, "d_in1", v1)
                nc.gpsimd.wait_ge(c.sems["s_pool"], c.val("s_pool"))
                s1 = c.inc(nc.gpsimd.tensor_add(out=a[:], in0=a[:], in1=b[:]),
                           "s_pool")
                nc.gpsimd.wait_ge(c.sems["s_pool"], s1)
                v2 = c.inc(nc.gpsimd.dma_start(out=b[:], in_=woT_r[4 * kv + 2]),
                           "d_in1", 16)
                c.wait(nc.gpsimd, "d_in1", v2)
                nc.gpsimd.wait_ge(c.sems["s_pool"], s1)
                s2 = c.inc(nc.gpsimd.tensor_add(out=a[:], in0=a[:], in1=b[:]),
                           "s_pool")
                nc.gpsimd.wait_ge(c.sems["s_pool"], s2)
                v3 = c.inc(nc.gpsimd.dma_start(out=b[:], in_=woT_r[4 * kv + 3]),
                           "d_in1", 16)
                c.wait(nc.gpsimd, "d_in1", v3)
                nc.gpsimd.wait_ge(c.sems["s_pool"], s2)
                if wo_wr:
                    c.wait(nc.gpsimd, "d_scr2", wo_wr)
                s3 = c.inc(nc.gpsimd.tensor_add(
                    out=wov[:, :, parity, :],
                    in0=a[:].rearrange("p (o i) -> p o i", i=128),
                    in1=b[:].rearrange("p (o i) -> p o i", i=128)),
                    "s_pool")
                wo_rd = s3
                if parity == 1:
                    nc.gpsimd.wait_ge(c.sems["s_pool"], s3)
                    wo_wr = c.inc(nc.gpsimd.dma_start(
                        out=wo_v[:, :, kv - 1:kv + 1, :].rearrange(
                            "o p s i -> p o (s i)"),
                        in_=wov.rearrange("p o two i -> p o (two i)")),
                        "d_scr2", 16)


            _dequant_matrix(nc, c, u_qw, u_sc, u_qz, u_v, 32, I_FFN // 8,
                            I_FFN, 33, ph)
            _dequant_matrix(nc, c, d_qw, d_sc, d_qz, d_v, 86, D // 8,
                            D, 87, ph)

        y_sb = e(nc.sbuf_tensor("y_sb", [128, 32 * TT], F32))
        xn_sb = e(nc.sbuf_tensor("xn_sb", [128, 32 * TT], BF16))
        h_sb = [e(nc.sbuf_tensor(f"h_sb{i}", [128, MAXCH * TT], BF16))
                for i in range(2)]
        wg_sb = [e(nc.sbuf_tensor(f"wg_sb{i}", [128, 33 * 128], BF16))
                 for i in range(2)]
        wu_sb = [e(nc.sbuf_tensor(f"wu_sb{i}", [128, 33 * 128], BF16))
                 for i in range(2)]
        wd_sb = [e(nc.sbuf_tensor(f"wd_sb{i}", [128, (MAXCH + 1) * 128], BF16))
                 for i in range(4)]
        sig_sb = [e(nc.sbuf_tensor(f"sig_sb{i}", [128, TT], BF16))
                  for i in range(2)]
        ret_sb = e(nc.sbuf_tensor("ret_sb", [128, HKV * TT], BF16))
        sq_sb = [e(nc.sbuf_tensor(f"sq_sb{i}", [128, TT], BF16))
                 for i in range(4)]
        t1_sb = [e(nc.sbuf_tensor(f"t1_sb{i}", [128, TT], F32))
                 for i in range(2)]
        silu_sb = [e(nc.sbuf_tensor(f"silu_sb{i}", [128, TT], BF16))
                   for i in range(2)]
        rb_sb = [e(nc.sbuf_tensor(f"rb_sb{i}", [128, TT], F32))
                 for i in range(2)]
        tex_sb = e(nc.sbuf_tensor("tex_sb", [128, TT], F32))
        tmx_sb = e(nc.sbuf_tensor("tmx_sb", [128, TT], F32))
        row_sb = e(nc.sbuf_tensor("row_sb", [1, 5 * TT], F32))
        pA = e(nc.psum_tensor("pA", [128, TT], F32))
        pB = e(nc.psum_tensor("pB", [128, TT], F32))
        pC = e(nc.psum_tensor("pC", [128, TT], F32))
        pC2 = e(nc.psum_tensor("pC2", [128, TT], F32))
        pD = e(nc.psum_tensor("pD", [128, TT], F32))
        pF = e(nc.psum_tensor("pF", [128, TT], F32))
        pF2 = e(nc.psum_tensor("pF2", [128, TT], F32))
        pE = e(nc.psum_tensor("pE", [128, TT], F32))

        snap = {k: c.val(k) for k in c.cnt}
        for eng in [nc.sync, nc.scalar, nc.vector, nc.gpsimd, nc.tensor]:
            for s in ["d_in0", "d_in1", "d_bq", "d_sc0", "d_sc1", "d_qz",
                      "d_qw0", "d_qw1", "d_scr", "d_scrB", "d_nz",
                      "d_scr2", "s_dve", "s_pool", "s_pe", "s_act"]:
                c.wait(eng, s, snap[s])

        xT_r = xT.rearrange("(k p) t -> p k t", p=128)
        yT_r = yT.rearrange("(k p) t -> p k t", p=128)
        y_r = y_sb[:].rearrange("p (k t) -> p k t", t=TT)
        xn_r = xn_sb[:].rearrange("p (k t) -> p k t", t=TT)

        free_gq = [0, 0]
        free_wu = [0, 0]
        free_dwo = [0, 0, 0, 0]
        free_h = [0, 0]
        y_store_v = 0

        for tt in range(NT):
            snap = {k: c.val(k) for k in ["s_pe", "s_dve", "s_act", "s_pool"]}
            for eng in [nc.tensor, nc.vector, nc.gpsimd, nc.scalar]:
                for s in ["s_pe", "s_dve", "s_act", "s_pool"]:
                    c.wait(eng, s, snap[s])

            c.wait(nc.sync, "d_y", y_store_v)
            xload_v = 0
            for part in range(4):
                ins = nc.sync.dma_start(
                    out=y_r[:, part * 8:(part + 1) * 8, :],
                    in_=xT_r[:, part * 8:(part + 1) * 8,
                             tt * TT:(tt + 1) * TT])
                xload_v = c.inc(ins, "d_x", 16)

            c.wait(nc.gpsimd, "d_x", xload_v)
            c.wait(nc.vector, "d_x", xload_v)
            for kc in range(6):
                ins = nc.gpsimd.tensor_copy(out=xn_r[:, kc, :],
                                            in_=y_r[:, kc, :])
            vxt_p = c.inc(ins, "s_pool")
            for kc in range(6, 32):
                ins = nc.vector.tensor_copy(out=xn_r[:, kc, :],
                                            in_=y_r[:, kc, :])
            vxt = c.inc(ins, "s_dve")
            c.wait(nc.tensor, "s_pool", vxt_p)
            c.wait(nc.tensor, "s_dve", vxt)
            for kc in range(32):
                nc.tensor.matmul(out=pA[0:1, :], lhsT=onesK[:],
                                 rhs=xn_r[:, kc, :], start=(kc == 0),
                                 stop=(kc == 31))
            c.wait(nc.scalar, "d_x", xload_v)
            sq_done = [0, 0, 0, 0]
            for kc in range(32):
                pr = kc % 4
                if sq_done[pr]:
                    c.wait(nc.scalar, "s_pe", sq_done[pr])
                va = c.inc(nc.scalar.activation(out=sq_sb[pr][:],
                                                in_=y_r[:, kc, :],
                                                func=AF.Square), "s_act")
                c.wait(nc.tensor, "s_act", va)
                ins = nc.tensor.matmul(out=pB[0:1, :], lhsT=onesK[:],
                                       rhs=sq_sb[pr][:], start=(kc == 0),
                                       stop=(kc == 31))
                sq_done[pr] = c.inc(ins, "s_pe")
            vstats = max(sq_done)
            mean = row_sb[0:1, 0:TT]
            ex2 = row_sb[0:1, TT:2 * TT]
            var = ex2
            m2 = row_sb[0:1, 2 * TT:3 * TT]
            rstd = row_sb[0:1, 3 * TT:4 * TT]
            mr = m2
            den = row_sb[0:1, 4 * TT:5 * TT]
            nc.vector.wait_ge(c.sems["s_pe"], vstats)
            nc.vector.tensor_copy(out=mean, in_=pA[0:1, :])
            vcp = c.inc(nc.vector.tensor_copy(out=ex2, in_=pB[0:1, :]),
                        "s_dve")
            nc.vector.wait_ge(c.sems["s_dve"], vcp)
            vmu = c.inc(nc.vector.tensor_mul(out=m2, in0=mean, in1=mean),
                        "s_dve")
            nc.vector.wait_ge(c.sems["s_dve"], vmu)
            vrow1 = c.inc(nc.vector.tensor_sub(out=var, in0=ex2, in1=m2),
                          "s_dve")
            c.wait(nc.scalar, "s_dve", vrow1)
            vln = c.inc(nc.scalar.activation(out=var, in_=var, func=AF.Ln,
                                             bias=cst[0:1, 0:1]), "s_act")
            nc.scalar.wait_ge(c.sems["s_act"], vln)
            vact2 = c.inc(nc.scalar.activation(out=rstd, in_=var, func=AF.Exp,
                                               scale=-0.5), "s_act")
            c.wait(nc.vector, "s_act", vact2)
            vrow2 = c.inc(nc.vector.tensor_mul(out=mr, in0=mean, in1=rstd),
                          "s_dve")
            c.wait(nc.tensor, "s_dve", vrow2)
            nc.tensor.matmul(out=pC[:, :], lhsT=ones1[:], rhs=rstd,
                             start=True, stop=True)
            vbc = c.inc(nc.tensor.matmul(out=pD[:, :], lhsT=ones1[:], rhs=mr,
                                         start=True, stop=True), "s_pe")
            c.wait(nc.scalar, "s_pe", vbc)
            nc.scalar.copy(out=rb_sb[0][:], in_=pC[:, :])
            vev = c.inc(nc.scalar.copy(out=rb_sb[1][:], in_=pD[:, :]),
                        "s_act")
            c.wait(nc.vector, "s_act", vev)
            c.wait(nc.gpsimd, "s_act", vev)
            for kc in range(24):
                pr = kc % 2
                vt = c.inc(nc.vector.tensor_mul(out=t1_sb[pr][:],
                                                in0=y_r[:, kc, :],
                                                in1=rb_sb[0][:]), "s_dve")
                nc.vector.wait_ge(c.sems["s_dve"], vt)
                ins = nc.vector.tensor_sub(out=xn_r[:, kc, :],
                                           in0=t1_sb[pr][:], in1=rb_sb[1][:])
            vxn = c.inc(ins, "s_dve")
            tp = [tex_sb, tmx_sb]
            for kc in range(24, 32):
                pr = kc % 2
                vt = c.inc(nc.gpsimd.tensor_mul(out=tp[pr][:],
                                                in0=y_r[:, kc, :],
                                                in1=rb_sb[0][:]), "s_pool")
                nc.gpsimd.wait_ge(c.sems["s_pool"], vt)
                ins = nc.gpsimd.tensor_sub(out=xn_r[:, kc, :],
                                           in0=tp[pr][:], in1=rb_sb[1][:])
            vxn_p = c.inc(ins, "s_pool")

            if COMPUTE_FFN:
                c.wait(nc.tensor, "s_dve", vxn)
                c.wait(nc.tensor, "s_pool", vxn_p)
                for kc in range(32):
                    ins = nc.tensor.matmul(
                        out=pE[0:32, :], lhsT=D32[:, 32 - kc:64 - kc],
                        rhs=xn_r[:, kc, :], start=(kc == 0), stop=(kc == 31))
                vsg = c.inc(ins, "s_pe")
                nc.vector.wait_ge(c.sems["s_pe"], vsg)
                vsgp = c.inc(nc.vector.tensor_scalar(
                    out=sgp[0:32, :], in0=pE[0:32, :], scalar1=-1.0,
                    scalar2=None, op0=ALU.mult), "s_dve")

                pg = [pA, pB]
                pu = [pC, pC2]
                py = [pD, pF]
                pg_rd = [0, 0]
                pu_rd = [0, 0]
                py_rd = [0, 0]
                silu_rd = [0, 0]
                vsdp = 0
                sgp_mm = 0
                for q in range(8):
                    nic = CHUNKS[q]
                    hq = h_sb[q % 2]
                    hq_r = hq[:].rearrange("p (s t) -> p s t", t=TT)
                    h_chunk_wr = 0
                    for j in range(nic):
                        ic = CHOFF[q] + j
                        pr = ic % 2
                        c.wait(nc.sync, "s_pe", free_gq[pr])
                        vg = c.inc(nc.sync.dma_start(
                            out=wg_sb[pr][:],
                            in_=g_v[ic].rearrange("p s i -> p (s i)")),
                            "d_wg%d" % pr, 16)
                        c.wait(nc.sync, "s_pe", free_wu[pr])
                        vu = c.inc(nc.sync.dma_start(
                            out=wu_sb[pr][:],
                            in_=u_v[ic].rearrange("p s i -> p (s i)")),
                            "d_wu%d" % pr, 16)
                        c.wait(nc.tensor, "d_wg%d" % pr, vg)
                        c.wait(nc.tensor, "s_dve", vsgp)
                        if pg_rd[pr]:
                            c.wait(nc.tensor, "s_act", pg_rd[pr])
                        wg_r = wg_sb[pr][:].rearrange("p (s i) -> p s i",
                                                      i=128)
                        for kc in range(32):
                            nc.tensor.matmul(out=pg[pr][:, :],
                                             lhsT=wg_r[:, kc, :],
                                             rhs=xn_r[:, kc, :],
                                             start=(kc == 0), stop=False)
                        ins = nc.tensor.matmul(out=pg[pr][:, :],
                                               lhsT=wg_r[:, 32, :],
                                               rhs=sgp[:, :],
                                               start=False, stop=True)
                        free_gq[pr] = sgp_mm = c.inc(ins, "s_pe")
                        c.wait(nc.tensor, "d_wu%d" % pr, vu)
                        if pu_rd[pr]:
                            c.wait(nc.tensor, "s_dve", pu_rd[pr])
                        wu_r = wu_sb[pr][:].rearrange("p (s i) -> p s i",
                                                      i=128)
                        for kc in range(32):
                            nc.tensor.matmul(out=pu[pr][:, :],
                                             lhsT=wu_r[:, kc, :],
                                             rhs=xn_r[:, kc, :],
                                             start=(kc == 0), stop=False)
                        ins = nc.tensor.matmul(out=pu[pr][:, :],
                                               lhsT=wu_r[:, 32, :],
                                               rhs=sgp[:, :],
                                               start=False, stop=True)
                        free_wu[pr] = c.inc(ins, "s_pe")
                        c.wait(nc.scalar, "s_pe", free_gq[pr])
                        if silu_rd[pr]:
                            c.wait(nc.scalar, "s_dve", silu_rd[pr])
                        va = c.inc(nc.scalar.activation(
                            out=silu_sb[pr][:], in_=pg[pr][:, :],
                            func=AF.Silu), "s_act")
                        pg_rd[pr] = va
                        nc.vector.wait_ge(c.sems["s_act"], va)
                        nc.vector.wait_ge(c.sems["s_pe"], free_wu[pr])
                        if free_h[q % 2]:
                            c.wait(nc.vector, "s_pe", free_h[q % 2])
                        ins = nc.vector.scalar_tensor_tensor(
                            out=hq_r[:, j, :], in0=pu[pr][:, :], scalar=0.0,
                            in1=silu_sb[pr][:], op0=ALU.bypass, op1=ALU.mult)
                        h_chunk_wr = c.inc(ins, "s_dve")
                        pu_rd[pr] = h_chunk_wr
                        silu_rd[pr] = h_chunk_wr
                    c.wait(nc.tensor, "s_dve", h_chunk_wr)
                    for j in range(nic):
                        ic = CHOFF[q] + j
                        ins = nc.tensor.matmul(
                            out=pE[0:86, :], lhsT=D86[:, 86 - ic:172 - ic],
                            rhs=hq_r[:, j, :], start=(ic == 0),
                            stop=(ic == 85))
                    if q == 7:
                        vsd = c.inc(ins, "s_pe")
                        nc.vector.wait_ge(c.sems["s_pe"], vsd)
                        vsdp = c.inc(nc.vector.tensor_scalar(
                            out=sdp[0:86, :], in0=pE[0:86, :], scalar1=-1.0,
                            scalar2=None, op0=ALU.mult), "s_dve")
                    s0 = CHOFF[q]
                    nsl = nic + (1 if q == 7 else 0)
                    for oc in range(32):
                        pr = oc % 4
                        c.wait(nc.scalar, "s_pe", free_dwo[pr])
                        vd = c.inc(nc.scalar.dma_start(
                            out=wd_sb[pr][:, 0:nsl * 128],
                            in_=d_v[oc, :, s0:s0 + nsl, :].rearrange(
                                "p s i -> p (s i)")), "d_wd%d" % pr, 16)
                        c.wait(nc.tensor, "d_wd%d" % pr, vd)
                        if q == 7:
                            c.wait(nc.tensor, "s_dve", vsdp)
                        bk = oc % 2
                        if py_rd[bk]:
                            c.wait(nc.tensor, "s_dve", py_rd[bk])
                        wd_r = wd_sb[pr][:, 0:nsl * 128].rearrange(
                            "p (s i) -> p s i", i=128)
                        for j in range(nic):
                            ins = nc.tensor.matmul(
                                out=py[bk][:, :], lhsT=wd_r[:, j, :],
                                rhs=hq_r[:, j, :], start=(j == 0),
                                stop=(j == nic - 1 and q != 7))
                        if q == 7:
                            ins = nc.tensor.matmul(out=py[bk][:, :],
                                                   lhsT=wd_r[:, nic, :],
                                                   rhs=sdp[:, :],
                                                   start=False, stop=True)
                        free_dwo[pr] = c.inc(ins, "s_pe")
                        nc.vector.wait_ge(c.sems["s_pe"], free_dwo[pr])
                        ins = nc.vector.scalar_tensor_tensor(
                            out=y_r[:, oc, :], in0=py[bk][:, :],
                            scalar=0.0, in1=y_r[:, oc, :], op0=ALU.bypass,
                            op1=ALU.add)
                        py_rd[bk] = c.inc(ins, "s_dve")
                    free_h[q % 2] = free_dwo[(31) % 2]

            if COMPUTE_ATTN:
                ret_r = ret_sb[:].rearrange("p (h t) -> p h t", t=TT)
                pq = [pA, pB]
                pn = [pC, pC2]
                pb_ = [pF, pF2]
                pq_rd = [0, 0]
                pn_rd = [0, 0]
                pb_rd = [0, 0]
                sig_rd = [0, 0]
                rb_rd = [0, 0]
                den_wr = 0
                den_rd = 0
                va_exp = 0
                vret = 0
                ffn_dve = c.val("s_dve")
                ffn_act = c.val("s_act")
                c.wait(nc.tensor, "s_dve", ffn_dve)
                c.wait(nc.tensor, "s_act", ffn_act)
                for hc in range(HKV):
                    pr = hc % 2
                    c.wait(nc.sync, "s_pe", free_gq[pr])
                    vq = c.inc(nc.sync.dma_start(
                        out=wg_sb[pr][:, 0:32 * 128],
                        in_=wq_v[hc].rearrange("p s i -> p (s i)")),
                        "d_wg%d" % pr, 16)
                    c.wait(nc.tensor, "d_wg%d" % pr, vq)
                    c.wait(nc.tensor, "s_pool", vxn_p)
                    if pq_rd[pr]:
                        c.wait(nc.tensor, "s_dve", pq_rd[pr])
                    wqr = wg_sb[pr][:, 0:32 * 128].rearrange(
                        "p (s i) -> p s i", i=128)
                    for kc in range(32):
                        ins = nc.tensor.matmul(out=pq[pr][:, :],
                                               lhsT=wqr[:, kc, :],
                                               rhs=xn_r[:, kc, :],
                                               start=(kc == 0),
                                               stop=(kc == 31))
                    free_gq[pr] = c.inc(ins, "s_pe")
                    nc.vector.wait_ge(c.sems["s_pe"], free_gq[pr])
                    if va_exp:
                        c.wait(nc.vector, "s_act", va_exp)
                    nc.vector.tensor_scalar(
                        out=tmx_sb[:], in0=pq[pr][:, :],
                        scalar1=bq_sb[:, hc:hc + 1], scalar2=0.0,
                        op0=ALU.add, op1=ALU.min)
                    v1 = c.inc(nc.vector.tensor_scalar(
                        out=tex_sb[:], in0=pq[pr][:, :],
                        scalar1=bq_sb[:, hc:hc + 1], scalar2=0.0,
                        op0=ALU.add, op1=ALU.max), "s_dve")
                    pq_rd[pr] = v1
                    c.wait(nc.scalar, "s_dve", v1)
                    va_exp = c.inc(nc.scalar.activation(out=tmx_sb[:],
                                                        in_=tmx_sb[:],
                                                        func=AF.Exp),
                                   "s_act")
                    c.wait(nc.vector, "s_act", va_exp)
                    nc.vector.wait_ge(c.sems["s_dve"], v1)
                    if sig_rd[pr]:
                        c.wait(nc.vector, "s_pe", sig_rd[pr])
                    vs = c.inc(nc.vector.tensor_add(
                        out=sig_sb[pr][:], in0=tmx_sb[:], in1=tex_sb[:]),
                        "s_dve")
                    c.wait(nc.tensor, "s_dve", vs)
                    if den_rd:
                        c.wait(nc.tensor, "s_dve", den_rd)
                    nc.tensor.matmul(out=pD[0:1, :],
                                     lhsT=z_sb[:, hc:hc + 1],
                                     rhs=sig_sb[pr][:], start=True,
                                     stop=True)
                    if pn_rd[pr]:
                        c.wait(nc.tensor, "s_dve", pn_rd[pr])
                    vmm = c.inc(nc.tensor.matmul(
                        out=pn[pr][:, :], lhsT=M_sb[:, hc * HD:(hc + 1) * HD],
                        rhs=sig_sb[pr][:], start=True, stop=True), "s_pe")
                    sig_rd[pr] = vmm
                    nc.vector.wait_ge(c.sems["s_pe"], vmm)
                    if den_wr:
                        nc.vector.wait_ge(c.sems["s_pe"], den_wr)
                    vds = c.inc(nc.vector.tensor_scalar(
                        out=den, in0=pD[0:1, :], scalar1=1e-6, scalar2=None,
                        op0=ALU.add), "s_dve")
                    nc.vector.wait_ge(c.sems["s_dve"], vds)
                    vr = c.inc(nc.vector.reciprocal(out=den, in_=den),
                               "s_dve")
                    den_rd = vr
                    c.wait(nc.tensor, "s_dve", vr)
                    if pb_rd[pr]:
                        c.wait(nc.tensor, "s_act", pb_rd[pr])
                    vb = c.inc(nc.tensor.matmul(out=pb_[pr][:, :],
                                                lhsT=ones1[:], rhs=den,
                                                start=True, stop=True),
                               "s_pe")
                    den_wr = vb
                    c.wait(nc.scalar, "s_pe", vb)
                    if rb_rd[pr]:
                        c.wait(nc.scalar, "s_dve", rb_rd[pr])
                    va = c.inc(nc.scalar.copy(out=rb_sb[pr][:],
                                              in_=pb_[pr][:, :]), "s_act")
                    pb_rd[pr] = va
                    nc.vector.wait_ge(c.sems["s_act"], va)
                    ins = nc.vector.scalar_tensor_tensor(
                        out=ret_r[:, hc, :], in0=pn[pr][:, :], scalar=0.0,
                        in1=rb_sb[pr][:], op0=ALU.bypass, op1=ALU.mult)
                    vret = c.inc(ins, "s_dve")
                    pn_rd[pr] = vret
                    rb_rd[pr] = vret
                c.wait(nc.tensor, "s_dve", vret)
                pyo = [pA, pB]
                pyo_rd = [pq_rd[0], pq_rd[1]]
                for oc in range(32):
                    pr = oc % 4
                    c.wait(nc.scalar, "s_pe", free_dwo[pr])
                    vo = c.inc(nc.scalar.dma_start(
                        out=wd_sb[pr][:, 0:8 * 128],
                        in_=wo_v[oc].rearrange("p s i -> p (s i)")),
                        "d_wd%d" % pr, 16)
                    c.wait(nc.tensor, "d_wd%d" % pr, vo)
                    bk = oc % 2
                    if pyo_rd[bk]:
                        c.wait(nc.tensor, "s_dve", pyo_rd[bk])
                    wor = wd_sb[pr][:, 0:8 * 128].rearrange(
                        "p (s i) -> p s i", i=128)
                    for cc in range(HKV):
                        ins = nc.tensor.matmul(out=pyo[bk][:, :],
                                               lhsT=wor[:, cc, :],
                                               rhs=ret_r[:, cc, :],
                                               start=(cc == 0),
                                               stop=(cc == 7))
                    free_dwo[pr] = c.inc(ins, "s_pe")
                    nc.vector.wait_ge(c.sems["s_pe"], free_dwo[pr])
                    ins = nc.vector.scalar_tensor_tensor(
                        out=y_r[:, oc, :], in0=pyo[bk][:, :], scalar=0.0,
                        in1=y_r[:, oc, :], op0=ALU.bypass, op1=ALU.add)
                    pyo_rd[bk] = c.inc(ins, "s_dve")
                    if oc % 8 == 7:
                        part = oc // 8
                        c.wait(nc.sync, "s_dve", pyo_rd[bk])
                        y_store_v = c.inc(nc.sync.dma_start(
                            out=yT_r[:, part * 8:(part + 1) * 8,
                                     tt * TT:(tt + 1) * TT],
                            in_=y_r[:, part * 8:(part + 1) * 8, :]),
                            "d_y", 16)

        c.wait(nc.sync, "d_y", y_store_v)

    return nc


def make_in_maps(inputs, ncore=NCORE):
    x = np.asarray(inputs["L0_attn_hidden_states"])
    Wq = np.asarray(inputs["L0_attn_Wq"])
    bq = np.asarray(inputs["L0_attn_bq"])
    Wo = np.asarray(inputs["L0_attn_Wo"])
    M = np.asarray(inputs["L0_attn_M"])
    z = np.asarray(inputs["L0_attn_z"])
    T = TT * NT
    xt = np.ascontiguousarray(x.reshape(x.shape[0] * x.shape[1], D).T)
    common = dict(
        wqT=np.ascontiguousarray(
            Wq.reshape(32, HD, D)[::4].reshape(HKV * HD, D).T),
        bqT=np.ascontiguousarray(bq.reshape(32, HD)[::4].T),
        woT=np.ascontiguousarray(Wo.T),
        m=np.ascontiguousarray(M),
        zT=np.ascontiguousarray(z.T),
        g_qw=np.ascontiguousarray(inputs["L0_ffn_gate_qweight"]),
        g_sc=np.ascontiguousarray(inputs["L0_ffn_gate_scales"]),
        g_qz=np.ascontiguousarray(inputs["L0_ffn_gate_qzeros"]),
        u_qw=np.ascontiguousarray(inputs["L0_ffn_up_qweight"]),
        u_sc=np.ascontiguousarray(inputs["L0_ffn_up_scales"]),
        u_qz=np.ascontiguousarray(inputs["L0_ffn_up_qzeros"]),
        d_qw=np.ascontiguousarray(inputs["L0_ffn_down_qweight"]),
        d_sc=np.ascontiguousarray(inputs["L0_ffn_down_scales"]),
        d_qz=np.ascontiguousarray(inputs["L0_ffn_down_qzeros"]),
    )
    in_maps = []
    for cid in range(ncore):
        m_ = dict(common)
        m_["xT"] = np.ascontiguousarray(xt[:, cid * T:(cid + 1) * T])
        in_maps.append(m_)
    return in_maps


def kernel(**inputs):
    in_maps = make_in_maps(inputs)
    nc = build_program()
    res = run_bass_kernel_spmd(nc, in_maps, core_ids=list(range(NCORE)))
    T = TT * NT
    out = np.empty((16384, D), dtype=np.float32)
    for cid in range(NCORE):
        out[cid * T:(cid + 1) * T, :] = res.results[cid]["yT"].T
    return out.reshape(2, 8192, D)
